# version 02
# baseline (speedup 1.0000x reference)
import math
import numpy as np
import ml_dtypes
import concourse.bass as bass
import concourse.mybir as mybir
from concourse.bass_utils import run_bass_kernel_spmd

F32 = mybir.dt.float32
BF16 = mybir.dt.bfloat16
AF = mybir.ActivationFunctionType
ALU = mybir.AluOpType
AX = mybir.AxisListType

T = 2048
D = 1024
NH = 16
DH = 64
NC = D // 128
TT = 512
NTT = T // TT
NKB = T // 128
WG = 256

_CACHE = {}


class Sem:
    __slots__ = ("name", "n", "h")

    def __init__(self, name):
        self.name = name
        self.n = 0
        self.h = None


class Prog:
    ENGINES = ("sync", "tensor", "vector", "scalar", "gpsimd")

    def __init__(self, nc):
        self.nc = nc
        self.ops = {e: [] for e in self.ENGINES}
        self.sems = []
        self.esem = {e: self.sem("es_" + e) for e in self.ENGINES}
        self.dchans = {}

    def dchan(self, name):
        if name not in self.dchans:
            self.dchans[name] = self.sem("dc_" + name)
        return self.dchans[name]

    def sem(self, name):
        s = Sem(name)
        self.sems.append(s)
        return s

    AUTO_SELF = ("vector", "scalar", "gpsimd")

    def op(self, engine, fn, waits=(), track=True):
        tok = None
        incs = ()
        waits = [w for w in waits if w]
        s = self.esem[engine]
        if engine in self.AUTO_SELF and fn is not None and s.n > 0:
            waits.append((s, s.n))
        if track:
            s.n += 1
            tok = (s, s.n)
            incs = ((s, 1),)
        self.ops[engine].append((fn, tuple(waits), incs))
        return tok

    def dma(self, queue, out, in_, waits=(), chan="misc", **kw):
        s = self.dchan(chan)
        s.n += 16
        tok = (s, s.n)

        def fn(eng):
            return eng.dma_start(out=out, in_=in_, **kw)
        self.ops[queue].append((fn, tuple(w for w in waits if w), ((s, 16),)))
        return tok

    def fence(self, engine):
        s = self.esem[engine]
        return (s, s.n)

    def dfence(self, chan):
        s = self.dchan(chan)
        return (s, s.n)

    def build(self):
        import contextlib
        nc = self.nc
        with contextlib.ExitStack() as stack:
            for s in self.sems:
                s.h = stack.enter_context(nc.semaphore(s.name))
            block = stack.enter_context(nc.Block())
            for ename in self.ENGINES:
                ops = self.ops[ename]
                if not ops:
                    continue

                def body(eng, ops=ops):
                    for fn, waits, incs in ops:
                        for s, v in waits:
                            eng.wait_ge(s.h, v)
                        instr = fn(eng) if fn is not None else None
                        if incs:
                            (s0, k0), rest = incs[0], incs[1:]
                            if instr is not None:
                                instr.then_inc(s0.h, k0)
                            else:
                                eng.nop().then_inc(s0.h, k0)
                            for s, k in rest:
                                eng.nop().then_inc(s.h, k)
                getattr(block, ename)(body)
        return nc


def _bf16(x):
    return np.ascontiguousarray(
        np.asarray(x, dtype=np.float32).astype(ml_dtypes.bfloat16))


def _f32(x):
    return np.ascontiguousarray(x, dtype=np.float32)


def _bias_cols(b):
    n = b.shape[0]
    return _f32(b.reshape(n // 128, 128).T)


def host_constants():
    c = {}
    c["c_ident"] = _bf16(np.eye(128))
    c["c_ident_f"] = _f32(np.eye(128))
    d16 = np.zeros((16, 128), np.float32)
    for h in range(16):
        d16[h, (h % 2) * 64:(h % 2) * 64 + 64] = 1.0
    c["c_delta16"] = _bf16(d16)
    c["c_ones_row"] = _bf16(np.ones((1, 128)))
    c["c_ones_row_f"] = _f32(np.ones((1, 128)))
    sc = np.zeros((128, 8), np.float32)
    sc[:, 0] = 1e-5
    sc[:, 1] = 1.0
    c["c_scal"] = _f32(sc)
    c["c_ones_col"] = _bf16(np.ones((128, 1)))
    c["c_ones_col_f"] = _f32(np.ones((128, 1)))
    masks = np.zeros((4, 128, TT), np.float32)
    fq = np.arange(TT)[None, :]
    pp = np.arange(128)[:, None]
    for j in range(4):
        masks[j] = (fq >= j * 128 + pp).astype(np.float32)
    c["c_cmask"] = _bf16(masks.transpose(1, 0, 2).reshape(128, 4 * TT))
    inv = 1.0 / (10000.0 ** (np.arange(0, DH, 2) / DH))
    fr = np.arange(T)[:, None] * inv[None, :]
    emb = np.concatenate([fr, fr], axis=-1)
    cos = np.cos(emb).T
    sin = np.sin(emb).T
    c["c_ropecos"] = _bf16(np.concatenate([cos, cos], 0))
    c["c_ropesin"] = _bf16(np.concatenate([sin, sin], 0))
    inv0 = 1.0 / (10000.0 ** (np.arange(DH) / DH))
    fr0 = np.arange(T)[:, None] * inv0[None, :]
    c["c_poscos_b"] = _bf16(np.concatenate([np.cos(fr0).T, np.cos(fr0).T], 0))
    c["c_possin_b"] = _bf16(np.concatenate([np.sin(fr0).T, np.sin(fr0).T], 0))
    P = np.zeros((128, 128), np.float32)
    for hl in range(2):
        for dd in range(DH):
            r = hl * 64 + dd
            if dd < 32:
                P[hl * 64 + dd + 32, r] = -1.0
            else:
                P[hl * 64 + dd - 32, r] = 1.0
    c["c_roperm"] = _bf16(P)
    return c


CONST_SPECS = {
    "c_ident": ([128, 128], BF16),
    "c_ident_f": ([128, 128], F32),
    "c_delta16": ([16, 128], BF16),
    "c_ones_row": ([1, 128], BF16),
    "c_ones_row_f": ([1, 128], F32),
    "c_scal": ([128, 8], F32),
    "c_ones_col": ([128, 1], BF16),
    "c_ones_col_f": ([128, 1], F32),
    "c_cmask": ([128, 4 * TT], BF16),
    "c_ropecos": ([128, T], BF16),
    "c_ropesin": ([128, T], BF16),
    "c_poscos_b": ([128, T], BF16),
    "c_possin_b": ([128, T], BF16),
    "c_roperm": ([128, 128], BF16),
}


class Ctx:
    def __init__(self, nc, p):
        self.nc = nc
        self.p = p
        self.cin = {}
        self.dram = {}
        self.b = {}
        self.uid = 0

    def name(self, base):
        self.uid += 1
        return f"{base}_{self.uid}"


def alloc_shared(ctx):
    nc = ctx.nc
    b = ctx.b
    b["xres"] = nc.alloc_sbuf_tensor("xres", [128, NC * T], F32)[:]
    b["arb"] = nc.alloc_sbuf_tensor("arb", [128, 61440], BF16)[:]
    b["arf"] = nc.alloc_sbuf_tensor("arf", [128, 1024], F32)[:]
    b["wf32"] = nc.alloc_sbuf_tensor("wf32", [128, 2 * WG], F32)[:]
    b["wbf"] = nc.alloc_sbuf_tensor("wbf", [128, 16 * WG], BF16)[:]
    b["rows"] = nc.alloc_sbuf_tensor("rows", [1, 2048], F32)[:]
    b["lrow"] = b["rows"][0:1, 0:1024]
    b["vt"] = b["arb"][:, 43008:43008 + NKB * 130]
    b["ebuf"] = b["arb"][:, 45088:45088 + 4 * TT]
    b["usb"] = b["arf"][:, 512:1024]
    for i in range(7):
        b[f"ps{i}"] = nc.alloc_psum_tensor(f"ps{i}", [128, 512], F32)[:]
    b["pst"] = nc.alloc_psum_tensor("pst", [128, 1024], BF16)[:]
    return b


def load_consts(ctx, names):
    nc, p = ctx.nc, ctx.p
    toks = []
    for n in names:
        shape, dt = CONST_SPECS[n]
        t = nc.alloc_sbuf_tensor(n + "_s", shape, dt)[:]
        ctx.cin[n] = t
        toks.append(p.dma("sync", t, ctx.dram[n], chan="misc"))
    return toks


def emit_ln(ctx, xn_off, xb_off=None):
    p, b, cin = ctx.p, ctx.b, ctx.cin
    xres = b["xres"]
    if xb_off is None:
        xb_off = xn_off + NC * T
    xn = b["arb"][:, xn_off:xn_off + NC * T]
    xb = b["arb"][:, xb_off:xb_off + NC * T]
    stat = b["ps0"]
    mtile = b["ps1"]
    rtile = b["ps2"]
    rows = b["rows"]
    tsc = b["arf"][:, 0:TT]
    onescol = cin["c_ones_col"]
    onesrow = cin["c_ones_row_f"]

    last = None
    prev_bcast_reads = ()
    prev_ev = ()
    for tt in range(NTT):
        tcopy = [p.op("vector", lambda v, c=c, tt=tt: v.tensor_copy(
            xb[:, c * T + tt * TT:c * T + tt * TT + TT],
            xres[:, c * T + tt * TT:c * T + tt * TT + TT]))
            for c in range(NC)]
        mmt = None
        for c in range(NC):
            sl = slice(c * T + tt * TT, c * T + tt * TT + TT)

            def mm(t, sl=sl, c=c):
                return t.matmul(stat[0:1, 0:TT], onescol[:, 0:1], xb[:, sl],
                                start=(c == 0), stop=(c == NC - 1))
            w = [tcopy[c]]
            if c == 0:
                w += list(prev_ev)
            mmt = p.op("tensor", mm, waits=tuple(w), track=(c == NC - 1))
        ev0 = p.op("scalar", lambda s: s.copy(rows[0:1, 0:TT],
                                              stat[0:1, 0:TT]), waits=(mmt,))
        tsq = [p.op("scalar", lambda s, c=c, tt=tt: s.activation(
            xb[:, c * T + tt * TT:c * T + tt * TT + TT],
            xb[:, c * T + tt * TT:c * T + tt * TT + TT], AF.Square),
            waits=(ev0,) if c == 0 else ())
            for c in range(NC)]
        mmt = None
        for c in range(NC):
            sl = slice(c * T + tt * TT, c * T + tt * TT + TT)

            def mm(t, sl=sl, c=c):
                return t.matmul(stat[0:1, 0:TT], onescol[:, 0:1], xb[:, sl],
                                start=(c == 0), stop=(c == NC - 1))
            w = [tsq[c]]
            if c == 0:
                w.append(ev0)
            mmt = p.op("tensor", mm, waits=tuple(w), track=(c == NC - 1))
        ev1 = p.op("scalar", lambda s: s.copy(rows[0:1, TT:2 * TT],
                                              stat[0:1, 0:TT]), waits=(mmt,))
        prev_ev = (ev1,)
        t_mu = p.op("scalar", lambda s: s.activation(
            rows[0:1, 2 * TT:3 * TT], rows[0:1, 0:TT], AF.Copy,
            scale=1.0 / D), waits=(ev1,))
        t_ms = p.op("scalar", lambda s: s.activation(
            rows[0:1, 3 * TT:4 * TT], rows[0:1, 2 * TT:3 * TT], AF.Square),
            waits=(t_mu,))
        t_var = p.op("vector", lambda v: v.scalar_tensor_tensor(
            rows[0:1, TT:2 * TT], rows[0:1, TT:2 * TT], 1.0 / D,
            rows[0:1, 3 * TT:4 * TT], ALU.mult, ALU.subtract), waits=(t_ms,))
        t_sd = p.op("scalar", lambda s: s.activation(
            rows[0:1, TT:2 * TT], rows[0:1, TT:2 * TT], AF.Sqrt,
            bias=cin["c_scal"][0:1, 0:1]), waits=(t_var,))
        t_rs = p.op("vector", lambda v: v.reciprocal(
            rows[0:1, TT:2 * TT], rows[0:1, TT:2 * TT]), waits=(t_sd,))
        tb1 = p.op("tensor", lambda t: t.matmul(
            mtile[:, 0:TT], onesrow[0:1, :], rows[0:1, 2 * TT:3 * TT],
            start=True, stop=True), waits=(t_rs,) + prev_bcast_reads)
        tb2 = p.op("tensor", lambda t: t.matmul(
            rtile[:, 0:TT], onesrow[0:1, :], rows[0:1, TT:2 * TT],
            start=True, stop=True))
        reads = []
        for c in range(NC):
            xsl = slice(c * T + tt * TT, c * T + tt * TT + TT)
            p.op("vector", lambda v, xsl=xsl: v.tensor_tensor(
                tsc, xres[:, xsl], mtile[:, 0:TT], ALU.subtract),
                waits=(tb1, tb2) if c == 0 else ())
            last = p.op("vector", lambda v, xsl=xsl: v.tensor_tensor(
                xn[:, xsl], tsc, rtile[:, 0:TT], ALU.mult))
            reads.append(last)
        prev_bcast_reads = (last,)
    return xn, last


class WStream:

    def __init__(self, ctx):
        self.ctx = ctx
        self.wslot_users = [(), ()]
        self.mm_last = [(), ()]
        self.par = 0
        self.ce_i = 0
        self.wcnt = 0
        self.bank_free = {}


def emit_linear(ctx, ws, cols_src, w_dram, n_in, n_out, out_cb,
                first_waits=(), col_off=0, bank_pair=("ps3", "ps4"),
                conv_engines=("vector", "scalar", "gpsimd"), G=None, nbuf=2,
                bf16_direct=False, tt_list=None):
    p, b = ctx.p, ctx.b
    nc_in = n_in // 128
    if G is None:
        G = WG
    assert nbuf * nc_in * G <= 16 * WG, "wbf overflow"
    if tt_list is None:
        tt_list = tuple(range(NTT))
    if nbuf == 1:
        ws.mm_last[0] = tuple(ws.mm_last[0]) + tuple(ws.mm_last[1])
    wf32, wbf = b["wf32"], b["wbf"]
    banks = (b[bank_pair[0]], b[bank_pair[1]])
    bank_free = ws.bank_free.setdefault(bank_pair, [(), ()])
    ngroups = (n_out + G - 1) // G
    for g in range(ngroups):
        if nbuf == 2:
            par = ws.par
            ws.par ^= 1
        else:
            par = 0
        cols = min(G, n_out - g * G)
        conv_toks = {}
        for dc in range(nc_in):
            wsl = w_dram[dc * 128:(dc + 1) * 128,
                         col_off + g * G:col_off + g * G + cols]
            dstb = wbf[:, (par * nc_in + dc) * G:(par * nc_in + dc) * G + cols]
            if bf16_direct:
                dtok = p.dma("sync", dstb, wsl,
                             waits=tuple(ws.mm_last[par]) if dc == 0 else (),
                             chan=f"wd{par}_{dc}")
                conv_toks[dc] = dtok
                continue
            slot = ws.wcnt % 2
            ws.wcnt += 1
            dst32 = wf32[:, slot * G:slot * G + cols]
            dtok = p.dma("sync", dst32, wsl, waits=ws.wslot_users[slot],
                         chan=f"wslot{slot}")
            eng = conv_engines[ws.ce_i % len(conv_engines)]
            ws.ce_i += 1

            def cv(e, dstb=dstb, dst32=dst32):
                if hasattr(e, "tensor_copy"):
                    return e.tensor_copy(dstb, dst32)
                return e.copy(dstb, dst32)
            ctok = p.op(eng, cv, waits=(dtok,) + tuple(ws.mm_last[par]))
            ws.wslot_users[slot] = (ctok,)
            conv_toks[dc] = ctok
        gsz = (cols + 127) // 128
        last_mm = None
        for oci in range(gsz):
            oc = g * (G // 128) + oci
            ocols = min(128, cols - oci * 128)
            bk = oc % 2
            pbank = banks[bk]
            for tt in tt_list:
                mmt = None
                for dc in range(nc_in):
                    lhsT = wbf[:, (par * nc_in + dc) * G + oci * 128:
                               (par * nc_in + dc) * G + oci * 128 + ocols]

                    def mm(t, lhsT=lhsT, dc=dc, tt=tt, ocols=ocols, pbank=pbank):
                        return t.matmul(pbank[0:ocols, 0:TT], lhsT,
                                        cols_src(dc, tt), start=(dc == 0),
                                        stop=(dc == nc_in - 1))
                    w = [conv_toks[dc]]
                    if dc == 0:
                        w += list(bank_free[bk]) + list(first_waits)
                        first_waits = ()
                    mmt = p.op("tensor", mm, waits=tuple(w),
                               track=(dc == nc_in - 1))
                etok = out_cb(oc, tt, pbank, mmt)
                bank_free[bk] = (etok,)
                last_mm = mmt
        ws.mm_last[par] = (last_mm,)
    return bank_free


def emit_vtok(ctx, v_src_pair, first_waits=()):
    p, b, cin = ctx.p, ctx.b, ctx.cin
    vt = b["vt"]
    pst = b["pst"]
    ident = cin["c_ident"]
    tok = None
    prev_ev = tuple(first_waits) + tuple(ctx.b.get("vt_last", ()))
    for kb in range(NKB):
        t1 = p.op("tensor", lambda t, kb=kb: t.transpose(
            pst[:, 0:128], v_src_pair[:, kb * 128:(kb + 1) * 128],
            ident[:, :]), waits=prev_ev)
        dst = vt[:, kb * 130:kb * 130 + 130].rearrange(
            "p (h c) -> p h c", h=2)[:, :, 0:64]
        srcv = pst[:, 0:128].rearrange("p (h c) -> p h c", h=2)
        e1 = p.op("scalar", lambda s, dst=dst, srcv=srcv: s.copy(dst, srcv),
                  waits=(t1,))
        prev_ev = (e1,)
        tok = e1
    ctx.b["vt_last"] = (tok,)
    return tok


def emit_attn_pair(ctx, pair, q_lhs, k_pair, y_dst, causal, scale,
                   extra_mask=None, first_waits=(), two_part=False):
    p, b, cin = ctx.p, ctx.b, ctx.cin
    sb = (b["ps0"], b["ps1"], b["ps2"], b["ps3"])
    ub = (b["ps4"], b["ps5"])
    scb = b["ps6"]
    vt = b["vt"]
    ebuf = b["arf"]
    eb = b["ebuf"]
    usb = b["usb"]
    cmask = cin.get("c_cmask")
    onesrow = cin["c_ones_row_f"]
    rrow = b["rows"]

    y_out_toks = []
    ufree = [(), ()]
    sfree = {(h, par): tuple(first_waits) for h in (0, 1) for par in (0, 1)}
    for qt in range(NTT):
        kb_list = list(range(NKB))
        if causal:
            kb_list = [kb for kb in kb_list if kb * 128 < (qt + 1) * TT]
        efree = {}
        utoks = [None, None]
        for ik, kb in enumerate(kb_list):
            parity = ik % 2
            for h in (0, 1):
                sbank = sb[h * 2 + parity]
                if two_part:
                    p.op("tensor", lambda t, h=h, kb=kb, qt=qt, sbank=sbank:
                         t.matmul(sbank[0:128, 0:TT], k_pair(h, kb, 0),
                                  q_lhs(h, qt, 0), start=True, stop=False),
                         waits=sfree[(h, parity)], track=False)
                    stok = p.op("tensor",
                                lambda t, h=h, kb=kb, qt=qt, sbank=sbank:
                                t.matmul(sbank[0:128, 0:TT], k_pair(h, kb, 1),
                                         q_lhs(h, qt, 1), start=False,
                                         stop=True))
                else:
                    stok = p.op("tensor",
                                lambda t, h=h, kb=kb, qt=qt, sbank=sbank:
                                t.matmul(sbank[0:128, 0:TT], k_pair(h, kb),
                                         q_lhs(h, qt), start=True, stop=True),
                                waits=sfree[(h, parity)])
                esl = eb[:, (h * 2 + parity) * TT:(h * 2 + parity) * TT + TT]
                etok = p.op("scalar", lambda s, esl=esl, sbank=sbank:
                            s.activation(esl, sbank[0:128, 0:TT], AF.Exp,
                                         scale=scale),
                            waits=(stok,) + efree.get((h, parity), ()))
                sfree[(h, parity)] = (etok,)
                mtok = etok
                if extra_mask is not None:
                    mtok = extra_mask(h, kb, qt, esl, etok)
                if causal and kb * 128 >= qt * TT:
                    j = kb - qt * 4
                    mtok = p.op("vector", lambda v, esl=esl, j=j: v.tensor_tensor(
                        esl, esl, cmask[:, j * TT:(j + 1) * TT], ALU.mult),
                        waits=(mtok,))
                first = (ik == 0)
                last = (ik == len(kb_list) - 1)
                vsl = vt[:, kb * 130 + h * 65:kb * 130 + h * 65 + 65]
                ut = p.op("tensor", lambda t, h=h, vsl=vsl, esl=esl, first=first,
                          last=last: t.matmul(ub[h][0:65, 0:TT], vsl, esl,
                                              start=first, stop=last),
                          waits=(mtok,) + (ufree[h] if first else ()))
                efree[(h, parity)] = (ut,)
                if last:
                    utoks[h] = ut
        for h in (0, 1):
            usl = usb
            ev = p.op("scalar", lambda s, usl=usl, h=h: s.copy(
                usl[0:65, :], ub[h][0:65, 0:TT]),
                waits=(utoks[h],) + tuple(y_out_toks[-1:]))
            lrow = b["lrow"][0:1, h * TT:h * TT + TT]
            l1 = p.op("scalar", lambda s, usl=usl, lrow=lrow: s.activation(
                lrow, usl[64:65, :], AF.Ln), waits=(ev,))
            l2 = p.op("scalar", lambda s, lrow=lrow: s.activation(
                lrow, lrow, AF.Exp, scale=-1.0), waits=(l1,))
            bt = p.op("tensor", lambda t, lrow=lrow: t.matmul(
                scb[0:64, 0:TT], onesrow[0:1, 0:64], lrow, start=True,
                stop=True), waits=(l2,) + tuple(y_out_toks[-1:]))
            yt = p.op("vector", lambda v, usl=usl, h=h, qt=qt: v.tensor_tensor(
                y_dst(h, qt), usl[0:64, :], scb[0:64, 0:TT], ALU.mult),
                waits=(bt,))
            ufree[h] = (ev,)
            y_out_toks.append(yt)
    return y_out_toks


def emit_mha(ctx, kind, upto="full"):
    p, b, cin, dram = ctx.p, ctx.b, ctx.cin, ctx.dram
    pfx = {"l0a": "w0a", "l1a": "w1a"}[kind]
    rope = (kind == "l1a")
    xn, t_xn = emit_ln(ctx, 0, xb_off=26624)
    if upto == "ln":
        return t_xn
    bqkv = ctx.nc.alloc_sbuf_tensor(ctx.name("bqkv"), [128, 32], F32)[:]
    bc_chan = ctx.name("bias")
    p.dma("sync", bqkv[:, 0:8], dram[pfx + "_bq"], chan=bc_chan)
    p.dma("sync", bqkv[:, 8:16], dram[pfx + "_bk"], chan=bc_chan)
    p.dma("sync", bqkv[:, 16:24], dram[pfx + "_bv"], chan=bc_chan)
    p.dma("sync", bqkv[:, 24:32], dram[pfx + "_bo"], chan=bc_chan)
    tb = p.dfence(bc_chan)

    QOF, KOF, VOF, TRIG, YOF = 16384, 18432, 20480, 22528, 26624
    qP = b["arb"][:, QOF:QOF + T]
    kP = b["arb"][:, KOF:KOF + T]
    vP = b["arb"][:, VOF:VOF + T]
    YB = b["arb"][:, YOF + 0:YOF + NC * T]
    ws = WStream(ctx)
    t_ones = p.op("vector", lambda v: v.memset(
        b["vt"][:].rearrange("p (k c) -> p k c", c=130)[:, :, 64:65], 1.0))
    t_ones2 = p.op("vector", lambda v: v.memset(
        b["vt"][:].rearrange("p (k c) -> p k c", c=130)[:, :, 129:130], 1.0))

    cos = b["arb"][:, TRIG:TRIG + T]
    sin = b["arb"][:, TRIG + T:TRIG + 2 * T]
    roperm = cin.get("c_roperm")
    prev_attn = (t_xn, tb)
    if rope:
        p.dma("sync", cos, dram["c_ropecos"], chan="trig")
        p.dma("sync", sin, dram["c_ropesin"], chan="trig")
        prev_attn = prev_attn + (p.dfence("trig"),)
    for pair in range(NC):
        toks = {"q": [], "k": [], "v": []}

        def mk_cb(dst, which, pair=pair):
            def cb(oc, tt, ps, mmtok):
                bc = bqkv[:, {"q": 0, "k": 8, "v": 16}[which] + pair:
                          {"q": 0, "k": 8, "v": 16}[which] + pair + 1]
                tok = p.op("scalar", lambda s, tt=tt, ps=ps: s.activation(
                    dst[:, tt * TT:tt * TT + TT], ps[0:128, 0:TT], AF.Identity,
                    bias=bc), waits=(mmtok,))
                toks[which].append(tok)
                return tok
            return cb
        for which, dst, wname in (("q", qP, "_wq"), ("k", kP, "_wk"),
                                  ("v", vP, "_wv")):
            emit_linear(ctx, ws, lambda dc, tt: xn[:, dc * T + tt * TT:
                                                   dc * T + tt * TT + TT],
                        dram[pfx + wname], D, 128, mk_cb(dst, which),
                        first_waits=prev_attn, col_off=pair * 128,
                        bank_pair=("ps3", "ps4"),
                        conv_engines=("vector", "scalar"))
            prev_attn = ()
        if rope:
            rope_prev = ()
            for which, buf in (("q", qP), ("k", kP)):
                for tt in range(NTT):
                    sl = slice(tt * TT, tt * TT + TT)
                    wt = (tuple(toks[which]) if tt == 0 else ()) + rope_prev
                    m1 = p.op("tensor", lambda t, sl=sl, buf=buf: t.matmul(
                        b["ps5"][:, 0:TT], roperm, buf[:, sl], start=True,
                        stop=True), waits=wt)
                    s1 = p.op("vector", lambda v, sl=sl, buf=buf: v.tensor_tensor(
                        b["arf"][:, 0:TT], b["ps5"][:, 0:TT], sin[:, sl],
                        ALU.mult), waits=(m1,))
                    rope_prev = (s1,)
                    s2 = p.op("vector", lambda v, sl=sl, buf=buf: v.tensor_tensor(
                        b["arf"][:, TT:2 * TT], buf[:, sl], cos[:, sl],
                        ALU.mult))
                    s3 = p.op("vector", lambda v, sl=sl, buf=buf: v.tensor_tensor(
                        buf[:, sl], b["arf"][:, 0:TT], b["arf"][:, TT:2 * TT],
                        ALU.add))
                    toks[which].append(s3)
        if upto == "qkv0":
            return toks["v"][-1]
        t_vt = emit_vtok(ctx, vP, first_waits=tuple(toks["v"]))
        if upto == "vtok0":
            return t_vt
        pre = tuple(toks["q"]) + tuple(toks["k"]) + (t_vt, t_ones, t_ones2)
        yt = emit_attn_pair(
            ctx, pair,
            q_lhs=lambda h, qt: qP[h * 64:h * 64 + 64, qt * TT:qt * TT + TT],
            k_pair=lambda h, kb: kP[h * 64:h * 64 + 64, kb * 128:kb * 128 + 128],
            y_dst=lambda h, qt, pair=pair: YB[
                h * 64:h * 64 + 64,
                pair * T + qt * TT:pair * T + qt * TT + TT],
            causal=False, scale=0.125, first_waits=pre)
        prev_attn = tuple(yt[-2:])
        if upto == "pair0":
            return yt[-1]
    if upto == "pairs":
        return prev_attn[-1]
    def ocb(oc, tt, ps, mmtok):
        bc = bqkv[:, 24 + oc:24 + oc + 1]
        sl = slice(oc * T + tt * TT, oc * T + tt * TT + TT)
        return p.op("vector", lambda v, sl=sl, ps=ps, bc=bc:
                    v.scalar_tensor_tensor(b["xres"][:, sl], ps[0:128, 0:TT],
                                           bc, b["xres"][:, sl], ALU.add,
                                           ALU.add), waits=(mmtok,))
    emit_linear(ctx, ws, lambda dc, tt: YB[:, dc * T + tt * TT:
                                           dc * T + tt * TT + TT],
                dram[pfx + "_wo"], D, D, ocb, first_waits=prev_attn,
                bank_pair=("ps3", "ps4"), conv_engines=("scalar", "gpsimd"))
    return p.fence("vector")




AOF = 16384


def emit_moe_gate(ctx, xn, t_xn):
    p, b, cin, dram = ctx.p, ctx.b, ctx.cin, ctx.dram
    arf = b["arf"]
    gl = arf[:, 512:640]
    CW = arf[:, 640:768]
    m1 = arf[:, 768:769]
    m2 = arf[:, 769:770]
    d12 = arf[:, 770:771]
    w1 = arf[:, 771:772]
    w2 = arf[:, 772:773]
    eq1 = arf[:, 776:784]
    tmp = arf[:, 784:792]
    eq2 = arf[:, 792:800]
    ps5 = b["ps5"]
    gw = b["wbf"][:, 0:64]
    gwf = b["wf32"][:, 0:64]
    ch = ctx.name("gw")
    dt0 = p.dma("sync", gwf.rearrange("p (c e) -> p c e", e=8),
                dram["moe_gate_w"].rearrange("(c p) e -> p c e", p=128),
                chan=ch)
    tcv = p.op("scalar", lambda s: s.copy(gw, gwf), waits=(dt0,))
    cw_tok = None
    prev_ev = ()
    for tb in range(NKB):
        mmt = None
        for dc in range(NC):
            def mm(t, tb=tb, dc=dc):
                return t.matmul(ps5[0:128, 0:8],
                                xn[:, dc * T + tb * 128:dc * T + (tb + 1) * 128],
                                gw[:, dc * 8:dc * 8 + 8],
                                start=(dc == 0), stop=(dc == NC - 1))
            w = [tcv, t_xn] if dc == 0 else []
            if dc == 0:
                w += list(prev_ev)
            mmt = p.op("tensor", mm, waits=tuple(w), track=(dc == NC - 1))
        ev = p.op("scalar", lambda s, tb=tb: s.copy(gl[:, tb * 8:(tb + 1) * 8],
                                                    ps5[0:128, 0:8]),
                  waits=(mmt,))
        prev_ev = (ev,)
        glt = gl[:, tb * 8:(tb + 1) * 8]
        t1 = p.op("vector", lambda v, glt=glt: v.tensor_reduce(
            m1, glt, AX.X, ALU.max), waits=(ev,))
        t2 = p.op("vector", lambda v, glt=glt: v.tensor_scalar(
            eq1, glt, m1[:, 0:1], None, ALU.is_ge))
        t3 = p.op("vector", lambda v, glt=glt: v.scalar_tensor_tensor(
            tmp, eq1, -1e30, glt, ALU.mult, ALU.add))
        t4 = p.op("vector", lambda v: v.tensor_reduce(m2, tmp, AX.X, ALU.max))
        t5 = p.op("vector", lambda v, glt=glt: v.tensor_scalar(
            eq2, tmp, m2[:, 0:1], None, ALU.is_ge))
        t6 = p.op("vector", lambda v: v.tensor_tensor(d12, m1, m2, ALU.subtract))
        t7 = p.op("scalar", lambda s: s.activation(w1, d12, AF.Sigmoid),
                  waits=(t6,))
        t8 = p.op("scalar", lambda s: s.activation(w2, d12, AF.Sigmoid,
                                                   scale=-1.0))
        t9 = p.op("vector", lambda v: v.tensor_scalar(
            eq1, eq1, w1[:, 0:1], None, ALU.mult), waits=(t8,))
        cw_tok = p.op("vector", lambda v, tb=tb: v.scalar_tensor_tensor(
            CW[:, tb * 8:(tb + 1) * 8], eq2, w2[:, 0:1], eq1, ALU.mult,
            ALU.add))
    return CW, cw_tok


def emit_cw_row(ctx, CW, e, cw_tok, prev=()):
    p, b, cin = ctx.p, ctx.b, ctx.cin
    rows = b["rows"]
    ps6 = b["ps6"]
    identf = cin["c_ident_f"]
    toks = []
    prev_t = prev
    for tb in range(NKB):
        t1 = p.op("tensor", lambda t, tb=tb: t.transpose(
            ps6[0:1, 0:128], CW[:, tb * 8 + e:tb * 8 + e + 1], identf),
            waits=(cw_tok,) + tuple(prev_t))
        e1 = p.op("scalar", lambda s, tb=tb: s.copy(
            rows[0:1, tb * 128:(tb + 1) * 128], ps6[0:1, 0:128]), waits=(t1,))
        prev_t = (e1,)
        toks.append(e1)
    return toks[-1]


def emit_moe_mlp(ctx, xn, wgu_dram, wd_dram, ws, first_waits, cw_row_tok,
                 weighted):
    p, b, cin = ctx.p, ctx.b, ctx.cin
    arb, arf = b["arb"], b["arf"]
    wbf = b["wbf"]
    wf32 = b["wf32"]
    sg = arf[:, 0:512]
    cwb = arf[:, 0:512]
    II = 22
    bankA, bankB = b["ps0"], b["ps1"]
    a_toks = []
    freeA = tuple(first_waits)
    freeB = ()
    sg_free = ()
    for ocp in range(II):
        par = ocp % 2
        conv = {}
        for half, coff in ((0, ocp * 128), (1, (II + ocp) * 128)):
            for dc in range(NC):
                slot = ws.wcnt % 2
                ws.wcnt += 1
                dst32 = wf32[:, slot * WG:slot * WG + 128]
                dtok = p.dma("sync", dst32,
                             wgu_dram[dc * 128:(dc + 1) * 128, coff:coff + 128],
                             waits=ws.wslot_users[slot], chan=f"wslot{slot}")
                eng = ("scalar", "gpsimd")[ws.ce_i % 2]
                ws.ce_i += 1
                dstb = wbf[:, (par * 16 + half * 8 + dc) * 128:
                           (par * 16 + half * 8 + dc) * 128 + 128]

                def cv(en, dstb=dstb, dst32=dst32):
                    if hasattr(en, "tensor_copy"):
                        return en.tensor_copy(dstb, dst32)
                    return en.copy(dstb, dst32)
                ctok = p.op(eng, cv, waits=(dtok,) + tuple(ws.mm_last[par]))
                ws.wslot_users[slot] = (ctok,)
                conv[(half, dc)] = ctok
        last_mm = None
        for tt in range(NTT):
            mg = None
            for dc in range(NC):
                lhsT = wbf[:, (par * 16 + dc) * 128:(par * 16 + dc) * 128 + 128]

                def mm(t, lhsT=lhsT, dc=dc, tt=tt):
                    return t.matmul(bankA[0:128, 0:TT], lhsT,
                                    xn[:, dc * T + tt * TT:dc * T + tt * TT + TT],
                                    start=(dc == 0), stop=(dc == NC - 1))
                w = [conv[(0, dc)]]
                if dc == 0:
                    w += list(freeA)
                mg = p.op("tensor", mm, waits=tuple(w), track=(dc == NC - 1))
            tsg = p.op("scalar", lambda s: s.activation(
                sg, bankA[0:128, 0:TT], AF.Silu), waits=(mg,) + sg_free)
            freeA = (tsg,)
            mu = None
            for dc in range(NC):
                lhsT = wbf[:, (par * 16 + 8 + dc) * 128:
                           (par * 16 + 8 + dc) * 128 + 128]

                def mm(t, lhsT=lhsT, dc=dc, tt=tt):
                    return t.matmul(bankB[0:128, 0:TT], lhsT,
                                    xn[:, dc * T + tt * TT:dc * T + tt * TT + TT],
                                    start=(dc == 0), stop=(dc == NC - 1))
                w = [conv[(1, dc)]]
                if dc == 0:
                    w += list(freeB)
                mu = p.op("tensor", mm, waits=tuple(w), track=(dc == NC - 1))
            asl = arb[:, AOF + ocp * T + tt * TT:AOF + ocp * T + tt * TT + TT]
            ta = p.op("vector", lambda v, asl=asl: v.tensor_tensor(
                asl, sg, bankB[0:128, 0:TT], ALU.mult), waits=(tsg, mu))
            freeB = (ta,)
            sg_free = (ta,)
            a_toks.append(ta)
            last_mm = mu
        ws.mm_last[par] = (last_mm,)

    rows = b["rows"]
    ps5 = b["ps5"]
    onesrow = cin["c_ones_row_f"]
    cw_state = {"tt": -1, "tok": (), "cwb_users": ()}

    def ocb(oc, tt, ps, mmtok):
        if not weighted:
            return p.op("vector", lambda v, oc=oc, tt=tt, ps=ps: v.tensor_tensor(
                b["xres"][:, oc * T + tt * TT:oc * T + tt * TT + TT],
                b["xres"][:, oc * T + tt * TT:oc * T + tt * TT + TT],
                ps[0:128, 0:TT], ALU.add), waits=(mmtok,))
        bt = p.op("tensor", lambda t, tt=tt: t.matmul(
            ps5[0:128, 0:TT], onesrow[0:1, :],
            rows[0:1, tt * TT:tt * TT + TT], start=True, stop=True),
            waits=(cw_row_tok,) + cw_state["tok"])
        ev = p.op("scalar", lambda s: s.copy(cwb, ps5[0:128, 0:TT]),
                  waits=(bt,) + cw_state["cwb_users"])
        cw_state["tok"] = (ev,)
        t1 = p.op("vector", lambda v, ps=ps: v.tensor_tensor(
            ps[0:128, 0:TT], ps[0:128, 0:TT], cwb, ALU.mult),
            waits=(mmtok, ev))
        t2 = p.op("vector", lambda v, oc=oc, tt=tt, ps=ps: v.tensor_tensor(
            b["xres"][:, oc * T + tt * TT:oc * T + tt * TT + TT],
            b["xres"][:, oc * T + tt * TT:oc * T + tt * TT + TT],
            ps[0:128, 0:TT], ALU.add))
        cw_state["cwb_users"] = (t1,)
        return t2

    emit_linear(ctx, ws, lambda dc, tt: arb[:, AOF + dc * T + tt * TT:
                                            AOF + dc * T + tt * TT + TT],
                wd_dram, 2816, D, ocb, first_waits=tuple(a_toks[-1:]),
                bank_pair=("ps3", "ps4"), conv_engines=("scalar", "gpsimd"),
                G=128, nbuf=1)
    return p.fence("vector")


def emit_moe(ctx, upto="full"):
    p, b, dram = ctx.p, ctx.b, ctx.dram
    xn, t_xn = emit_ln(ctx, 0, xb_off=AOF)
    if upto == "ln":
        return t_xn
    CW, cw_tok = emit_moe_gate(ctx, xn, t_xn)
    if upto == "gate":
        return cw_tok
    ws = WStream(ctx)
    seed = (cw_tok, p.fence("tensor"))
    ws.wslot_users = [seed, seed]
    ws.mm_last = [seed, seed]
    tok = emit_moe_mlp(ctx, xn, dram["moe_base_wgu"], dram["moe_base_wd"], ws,
                       (t_xn,), None, weighted=False)
    if upto == "base":
        return tok
    prev_cw = ()
    for e in range(8):
        crt = emit_cw_row(ctx, CW, e, cw_tok, prev=prev_cw)
        prev_cw = (crt,)
        wgu_e = dram["moe_exp_wgu"].rearrange("(e h) i -> e h i", e=8)[e]
        wd_e = dram["moe_exp_wd"].rearrange("(e i) h -> e i h", e=8)[e]
        tok = emit_moe_mlp(ctx, xn, wgu_e, wd_e, ws, (tok,), crt,
                           weighted=True)
        prev_cw = (tok, p.fence("tensor"))
    return tok


def build_sub_moe(upto="full"):
    nc = bass.Bass("TRN2", target_bir_lowering=False)
    p = Prog(nc)
    ctx = Ctx(nc, p)
    xin = nc.dram_tensor("xin", [D, T], F32, kind="ExternalInput").ap()
    xout = nc.dram_tensor("xout", [D, T], F32, kind="ExternalOutput").ap()
    ctx.dram["moe_gate_w"] = nc.dram_tensor("moe_gate_w", [D, 8], F32,
                                            kind="ExternalInput").ap()
    ctx.dram["moe_base_wgu"] = nc.dram_tensor("moe_base_wgu", [D, 5632], F32,
                                              kind="ExternalInput").ap()
    ctx.dram["moe_base_wd"] = nc.dram_tensor("moe_base_wd", [2816, D], F32,
                                             kind="ExternalInput").ap()
    ctx.dram["moe_exp_wgu"] = nc.dram_tensor("moe_exp_wgu", [8 * D, 5632], F32,
                                             kind="ExternalInput").ap()
    ctx.dram["moe_exp_wd"] = nc.dram_tensor("moe_exp_wd", [8 * 2816, D], F32,
                                            kind="ExternalInput").ap()
    cnames = ["c_ident", "c_ident_f", "c_ones_row", "c_ones_row_f",
              "c_ones_col", "c_scal"]
    for n in cnames:
        shape, dt = CONST_SPECS[n]
        ctx.dram[n] = nc.dram_tensor(n, shape, dt, kind="ExternalInput").ap()
    alloc_shared(ctx)
    load_consts(ctx, cnames)
    xres = ctx.b["xres"]
    tlds = [p.dma("gpsimd", xres[:, c * T:(c + 1) * T],
                  xin[c * 128:(c + 1) * 128, :], chan="xio") for c in range(NC)]
    startw = (p.dfence("xio"), p.dfence("misc"))
    for e in ("vector", "tensor", "scalar", "gpsimd"):
        p.op(e, None, waits=startw)
    tdone = emit_moe(ctx, upto)
    for c in range(NC):
        p.dma("gpsimd", xout[c * 128:(c + 1) * 128, :],
              xres[:, c * T:(c + 1) * T],
              waits=(tdone, p.fence("scalar"), p.fence("tensor"),
                     p.fence("vector")) if c == 0 else (), chan="xio")
    p.op("gpsimd", None, waits=(p.dfence("xio"),))
    return p.build()




def emit_quantize(ctx, w_dram, wq_dram, n_in, n_out, gslot, first_waits=()):
    p, b, cin = ctx.p, ctx.b, ctx.cin
    wf32 = b["wf32"]
    wbf = b["wbf"]
    arf = b["arf"]
    acc = arf[:, 768 + gslot:769 + gslot]
    gcol = arf[:, 772 + gslot:773 + gslot]
    ps6 = b["ps6"]
    onescol = cin["c_ones_col"]
    onesrow = cin["c_ones_row_f"]
    onescolf = cin["c_ones_col_f"]
    ncol = n_out // WG
    ntile = (n_in // 128) * ncol
    ch = ctx.name("qz")
    t_acc = p.op("vector", lambda v: v.memset(acc, 0.0), waits=first_waits)
    slot_free = [tuple(first_waits), tuple(first_waits)]
    red = arf[:, 774 + gslot:775 + gslot]
    for i in range(ntile):
        a, c0 = i // ncol, (i % ncol) * WG
        slot = i % 2
        dst = wf32[:, slot * WG:slot * WG + WG]
        dtok = p.dma("sync", dst,
                     w_dram[a * 128:(a + 1) * 128, c0:c0 + WG],
                     waits=slot_free[slot], chan=ch + str(slot))
        t1 = p.op("vector", lambda v, dst=dst: v.tensor_reduce(
            red, dst, AX.X, ALU.add, apply_absolute_value=True),
            waits=(dtok, t_acc))
        t_acc = p.op("vector", lambda v: v.tensor_tensor(acc, acc, red,
                                                         ALU.add))
        slot_free[slot] = (t_acc,)
    tmm = p.op("tensor", lambda t: t.matmul(ps6[0:1, 0:1], onescolf[:, 0:1],
                                            acc, start=True, stop=True),
               waits=(t_acc,))
    gl = b["rows"][0:1, 4:5]
    t2 = p.op("scalar", lambda s: s.activation(
        gl, ps6[0:1, 0:1], AF.Identity, scale=1.0 / (n_in * n_out),
        bias=cin["c_scal"][0:1, 0:1]), waits=(tmm,))
    t3 = p.op("vector", lambda v: v.reciprocal(gl, gl), waits=(t2,))
    t4 = p.op("tensor", lambda t: t.matmul(ps6[0:128, 1:2], onesrow[0:1, :],
                                           gl, start=True, stop=True),
               waits=(t3,))
    t5 = p.op("scalar", lambda s: s.copy(gcol, ps6[0:128, 1:2]), waits=(t4,))
    sgn = arf[:, 0:WG]
    ab = arf[:, 512:512 + WG]
    prev_q = ()
    slot_free = [(t5,), (t5,)]
    wq_free = [tuple(first_waits), tuple(first_waits)]
    for i in range(ntile):
        a, c0 = i // ncol, (i % ncol) * WG
        slot = i % 2
        dst = wf32[:, slot * WG:slot * WG + WG]
        dtok = p.dma("sync", dst,
                     w_dram[a * 128:(a + 1) * 128, c0:c0 + WG],
                     waits=slot_free[slot], chan=ch + str(slot))
        t6 = p.op("vector", lambda v, dst=dst: v.tensor_scalar(
            dst, dst, gcol, None, ALU.mult), waits=(dtok,) + prev_q)
        t7 = p.op("vector", lambda v, dst=dst: v.tensor_scalar(
            dst, dst, 1.0, -1.0, ALU.min, ALU.max))
        t8 = p.op("scalar", lambda s, dst=dst: s.activation(sgn, dst, AF.Sign),
                  waits=(t7,))
        t9 = p.op("scalar", lambda s, dst=dst: s.activation(ab, dst,
                                                            AF.Square),
                  waits=(t7,))
        t10 = p.op("vector", lambda v: v.tensor_scalar(
            ab, ab, 0.25, None, ALU.is_ge), waits=(t9,))
        qsl = wbf[:, slot * WG:slot * WG + WG]
        t11 = p.op("vector", lambda v, qsl=qsl: v.tensor_tensor(
            qsl, ab, sgn, ALU.mult), waits=(t8,) + wq_free[slot])
        prev_q = (t11,)
        stok = p.dma("gpsimd", wq_dram[a * 128:(a + 1) * 128, c0:c0 + WG],
                     qsl, waits=(t11,), chan=ch + "s" + str(slot))
        slot_free[slot] = (t11,)
        wq_free[slot] = (stok,)
    return p.dfence(ch + "s0"), p.dfence(ch + "s1")


def emit_bitffn(ctx, upto="full"):
    p, b, cin, dram = ctx.p, ctx.b, ctx.cin, ctx.dram
    arb = b["arb"]
    xn, t_xn = emit_ln(ctx, 0, xb_off=AOF)
    bcol = ctx.nc.alloc_sbuf_tensor(ctx.name("bb"), [128, 40], F32)[:]
    bch = ctx.name("bias")
    p.dma("sync", bcol[:, 0:32], dram["bit_b1"], chan=bch)
    p.dma("sync", bcol[:, 32:40], dram["bit_b2"], chan=bch)
    tb = p.dfence(bch)
    qt1, qt2 = ctx.b.get("bit_quant_toks", ((), ()))
    ws = WStream(ctx)
    seed = tuple(qt1) + tuple(qt2)
    ws.mm_last = [seed, seed]
    ws.wslot_users = [seed, seed]
    HOF = AOF
    TH = T // 2
    tok = (t_xn, tb)
    for th in range(2):
        hbuf = arb[:, HOF:HOF + 32 * TH]

        def hcb(oc, tt, ps, mmtok, th=th, hbuf=hbuf):
            sl = slice(oc * TH + (tt - th * 2) * TT, oc * TH + (tt - th * 2) * TT + TT)
            return p.op("scalar", lambda s, sl=sl, ps=ps, oc=oc: s.activation(
                hbuf[:, sl], ps[0:128, 0:TT], AF.Relu,
                bias=bcol[:, oc:oc + 1]), waits=(mmtok,))

        emit_linear(ctx, ws, lambda dc, tt, th=th: xn[
            :, dc * T + th * TH + (tt - th * 2) * TT:
            dc * T + th * TH + (tt - th * 2) * TT + TT],
            ctx.dram["bit_w1q"], D, 4096, hcb,
            first_waits=tok + tuple(qt1), bank_pair=("ps3", "ps4"),
            conv_engines=(), bf16_direct=True, G=WG, nbuf=2,
            tt_list=(th * 2, th * 2 + 1))

        def ocb(oc, tt, ps, mmtok, th=th):
            sl = slice(oc * T + th * TH + (tt - th * 2) * TT,
                       oc * T + th * TH + (tt - th * 2) * TT + TT)
            return p.op("vector", lambda v, sl=sl, ps=ps, oc=oc:
                        v.scalar_tensor_tensor(
                            b["xres"][:, sl], ps[0:128, 0:TT],
                            bcol[:, 32 + oc:33 + oc], b["xres"][:, sl],
                            ALU.add, ALU.add), waits=(mmtok,))

        emit_linear(ctx, ws, lambda dc, tt, th=th, hbuf=hbuf: hbuf[
            :, dc * TH + (tt - th * 2) * TT:dc * TH + (tt - th * 2) * TT + TT],
            ctx.dram["bit_w2q"], 4096, D, ocb,
            first_waits=tuple(qt2) + (p.fence("scalar"),),
            bank_pair=("ps3", "ps4"),
            conv_engines=(), bf16_direct=True, G=128, nbuf=1,
            tt_list=(th * 2, th * 2 + 1))
        tok = (p.fence("vector"),)
    return p.fence("vector")


def build_sub_bitffn(upto="full"):
    nc = bass.Bass("TRN2", target_bir_lowering=False)
    p = Prog(nc)
    ctx = Ctx(nc, p)
    xin = nc.dram_tensor("xin", [D, T], F32, kind="ExternalInput").ap()
    xout = nc.dram_tensor("xout", [D, T], F32, kind="ExternalOutput").ap()
    ctx.dram["bit_w1"] = nc.dram_tensor("bit_w1", [D, 4096], F32,
                                        kind="ExternalInput").ap()
    ctx.dram["bit_w2"] = nc.dram_tensor("bit_w2", [4096, D], F32,
                                        kind="ExternalInput").ap()
    ctx.dram["bit_b1"] = nc.dram_tensor("bit_b1", [128, 32], F32,
                                        kind="ExternalInput").ap()
    ctx.dram["bit_b2"] = nc.dram_tensor("bit_b2", [128, 8], F32,
                                        kind="ExternalInput").ap()
    ctx.dram["bit_w1q"] = nc.dram_tensor("bit_w1q", [D, 4096], BF16,
                                         kind="Internal").ap()
    ctx.dram["bit_w2q"] = nc.dram_tensor("bit_w2q", [4096, D], BF16,
                                         kind="Internal").ap()
    cnames = ["c_ident", "c_ones_row", "c_ones_row_f", "c_ones_col",
              "c_ones_col_f", "c_scal"]
    for n in cnames:
        shape, dt = CONST_SPECS[n]
        ctx.dram[n] = nc.dram_tensor(n, shape, dt, kind="ExternalInput").ap()
    alloc_shared(ctx)
    load_consts(ctx, cnames)
    xres = ctx.b["xres"]
    for c in range(NC):
        p.dma("gpsimd", xres[:, c * T:(c + 1) * T],
              xin[c * 128:(c + 1) * 128, :], chan="xio")
    startw = (p.dfence("xio"), p.dfence("misc"))
    for e in ("vector", "tensor", "scalar", "gpsimd"):
        p.op(e, None, waits=startw)
    q1 = emit_quantize(ctx, ctx.dram["bit_w1"], ctx.dram["bit_w1q"], D, 4096, 0)
    q2 = emit_quantize(ctx, ctx.dram["bit_w2"], ctx.dram["bit_w2q"], 4096, D, 1,
                       first_waits=(p.fence("vector"), p.fence("scalar")) + q1)
    ctx.b["bit_quant_toks"] = (q1, q2)
    tdone = emit_bitffn(ctx, upto)
    for c in range(NC):
        p.dma("gpsimd", xout[c * 128:(c + 1) * 128, :],
              xres[:, c * T:(c + 1) * T],
              waits=(tdone, p.fence("scalar"), p.fence("tensor"),
                     p.fence("vector")) if c == 0 else (), chan="xio")
    p.op("gpsimd", None, waits=(p.dfence("xio"),))
    return p.build()




QROF = 47136
QIOF = 49184
KROF = 51232
KIOF = 53280
CMOF = 55328


def emit_cosformer(ctx, upto="full"):
    p, b, cin, dram = ctx.p, ctx.b, ctx.cin, ctx.dram
    arb, arf = b["arb"], b["arf"]
    xn, t_xn = emit_ln(ctx, 0, xb_off=26624)
    QOF, KOF, VOF, TRIG = 16384, 18432, 20480, 22528
    qP = arb[:, QOF:QOF + T]
    kP = arb[:, KOF:KOF + T]
    vP = arb[:, VOF:VOF + T]
    YB = arb[:, 26624:26624 + NC * T]
    cos = arb[:, TRIG:TRIG + T]
    sin = arb[:, TRIG + T:TRIG + 2 * T]
    qre = arb[:, QROF:QROF + T]
    qim = arb[:, QIOF:QIOF + T]
    kre = arb[:, KROF:KROF + T]
    kim = arb[:, KIOF:KIOF + T]
    p.dma("sync", cos, dram["c_poscos_b"], chan="trig")
    p.dma("sync", sin, dram["c_possin_b"], chan="trig")
    t_trig = p.dfence("trig")
    dcol = ctx.nc.alloc_sbuf_tensor(ctx.name("dcol"), [128, 40], F32)[:]
    dch = ctx.name("delta")
    p.dma("sync", dcol[:, 0:8], dram["cb_delta"], chan=dch)
    t_d = p.dfence(dch)
    PI = math.pi
    t1 = p.op("vector", lambda v: v.tensor_scalar(
        dcol[:, 0:8], dcol[:, 0:8], 0.0, -2.0 * PI, ALU.min, ALU.max),
        waits=(t_d,))
    t2 = p.op("scalar", lambda s: s.activation(dcol[:, 24:32], dcol[:, 0:8],
                                               AF.Sin, scale=0.25), waits=(t1,))
    t3 = p.op("scalar", lambda s: s.activation(dcol[:, 32:40], dcol[:, 24:32],
                                               AF.Square), waits=(t2,))
    t4 = p.op("vector", lambda v: v.tensor_scalar(
        dcol[:, 32:40], dcol[:, 32:40], -1.0, None, ALU.mult), waits=(t3,))
    t5 = p.op("scalar", lambda s: s.activation(
        dcol[:, 32:40], dcol[:, 32:40], AF.Sqrt,
        bias=cin["c_scal"][:, 1:2]), waits=(t4,))
    t6 = p.op("vector", lambda v: v.tensor_tensor(
        dcol[:, 8:16], dcol[:, 24:32], dcol[:, 32:40], ALU.mult), waits=(t5,))
    t7 = p.op("vector", lambda v: v.tensor_scalar(
        dcol[:, 8:16], dcol[:, 8:16], 2.0, None, ALU.mult))
    t8 = p.op("scalar", lambda s: s.activation(
        dcol[:, 16:24], dcol[:, 24:32], AF.Square), waits=(t7,))
    t9 = p.op("vector", lambda v: v.tensor_scalar(
        dcol[:, 16:24], dcol[:, 16:24], -2.0, 1.0, ALU.mult, ALU.add),
        waits=(t8,))
    t10 = p.op("vector", lambda v: v.tensor_tensor(
        dcol[:, 24:32], dcol[:, 8:16], dcol[:, 16:24], ALU.mult),
        waits=(t9, t8))
    t11 = p.op("vector", lambda v: v.tensor_scalar(
        dcol[:, 24:32], dcol[:, 24:32], 2.0, None, ALU.mult))
    t12 = p.op("scalar", lambda s: s.activation(
        dcol[:, 32:40], dcol[:, 8:16], AF.Square), waits=(t11,))
    t13 = p.op("vector", lambda v: v.tensor_scalar(
        dcol[:, 32:40], dcol[:, 32:40], -2.0, 1.0, ALU.mult, ALU.add),
        waits=(t12,))
    t_dc = t13

    ws = WStream(ctx)
    t_ones = p.op("vector", lambda v: v.memset(
        b["vt"].rearrange("p (k c) -> p k c", c=130)[:, :, 64:65], 1.0))
    t_ones2 = p.op("vector", lambda v: v.memset(
        b["vt"].rearrange("p (k c) -> p k c", c=130)[:, :, 129:130], 1.0))
    prev_attn = (t_xn, t_trig, t_dc)
    for pair in range(NC):
        toks = {"q": [], "k": [], "v": []}
        mu_toks = {"q": [], "k": []}

        def mk_cb(dst, which):
            def cb(oc, tt, ps, mmtok):
                if which == "v":
                    tok = p.op("scalar", lambda s, tt=tt, ps=ps: s.copy(
                        dst[:, tt * TT:tt * TT + TT], ps[0:128, 0:TT]),
                        waits=(mmtok,))
                    toks[which].append(tok)
                    return tok
                tA = arf[:, 0:512]
                tB = b["usb"]
                o1 = p.op("scalar", lambda s, ps=ps: s.activation(
                    tA, ps[0:128, 0:TT], AF.Abs), waits=(mmtok,))
                o2 = p.op("scalar", lambda s: s.activation(
                    tA, tA, AF.Exp, scale=-1.0))
                o3 = p.op("scalar", lambda s: s.activation(
                    tA, tA, AF.Ln, bias=cin["c_scal"][:, 1:2]))
                o4 = p.op("scalar", lambda s, ps=ps: s.activation(
                    tB, ps[0:128, 0:TT], AF.Relu))
                o5 = p.op("vector", lambda v, tt=tt: v.tensor_tensor(
                    dst[:, tt * TT:tt * TT + TT], tA, tB, ALU.add),
                    waits=(o3, o4))
                toks[which].append(o5)
                mu_toks[which].append(o5)
                return o5
            return cb
        for which, dst, coff in (("q", qP, 0), ("k", kP, D),
                                 ("v", vP, 2 * D)):
            emit_linear(ctx, ws, lambda dc, tt: xn[:, dc * T + tt * TT:
                                                   dc * T + tt * TT + TT],
                        dram["cb_w_attn"], D, 128, mk_cb(dst, which),
                        first_waits=prev_attn, col_off=coff + pair * 128,
                        bank_pair=("ps3", "ps4"),
                        conv_engines=("vector", "scalar"))
            prev_attn = ()
        rot = []
        for tt in range(NTT):
            sl = slice(tt * TT, tt * TT + TT)
            w0 = (mu_toks["q"][-1], mu_toks["k"][-1]) if tt == 0 else ()
            rot.append(p.op("vector", lambda v, sl=sl: v.tensor_tensor(
                qre[:, sl], qP[:, sl], cos[:, sl], ALU.mult), waits=w0))
            rot.append(p.op("vector", lambda v, sl=sl: v.tensor_tensor(
                qim[:, sl], qP[:, sl], sin[:, sl], ALU.mult)))
            cpd = arf[:, 0:512]
            spd = b["usb"]
            p.op("vector", lambda v, sl=sl, pair=pair: v.tensor_scalar(
                cpd, sin[:, sl], dcol[:, 24 + pair:25 + pair], None, ALU.mult))
            p.op("vector", lambda v, sl=sl, pair=pair: v.scalar_tensor_tensor(
                cpd, cos[:, sl], dcol[:, 32 + pair:33 + pair], cpd,
                ALU.mult, ALU.subtract))
            p.op("vector", lambda v, sl=sl, pair=pair: v.tensor_scalar(
                spd, cos[:, sl], dcol[:, 24 + pair:25 + pair], None, ALU.mult))
            p.op("vector", lambda v, sl=sl, pair=pair: v.scalar_tensor_tensor(
                spd, sin[:, sl], dcol[:, 32 + pair:33 + pair], spd,
                ALU.mult, ALU.add))
            rot.append(p.op("vector", lambda v, sl=sl: v.tensor_tensor(
                kre[:, sl], kP[:, sl], cpd, ALU.mult)))
            rot.append(p.op("vector", lambda v, sl=sl: v.tensor_tensor(
                kim[:, sl], kP[:, sl], spd, ALU.mult)))
        if upto == "rot0":
            return rot[-1]
        t_vt = emit_vtok(ctx, vP, first_waits=tuple(toks["v"]))
        pre = tuple(rot[-4:]) + (t_vt, t_ones, t_ones2)

        def q_lhs2(h, qt, part):
            buf = qre if part == 0 else qim
            return buf[h * 64:h * 64 + 64, qt * TT:qt * TT + TT]

        def k_pair2(h, kb, part):
            buf = kre if part == 0 else kim
            return buf[h * 64:h * 64 + 64, kb * 128:kb * 128 + 128]

        yt = emit_attn_pair(
            ctx, pair,
            q_lhs=q_lhs2, k_pair=k_pair2,
            y_dst=lambda h, qt, pair=pair: YB[
                h * 64:h * 64 + 64,
                pair * T + qt * TT:pair * T + qt * TT + TT],
            causal=True, scale=0.125, first_waits=pre, two_part=True)
        prev_attn = tuple(yt[-2:])
    if upto == "pairs":
        return prev_attn[-1]

    def ocb(oc, tt, ps, mmtok):
        sl = slice(oc * T + tt * TT, oc * T + tt * TT + TT)
        return p.op("vector", lambda v, sl=sl, ps=ps: v.tensor_tensor(
            b["xres"][:, sl], b["xres"][:, sl], ps[0:128, 0:TT], ALU.add),
            waits=(mmtok,))
    emit_linear(ctx, ws, lambda dc, tt: YB[:, dc * T + tt * TT:
                                           dc * T + tt * TT + TT],
                dram["cb_w_proj"], D, D, ocb, first_waits=prev_attn,
                bank_pair=("ps3", "ps4"), conv_engines=("scalar", "gpsimd"))
    return p.fence("vector")


def build_sub_cosformer(upto="full"):
    nc = bass.Bass("TRN2", target_bir_lowering=False)
    p = Prog(nc)
    ctx = Ctx(nc, p)
    xin = nc.dram_tensor("xin", [D, T], F32, kind="ExternalInput").ap()
    xout = nc.dram_tensor("xout", [D, T], F32, kind="ExternalOutput").ap()
    ctx.dram["cb_w_attn"] = nc.dram_tensor("cb_w_attn", [D, 3 * D], F32,
                                           kind="ExternalInput").ap()
    ctx.dram["cb_w_proj"] = nc.dram_tensor("cb_w_proj", [D, D], F32,
                                           kind="ExternalInput").ap()
    ctx.dram["cb_delta"] = nc.dram_tensor("cb_delta", [128, 8], F32,
                                          kind="ExternalInput").ap()
    ctx.dram["c_poscos_b"] = nc.dram_tensor("c_poscos_b", [128, T], BF16,
                                            kind="ExternalInput").ap()
    ctx.dram["c_possin_b"] = nc.dram_tensor("c_possin_b", [128, T], BF16,
                                            kind="ExternalInput").ap()
    cnames = ["c_ident", "c_ones_row", "c_ones_row_f", "c_ones_col",
              "c_scal"]
    for n in cnames + ["c_cmask"]:
        shape, dt = CONST_SPECS[n]
        ctx.dram[n] = nc.dram_tensor(n, shape, dt, kind="ExternalInput").ap()
    alloc_shared(ctx)
    load_consts(ctx, cnames)
    ctx.cin["c_cmask"] = ctx.b["arb"][:, CMOF:CMOF + 2048]
    p.dma("sync", ctx.cin["c_cmask"], ctx.dram["c_cmask"], chan="misc")
    xres = ctx.b["xres"]
    for c in range(NC):
        p.dma("gpsimd", xres[:, c * T:(c + 1) * T],
              xin[c * 128:(c + 1) * 128, :], chan="xio")
    startw = (p.dfence("xio"), p.dfence("misc"))
    for e in ("vector", "tensor", "scalar", "gpsimd"):
        p.op(e, None, waits=startw)
    tdone = emit_cosformer(ctx, upto)
    for c in range(NC):
        p.dma("gpsimd", xout[c * 128:(c + 1) * 128, :],
              xres[:, c * T:(c + 1) * T],
              waits=(tdone, p.fence("scalar"), p.fence("tensor"),
                     p.fence("vector")) if c == 0 else (), chan="xio")
    p.op("gpsimd", None, waits=(p.dfence("xio"),))
    return p.build()




KW0, KW1 = 0, 8192
QW0, QW1 = 16384, 24576
VW0, VW1 = 32768, 34816
YSOF = 36864
SW = 16


def emit_hybrid(ctx, upto="full"):
    p, b, cin, dram = ctx.p, ctx.b, ctx.cin, ctx.dram
    nc = ctx.nc
    arb, arf = b["arb"], b["arf"]
    xres = b["xres"]
    xn, t_xn = emit_ln(ctx, 0, xb_off=26624)
    QOF, KOF, VOF = 16384, 18432, 20480
    qP = arb[:, QOF:QOF + T]
    kP = arb[:, KOF:KOF + T]
    vP = arb[:, VOF:VOF + T]
    YB = arb[:, 26624:26624 + NC * T]
    GOF, AOF2 = 47136, 55328
    gbuf = arb[:, GOF:GOF + 4 * T]
    abuf = arf[:, 300:556]
    bcol = arf[:, 0:44]
    b2row = arf[:, 44:60]
    bch = ctx.name("hbias")
    p.dma("sync", bcol[:, 0:24], dram["hy_bqkv"], chan=bch, waits=(t_xn,))
    p.dma("sync", bcol[:, 24:32], dram["hy_bo"], chan=bch)
    p.dma("sync", bcol[:, 32:36], dram["hy_bg1"], chan=bch)
    p.dma("sync", b2row, dram["hy_b2row"], chan=bch)
    tb_tok = p.dfence(bch)

    ws = WStream(ctx)
    t_ones = p.op("vector", lambda v: v.memset(
        b["vt"].rearrange("p (k c) -> p k c", c=130)[:, :, 64:65], 1.0))
    t_ones2 = p.op("vector", lambda v: v.memset(
        b["vt"].rearrange("p (k c) -> p k c", c=130)[:, :, 129:130], 1.0))
    cmask = ctx.cin["c_cmask"]

    qkv_dma = []
    prev_attn = (t_xn, tb_tok)
    for pair in range(NC):
        toks = {"q": [], "k": [], "v": []}

        def mk_cb(dst, which, pair=pair):
            def cb(oc, tt, ps, mmtok):
                bc = bcol[:, {"q": 0, "k": 8, "v": 16}[which] + pair:
                          {"q": 0, "k": 8, "v": 16}[which] + pair + 1]
                tok = p.op("scalar", lambda s, tt=tt, ps=ps: s.activation(
                    dst[:, tt * TT:tt * TT + TT], ps[0:128, 0:TT], AF.Identity,
                    bias=bc), waits=(mmtok,))
                toks[which].append(tok)
                return tok
            return cb
        for which, dst, coff in (("q", qP, 0), ("k", kP, D), ("v", vP, 2 * D)):
            emit_linear(ctx, ws, lambda dc, tt: xn[:, dc * T + tt * TT:
                                                   dc * T + tt * TT + TT],
                        dram["hy_wqkv"], D, 128, mk_cb(dst, which),
                        first_waits=prev_attn, col_off=coff + pair * 128,
                        bank_pair=("ps3", "ps4"),
                        conv_engines=("vector", "scalar"))
            prev_attn = ()
            dname = {"q": "hy_qd", "k": "hy_kd", "v": "hy_vd"}[which]
            qkv_dma.append(p.dma(
                "gpsimd", dram[dname][pair * 128:(pair + 1) * 128, :], dst,
                waits=(toks[which][-1],), chan="qkvst_" + which))
        t_vt = emit_vtok(ctx, vP, first_waits=tuple(toks["v"]))
        pre = tuple(toks["q"]) + tuple(toks["k"]) + (t_vt, t_ones, t_ones2)
        yt = emit_attn_pair(
            ctx, pair,
            q_lhs=lambda h, qt: qP[h * 64:h * 64 + 64, qt * TT:qt * TT + TT],
            k_pair=lambda h, kb: kP[h * 64:h * 64 + 64, kb * 128:kb * 128 + 128],
            y_dst=lambda h, qt, pair=pair: YB[
                h * 64:h * 64 + 64,
                pair * T + qt * TT:pair * T + qt * TT + TT],
            causal=True, scale=0.125, first_waits=pre)
        prev_attn = tuple(yt[-2:])
        stage = b["ebuf"][:, 0:128]
        pst = b["pst"]
        ident = cin["c_ident"]
        sp_prev = prev_attn
        for which, buf, dnm in (("q", qP, "hy_qt"), ("k", kP, "hy_kt"),
                                ("v", vP, "hy_vt")):
            for tb in range(NKB):
                tr = p.op("tensor", lambda te, buf=buf, tb=tb: te.transpose(
                    pst[:, 128:256], buf[:, tb * 128:(tb + 1) * 128],
                    ident), waits=sp_prev)
                ev = p.op("scalar", lambda s: s.copy(stage, pst[:, 128:256]),
                          waits=(tr,))
                dmt = p.dma("gpsimd",
                            dram[dnm][tb * 128:(tb + 1) * 128,
                                      pair * 128:(pair + 1) * 128],
                            stage, waits=(ev,), chan="qkvtr")
                sp_prev = (dmt,)
        prev_attn = prev_attn + (sp_prev[-1],) + tuple(qkv_dma[-3:])
    t_qkvt = p.dfence("qkvtr")
    ysp = []
    for c in range(NC):
        ysp.append(p.dma("gpsimd", dram["hy_yd"][c * 128:(c + 1) * 128, :],
                         YB[:, c * T:(c + 1) * T],
                         waits=prev_attn if c == 0 else (), chan="ysp"))
    t_ysp = p.dfence("ysp")

    gtoks = []

    def gcb(oc, tt, ps, mmtok):
        tok = p.op("scalar", lambda s, oc=oc, tt=tt, ps=ps: s.activation(
            gbuf[:, oc * T + tt * TT:oc * T + tt * TT + TT], ps[0:128, 0:TT],
            AF.Gelu, bias=bcol[:, 32 + oc:33 + oc]), waits=(mmtok,))
        gtoks.append(tok)
        return tok
    emit_linear(ctx, ws, lambda dc, tt: xn[:, dc * T + tt * TT:
                                           dc * T + tt * TT + TT],
                dram["hy_gw1"], D, 512, gcb, first_waits=prev_attn,
                bank_pair=("ps3", "ps4"), conv_engines=("vector", "scalar"))
    gw2 = arb[:, 57632:57696]
    gw2f = arf[:, 64:128]
    gch = ctx.name("gw2")
    dt0 = p.dma("sync", gw2f.rearrange("p (c e) -> p c e", e=16),
                dram["hy_gw2"].rearrange("(c p) e -> p c e", p=128), chan=gch)
    tcv = p.op("scalar", lambda s: s.copy(gw2, gw2f), waits=(dt0,))
    ps5 = b["ps5"]
    a_toks = []
    prev_ev = ()
    for tb in range(NKB):
        mmt = None
        for gc in range(4):
            def mm(t, tb=tb, gc=gc):
                return t.matmul(ps5[0:128, 0:16],
                                gbuf[:, gc * T + tb * 128:gc * T + (tb + 1) * 128],
                                gw2[:, gc * 16:gc * 16 + 16],
                                start=(gc == 0), stop=(gc == 3))
            w = [tcv, gtoks[-1]] if gc == 0 else []
            if gc == 0:
                w += list(prev_ev)
            mmt = p.op("tensor", mm, waits=tuple(w), track=(gc == 3))
        t1 = p.op("vector", lambda v, tb=tb: v.tensor_tensor(
            arf[:, 256:272], ps5[0:128, 0:16], b2row[:, 0:16], ALU.add),
            waits=(mmt, tb_tok))
        t2 = p.op("scalar", lambda s, tb=tb: s.activation(
            abuf[:, tb * 16:(tb + 1) * 16], arf[:, 256:272], AF.Sigmoid),
            waits=(t1,))
        prev_ev = (t2,)
        a_toks.append(t2)
    if upto == "gate":
        return a_toks[-1]

    xsp = []
    for c in range(NC):
        xsp.append(p.dma("gpsimd", dram["hy_xd"][c * 128:(c + 1) * 128, :],
                         xres[:, c * T:(c + 1) * T], chan="xsp"))
    t_xsp = p.dfence("xsp")
    Rt = xres[:, 0:512]
    Gs = xres[:, 512:1024]
    QG = xres[:, 1024:1536]
    Mb = xres[:, 1536:2048]
    lam = xres[:, 2048:2560]
    wst = xres[:, 2560:3072]
    sch = ctx.name("scan")
    d1 = p.dma("sync", lam, dram["hy_lam"], waits=(t_xsp,), chan=sch)
    d2 = p.dma("sync", Gs, dram["hy_gam"], chan=sch)
    d3 = p.dma("sync", wst, dram["hy_wst"], chan=sch)
    t_sc = p.dfence(sch)
    tR = p.op("scalar", lambda s: s.activation(Rt, lam, AF.Sigmoid,
                                               scale=-1.0), waits=(t_sc,))
    wstb = arb[:, 57696:58208]
    t_wstb = p.op("vector", lambda v: v.tensor_copy(wstb, wst), waits=(t_sc,))
    tG = p.op("vector", lambda v: v.tensor_scalar(Gs, Gs, 0.125, None,
                                                  ALU.mult), waits=(t_sc,))
    zts = []
    for off, ln in ((KW0, 8192), (KW1, 8192), (QW0, 8192), (QW1, 8192),
                    (VW0, 2048), (VW1, 2048)):
        zts.append(p.op("vector", lambda v, off=off, ln=ln: v.memset(
            arb[0:16, off:off + ln], 0.0), waits=(t_ysp,) if not zts else ()))
    t_z = zts[-1]
    ybuf = arb[:, YSOF:YSOF + T * 8]

    def refill(t0, buf_par, wait):
        toks = []
        kwo = (KW0, KW1)[buf_par]
        qwo = (QW0, QW1)[buf_par]
        vwo = (VW0, VW1)[buf_par]
        for nm, wo, q in (("hy_kt", kwo, "scalar"), ("hy_qt", qwo, "gpsimd")):
            dd = arb[0:16, wo:wo + SW * 512].rearrange(
                "p (dt f) -> p dt f", dt=SW)
            sv = dram[nm].rearrange("t (h j) -> h t j", h=16)
            for g in range(8):
                toks.append(p.dma(
                    q, dd[2 * g:2 * g + 2, :, g * 64:g * 64 + 64],
                    sv[2 * g:2 * g + 2, t0:t0 + SW, :],
                    waits=wait if g == 0 else (),
                    chan=f"rf_{nm}_{buf_par}"))
        dd = arb[0:16, vwo:vwo + SW * 128].rearrange(
            "p (dt f) -> p dt f", dt=SW)
        sv = dram["hy_vt"].rearrange("t (h i) -> h t i", h=16)
        toks.append(p.dma("gpsimd", dd[0:16:2, :, 0:64],
                          sv[0:16:2, t0:t0 + SW, :], waits=wait,
                          chan=f"rf_v_{buf_par}"))
        toks.append(p.dma("gpsimd", dd[1:16:2, :, 64:128],
                          sv[1:16:2, t0:t0 + SW, :],
                          chan=f"rf_v_{buf_par}"))
        return (p.dfence(f"rf_hy_kt_{buf_par}"), p.dfence(f"rf_hy_qt_{buf_par}"),
                p.dfence(f"rf_v_{buf_par}"))

    d16 = cin["c_delta16"]
    Zb = b["ps0"]
    Qb = b["ps1"]
    win_tok = [None, None]
    win_tok[0] = refill(0, 0, (t_z, t_qkvt))
    win_tok[1] = refill(SW, 1, (t_z,))
    dec_tok = None
    mmh_tok = None
    m_tok = ()
    qg_tok = ()
    mmq_tok = None
    y_toks = []
    win_free = [(), ()]
    for t in range(T):
        par = (t // SW) % 2
        kwo, qwo, vwo = (KW0, KW1)[par], (QW0, QW1)[par], (VW0, VW1)[par]
        dt = t % SW
        if dt == 0 and t > 0:
            opar = par ^ 1
            nxt = t + SW
            if nxt < T:
                win_tok[opar] = refill(nxt, opar, win_free[opar])
        if t > 0:
            dec_tok = p.op("vector", lambda v: v.tensor_tensor(
                Zb[0:128, 0:TT], Zb[0:128, 0:TT], Rt, ALU.mult),
                waits=(mmh_tok, tR) + m_tok)
        vsl = arb[0:16, vwo + dt * 128:vwo + dt * 128 + 128]
        ksl = arb[0:16, kwo + dt * 512:kwo + dt * 512 + 512]
        qsl = arb[0:16, qwo + dt * 512:qwo + dt * 512 + 512]
        w = [win_tok[par][0], win_tok[par][2]] if dt == 0 else []
        if t > 0:
            w.append(dec_tok)
        mmh_tok = p.op("tensor", lambda te, vsl=vsl, ksl=ksl, t=t: te.matmul(
            Zb[0:128, 0:TT], vsl, ksl, start=(t == 0), stop=(t == T - 1),
            skip_group_check=True), waits=tuple(w))
        w2 = [win_tok[par][1]] if dt == 0 else []
        w2 += list(qg_tok)
        mmq_tok = p.op("tensor", lambda te, qsl=qsl: te.matmul(
            Qb[0:128, 0:TT], d16, qsl, start=True, stop=True),
            waits=tuple(w2))
        qg_tok = (p.op("vector", lambda v: v.tensor_tensor(
            QG, Qb[0:128, 0:TT], Gs, ALU.mult), waits=(mmq_tok, tG)),)
        m_tok = (p.op("vector", lambda v: v.tensor_tensor(
            Mb, Zb[0:128, 0:TT], QG, ALU.mult), waits=(mmh_tok,) + qg_tok),)
        def _red(v, t=t):
            with ctx.nc.allow_low_precision(reason="bf16 y_stp store"):
                return v.tensor_reduce(
                    ybuf[:, t * 8:t * 8 + 8],
                    Mb.rearrange("p (hh j) -> p hh j", hh=8), AX.X, ALU.add)
        yv = p.op("vector", _red, waits=m_tok)
        y_toks.append(yv)
        if dt == SW - 1:
            win_free[par] = (mmh_tok, mmq_tok)
    t_scan = y_toks[-1]
    if upto == "scan":
        return t_scan

    psA, psB = b["ps2"], b["ps3"]
    wtoks = []
    prevA = ()
    prevB = ()
    for pair in range(NC):
        for tt in range(NTT):
            qsl = arb[:, 53248 + (tt % 2) * 2048:53248 + (tt % 2) * 2048 + TT]
            dq = p.dma("sync", qsl,
                       dram["hy_qd"][pair * 128:(pair + 1) * 128,
                                     tt * TT:tt * TT + TT],
                       waits=prevA + prevB + ((a_toks[-1],) if pair == 0
                                              and tt == 0 else ()),
                       chan=f"wq{tt % 2}")
            mA = p.op("tensor", lambda te, qsl=qsl, pair=pair: te.matmul(
                psA[0:64, 0:TT], wstb[0:64, pair * 64:pair * 64 + 64],
                qsl[0:64, :], start=True, stop=True),
                waits=(dq, t_wstb) + prevA)
            mB = p.op("tensor", lambda te, qsl=qsl, pair=pair: te.matmul(
                psB[0:64, 0:TT], wstb[64:128, pair * 64:pair * 64 + 64],
                qsl[64:128, :], start=True, stop=True),
                waits=(dq,) + prevB)
            yA = ybuf[0:64, :].rearrange("p (t hh) -> p t hh", hh=8)[
                :, tt * TT:tt * TT + TT, pair:pair + 1]
            yB = ybuf[64:128, :].rearrange("p (t hh) -> p t hh", hh=8)[
                :, tt * TT:tt * TT + TT, pair:pair + 1]
            wA = p.op("vector", lambda v, yA=yA: v.tensor_tensor(
                yA, yA, psA[0:64, 0:TT].rearrange("p (t o) -> p t o", o=1),
                ALU.add), waits=(mA, t_scan))
            wB = p.op("vector", lambda v, yB=yB: v.tensor_tensor(
                yB, yB, psB[0:64, 0:TT].rearrange("p (t o) -> p t o", o=1),
                ALU.add), waits=(mB,))
            prevA = (wA,)
            prevB = (wB,)
            wtoks += [wA, wB]
    t_w = wtoks[-1]
    if upto == "wstat":
        return t_w

    xrl = []
    for c in range(NC):
        xrl.append(p.dma("gpsimd", xres[:, c * T:(c + 1) * T],
                         dram["hy_xd"][c * 128:(c + 1) * 128, :],
                         waits=(t_w, p.fence("vector")) if c == 0 else (),
                         chan="xsp"))
    t_xrl = p.dfence("xsp")
    onesrow = cin["c_ones_row_f"]
    identf = cin["c_ident_f"]
    ps6 = b["ps6"]
    rows = b["rows"]
    YC = arb[:, 26624:26624 + NC * T]
    scb = b["ps5"]
    prev_cmb = ()
    for pair in range(NC):
        ysl = arb[:, 16384 + (pair % 2) * T:16384 + (pair % 2) * T + T]
        dy = p.dma("sync", ysl,
                   dram["hy_yd"][pair * 128:(pair + 1) * 128, :],
                   waits=(t_ysp,) + prev_cmb, chan=f"yrl{pair % 2}")
        for hl in (0, 1):
            h = 2 * pair + hl
            art = ()
            for tb in range(NKB):
                t1 = p.op("tensor", lambda te, tb=tb, h=h: te.transpose(
                    ps6[0:1, 0:128], abuf[:, tb * 16 + h:tb * 16 + h + 1],
                    identf), waits=(a_toks[-1],) + art)
                e1 = p.op("scalar", lambda s, tb=tb: s.copy(
                    rows[0:1, tb * 128:(tb + 1) * 128], ps6[0:1, 0:128]),
                    waits=(t1,))
                art = (e1,)
            for tt in range(NTT):
                bt = p.op("tensor", lambda te, tt=tt: te.matmul(
                    scb[0:64, 0:TT], onesrow[0:1, 0:64],
                    rows[0:1, tt * TT:tt * TT + TT], start=True, stop=True),
                    waits=art + prev_cmb)
                ysm = ysl[hl * 64:hl * 64 + 64, tt * TT:tt * TT + TT]
                yst = ybuf[hl * 64:hl * 64 + 64, :].rearrange(
                    "p (t hh) -> p t hh", hh=8)[:, tt * TT:tt * TT + TT,
                                                pair:pair + 1]
                dst = YC[hl * 64:hl * 64 + 64,
                         pair * T + tt * TT:pair * T + tt * TT + TT]
                c1 = p.op("vector", lambda v, ysm=ysm, yst=yst, dst=dst:
                          v.tensor_tensor(
                              dst, ysm,
                              yst.rearrange("p t o -> p (t o)"), ALU.subtract),
                          waits=(dy, bt))
                c2 = p.op("vector", lambda v, dst=dst: v.tensor_tensor(
                    dst, dst, scb[0:64, 0:TT], ALU.mult))
                c3 = p.op("vector", lambda v, yst=yst, dst=dst: v.tensor_tensor(
                    dst, dst, yst.rearrange("p t o -> p (t o)"), ALU.add))
                prev_cmb = (c3,)
    t_cmb = prev_cmb[0]
    if upto == "combine":
        return t_cmb

    def ocb(oc, tt, ps, mmtok):
        bc = bcol[:, 24 + oc:24 + oc + 1]
        sl = slice(oc * T + tt * TT, oc * T + tt * TT + TT)
        return p.op("vector", lambda v, sl=sl, ps=ps, bc=bc:
                    v.scalar_tensor_tensor(xres[:, sl], ps[0:128, 0:TT],
                                           bc, xres[:, sl], ALU.add, ALU.add),
                    waits=(mmtok,))
    emit_linear(ctx, ws, lambda dc, tt: YC[:, dc * T + tt * TT:
                                           dc * T + tt * TT + TT],
                dram["hy_wo"], D, D, ocb,
                first_waits=(t_cmb, t_xrl), bank_pair=("ps3", "ps4"),
                conv_engines=("scalar", "gpsimd"))
    return p.fence("vector")


def build_sub_hybrid(upto="full"):
    nc = bass.Bass("TRN2", target_bir_lowering=False)
    p = Prog(nc)
    ctx = Ctx(nc, p)
    xin = nc.dram_tensor("xin", [D, T], F32, kind="ExternalInput").ap()
    xout = nc.dram_tensor("xout", [D, T], F32, kind="ExternalOutput").ap()
    for nm, shape in (("hy_wqkv", [D, 3 * D]), ("hy_wo", [D, D]),
                      ("hy_gw1", [D, 512]), ("hy_gw2", [512, 16]),
                      ("hy_bqkv", [128, 24]), ("hy_bo", [128, 8]),
                      ("hy_bg1", [128, 4]), ("hy_b2row", [128, 16]),
                      ("hy_lam", [128, 512]), ("hy_gam", [128, 512]),
                      ("hy_wst", [128, 512])):
        ctx.dram[nm] = nc.dram_tensor(nm, shape, F32,
                                      kind="ExternalInput").ap()
    for nm, shape in (("hy_qd", [D, T]), ("hy_kd", [D, T]), ("hy_vd", [D, T]),
                      ("hy_yd", [D, T]), ("hy_qt", [T, D]), ("hy_kt", [T, D]),
                      ("hy_vt", [T, D])):
        ctx.dram[nm] = nc.dram_tensor(nm, shape, BF16, kind="Internal").ap()
    ctx.dram["hy_xd"] = nc.dram_tensor("hy_xd", [D, T], F32,
                                       kind="Internal").ap()
    cnames = ["c_ident", "c_ident_f", "c_ones_row", "c_ones_row_f",
              "c_ones_col", "c_scal", "c_delta16"]
    for n in cnames + ["c_cmask"]:
        shape, dt = CONST_SPECS[n]
        ctx.dram[n] = nc.dram_tensor(n, shape, dt, kind="ExternalInput").ap()
    alloc_shared(ctx)
    load_consts(ctx, cnames)
    ctx.cin["c_cmask"] = ctx.b["arb"][:, 55584:55584 + 2048]
    p.dma("sync", ctx.cin["c_cmask"], ctx.dram["c_cmask"], chan="misc")
    xres = ctx.b["xres"]
    for c in range(NC):
        p.dma("gpsimd", xres[:, c * T:(c + 1) * T],
              xin[c * 128:(c + 1) * 128, :], chan="xio")
    startw = (p.dfence("xio"), p.dfence("misc"))
    for e in ("vector", "tensor", "scalar", "gpsimd"):
        p.op(e, None, waits=startw)
    tdone = emit_hybrid(ctx, upto)
    for c in range(NC):
        p.dma("gpsimd", xout[c * 128:(c + 1) * 128, :],
              xres[:, c * T:(c + 1) * T],
              waits=(tdone, p.fence("scalar"), p.fence("tensor"),
                     p.fence("vector")) if c == 0 else (), chan="xio")
    p.op("gpsimd", None, waits=(p.dfence("xio"),))
    return p.build()


def build_sub_mha(kind, upto="full"):
    nc = bass.Bass("TRN2", target_bir_lowering=False)
    p = Prog(nc)
    ctx = Ctx(nc, p)
    pfx = {"l0a": "w0a", "l1a": "w1a"}[kind]
    xin = nc.dram_tensor("xin", [D, T], F32, kind="ExternalInput").ap()
    xout = nc.dram_tensor("xout", [D, T], F32, kind="ExternalOutput").ap()
    for wname in ("_wq", "_wk", "_wv", "_wo"):
        ctx.dram[pfx + wname] = nc.dram_tensor(pfx + wname, [D, D], F32,
                                               kind="ExternalInput").ap()
    for bname in ("_bq", "_bk", "_bv", "_bo"):
        ctx.dram[pfx + bname] = nc.dram_tensor(pfx + bname, [128, 8], F32,
                                               kind="ExternalInput").ap()
    cnames = ["c_ident", "c_ones_row", "c_ones_row_f", "c_ones_col", "c_scal"]
    lnames = list(cnames)
    if kind == "l1a":
        cnames += ["c_ropecos", "c_ropesin", "c_roperm"]
        lnames += ["c_roperm"]
    for n in cnames:
        shape, dt = CONST_SPECS[n]
        ctx.dram[n] = nc.dram_tensor(n, shape, dt, kind="ExternalInput").ap()
    alloc_shared(ctx)
    load_consts(ctx, lnames)
    xres = ctx.b["xres"]
    tlds = [p.dma("gpsimd", xres[:, c * T:(c + 1) * T],
                  xin[c * 128:(c + 1) * 128, :], chan="xio") for c in range(NC)]
    startw = (p.dfence("xio"), p.dfence("misc"))
    for e in ("vector", "tensor", "scalar", "gpsimd"):
        p.op(e, None, waits=startw)
    tdone = emit_mha(ctx, kind, upto)
    for c in range(NC):
        p.dma("gpsimd", xout[c * 128:(c + 1) * 128, :],
              xres[:, c * T:(c + 1) * T],
              waits=(tdone, p.fence("scalar"), p.fence("tensor"))
              if c == 0 else (), chan="xio")
    p.op("gpsimd", None, waits=(p.dfence("xio"),))
    return p.build()


def prep_mha_weights(kind, inputs):
    pfx = {"l0a": "w0a", "l1a": "w1a"}[kind]
    ipfx = {"l0a": "L0_attn_a", "l1a": "L1_attn_a"}[kind]
    d = {}
    if kind == "l0a":
        for nm in ("Wq", "Wk", "Wv"):
            w = np.asarray(inputs[f"{ipfx}_{nm}"])
            d[pfx + "_w" + nm[1].lower()] = _f32(w.transpose(2, 0, 1).reshape(D, D))
        for nm in ("bq", "bk", "bv"):
            d[pfx + "_" + nm] = _bias_cols(np.asarray(inputs[f"{ipfx}_{nm}"]).reshape(-1))
    else:
        for nm in ("Wq", "Wk", "Wv"):
            w = np.asarray(inputs[f"{ipfx}_{nm}"])
            d[pfx + "_w" + nm[1].lower()] = _f32(w.T)
        for nm in ("bq", "bk", "bv"):
            d[pfx + "_" + nm] = _bias_cols(np.asarray(inputs[f"{ipfx}_{nm}"]))
    wo = np.asarray(inputs[f"{ipfx}_Wo"])
    d[pfx + "_wo"] = _f32(wo.T)
    d[pfx + "_bo"] = _bias_cols(np.asarray(inputs[f"{ipfx}_bo"]))
    return d


def kernel(**inputs):
    raise NotImplementedError


def _prep_all(inputs):
    wd = {}
    wd.update(prep_mha_weights("l0a", inputs))
    wd.update(prep_mha_weights("l1a", inputs))
    wd.update({
        "moe_gate_w": _f32(inputs["L0_ffn_gate_w"]),
        "moe_base_wgu": _f32(inputs["L0_ffn_base_wgu"]),
        "moe_base_wd": _f32(inputs["L0_ffn_base_wd"]),
        "moe_exp_wgu": _f32(np.asarray(inputs["L0_ffn_exp_wgu"]).reshape(
            8 * 1024, 5632)),
        "moe_exp_wd": _f32(np.asarray(inputs["L0_ffn_exp_wd"]).reshape(
            8 * 2816, 1024)),
        "cb_w_attn": _f32(np.asarray(inputs["L0_attn_b_w_attn"]).T),
        "cb_w_proj": _f32(np.asarray(inputs["L0_attn_b_w_proj"]).T),
        "bit_w1": _f32(np.asarray(inputs["L1_ffn_W1"]).T),
        "bit_w2": _f32(np.asarray(inputs["L1_ffn_W2"]).T),
        "bit_b1": _bias_cols(np.asarray(inputs["L1_ffn_b1"])),
        "bit_b2": _bias_cols(np.asarray(inputs["L1_ffn_b2"])),
        "hy_wqkv": _f32(inputs["L1_attn_b_Wqkv_w"]),
        "hy_wo": _f32(inputs["L1_attn_b_out_w"]),
        "hy_gw1": _f32(inputs["L1_attn_b_gate_w1"]),
        "hy_gw2": _f32(inputs["L1_attn_b_gate_w2"]),
        "hy_bqkv": _bias_cols(np.asarray(inputs["L1_attn_b_Wqkv_b"])),
        "hy_bo": _bias_cols(np.asarray(inputs["L1_attn_b_out_b"])),
        "hy_bg1": _bias_cols(np.asarray(inputs["L1_attn_b_gate_b1"])),
        "hy_b2row": _f32(np.tile(np.asarray(
            inputs["L1_attn_b_gate_b2"])[None, :], (128, 1))),
    })
    delta = np.asarray(inputs["L0_attn_b_delta"])
    dcol = np.zeros((128, 8), np.float32)
    for h in range(NH):
        dcol[(h % 2) * 64:(h % 2) * 64 + 64, h // 2] = delta[h]
    wd["cb_delta"] = dcol
    lam = np.asarray(inputs["L1_attn_b_Lambda_raw"])
    gam = np.asarray(inputs["L1_attn_b_Gamma"])
    wst = np.asarray(inputs["L1_attn_b_W_static"])
    lamL = np.zeros((128, 512), np.float32)
    gamL = np.zeros((128, 512), np.float32)
    wstL = np.zeros((128, 512), np.float32)
    for h in range(NH):
        hh, hl = h // 2, h % 2
        lamL[hl * 64:(hl + 1) * 64, hh * 64:(hh + 1) * 64] = lam[h]
        gamL[hl * 64:(hl + 1) * 64, hh * 64:(hh + 1) * 64] = gam[h]
        wstL[hl * 64:(hl + 1) * 64, hh * 64:(hh + 1) * 64] = wst[h].T
    wd["hy_lam"] = lamL
    wd["hy_gam"] = gamL
    wd["hy_wst"] = wstL
    wd.update(host_constants())
    return wd


_SEQ = [("l0a", lambda: build_sub_mha("l0a")),
        ("moe", build_sub_moe),
        ("cos", build_sub_cosformer),
        ("moe", build_sub_moe),
        ("l1a", lambda: build_sub_mha("l1a")),
        ("bit", build_sub_bitffn),
        ("hyb", build_sub_hybrid),
        ("bit", build_sub_bitffn)]


def _declared_inputs(nc):
    names = set()
    for alloc in nc.m.functions[0].allocations:
        if isinstance(alloc, mybir.MemoryLocationSet) and                 alloc.kind == "ExternalInput":
            names.add(alloc.memorylocations[0].name)
    names.discard("partition_id")
    return names


def kernel(**inputs):
    wd = _prep_all(inputs)
    x = np.asarray(inputs["L0_attn_a_hidden_state"])
    xs = [_f32(x[b].T) for b in range(8)]
    for key, builder in _SEQ:
        if key not in _CACHE:
            _CACHE[key] = builder()
        nc = _CACHE[key]
        decl = _declared_inputs(nc)
        in_maps = []
        for b in range(8):
            m = {"xin": xs[b]}
            m.update({k: v for k, v in wd.items() if k in decl})
            in_maps.append(m)
        res = run_bass_kernel_spmd(nc, in_maps, core_ids=list(range(8)))
        xs = [np.ascontiguousarray(res.results[b]["xout"])
              for b in range(8)]
    out = np.stack([xs[b].T for b in range(8)], axis=0)
    return np.ascontiguousarray(out, dtype=np.float32)


# version 03
# speedup vs baseline: 1.1665x; 1.1665x over previous
import math
import numpy as np
import ml_dtypes
import concourse.bass as bass
import concourse.mybir as mybir
from concourse.bass_utils import run_bass_kernel_spmd

F32 = mybir.dt.float32
BF16 = mybir.dt.bfloat16
AF = mybir.ActivationFunctionType
ALU = mybir.AluOpType
AX = mybir.AxisListType

T = 2048
D = 1024
NH = 16
DH = 64
NC = D // 128
TT = 512
NTT = T // TT
NKB = T // 128
WG = 256

_CACHE = {}


class Sem:
    __slots__ = ("name", "n", "h")

    def __init__(self, name):
        self.name = name
        self.n = 0
        self.h = None


class Prog:
    ENGINES = ("sync", "tensor", "vector", "scalar", "gpsimd")

    def __init__(self, nc):
        self.nc = nc
        self.ops = {e: [] for e in self.ENGINES}
        self.sems = []
        self.esem = {e: self.sem("es_" + e) for e in self.ENGINES}
        self.dchans = {}

    def dchan(self, name):
        if name not in self.dchans:
            self.dchans[name] = self.sem("dc_" + name)
        return self.dchans[name]

    def sem(self, name):
        s = Sem(name)
        self.sems.append(s)
        return s

    AUTO_SELF = ("vector", "scalar", "gpsimd")

    def op(self, engine, fn, waits=(), track=True):
        tok = None
        incs = ()
        waits = [w for w in waits if w]
        s = self.esem[engine]
        if engine in self.AUTO_SELF and fn is not None and s.n > 0:
            waits.append((s, s.n))
        if track:
            s.n += 1
            tok = (s, s.n)
            incs = ((s, 1),)
        self.ops[engine].append((fn, tuple(waits), incs))
        return tok

    def dma(self, queue, out, in_, waits=(), chan="misc", **kw):
        s = self.dchan(chan)
        s.n += 16
        tok = (s, s.n)

        def fn(eng):
            return eng.dma_start(out=out, in_=in_, **kw)
        self.ops[queue].append((fn, tuple(w for w in waits if w), ((s, 16),)))
        return tok

    def fence(self, engine):
        s = self.esem[engine]
        return (s, s.n)

    def dfence(self, chan):
        s = self.dchan(chan)
        return (s, s.n)

    def build(self):
        import contextlib
        nc = self.nc
        with contextlib.ExitStack() as stack:
            for s in self.sems:
                s.h = stack.enter_context(nc.semaphore(s.name))
            block = stack.enter_context(nc.Block())
            for ename in self.ENGINES:
                ops = self.ops[ename]
                if not ops:
                    continue

                def body(eng, ops=ops):
                    for fn, waits, incs in ops:
                        for s, v in waits:
                            eng.wait_ge(s.h, v)
                        instr = fn(eng) if fn is not None else None
                        if incs:
                            (s0, k0), rest = incs[0], incs[1:]
                            if instr is not None:
                                instr.then_inc(s0.h, k0)
                            else:
                                eng.nop().then_inc(s0.h, k0)
                            for s, k in rest:
                                eng.nop().then_inc(s.h, k)
                getattr(block, ename)(body)
        return nc


def _bf16(x):
    return np.ascontiguousarray(
        np.asarray(x, dtype=np.float32).astype(ml_dtypes.bfloat16))


def _f32(x):
    return np.ascontiguousarray(x, dtype=np.float32)


def _bias_cols(b):
    n = b.shape[0]
    return _f32(b.reshape(n // 128, 128).T)


def host_constants():
    c = {}
    c["c_ident"] = _bf16(np.eye(128))
    c["c_ident_f"] = _f32(np.eye(128))
    d16 = np.zeros((16, 128), np.float32)
    for h in range(16):
        d16[h, (h % 2) * 64:(h % 2) * 64 + 64] = 1.0
    c["c_delta16"] = _bf16(d16)
    c["c_ones_row"] = _bf16(np.ones((1, 128)))
    c["c_ones_row_f"] = _f32(np.ones((1, 128)))
    sc = np.zeros((128, 8), np.float32)
    sc[:, 0] = 1e-5
    sc[:, 1] = 1.0
    c["c_scal"] = _f32(sc)
    c["c_ones_col"] = _bf16(np.ones((128, 1)))
    c["c_ones_col_f"] = _f32(np.ones((128, 1)))
    masks = np.zeros((4, 128, TT), np.float32)
    fq = np.arange(TT)[None, :]
    pp = np.arange(128)[:, None]
    for j in range(4):
        masks[j] = (fq >= j * 128 + pp).astype(np.float32)
    c["c_cmask"] = _bf16(masks.transpose(1, 0, 2).reshape(128, 4 * TT))
    inv = 1.0 / (10000.0 ** (np.arange(0, DH, 2) / DH))
    fr = np.arange(T)[:, None] * inv[None, :]
    emb = np.concatenate([fr, fr], axis=-1)
    cos = np.cos(emb).T
    sin = np.sin(emb).T
    c["c_ropecos"] = _bf16(np.concatenate([cos, cos], 0))
    c["c_ropesin"] = _bf16(np.concatenate([sin, sin], 0))
    inv0 = 1.0 / (10000.0 ** (np.arange(DH) / DH))
    fr0 = np.arange(T)[:, None] * inv0[None, :]
    c["c_poscos_b"] = _bf16(np.concatenate([np.cos(fr0).T, np.cos(fr0).T], 0))
    c["c_possin_b"] = _bf16(np.concatenate([np.sin(fr0).T, np.sin(fr0).T], 0))
    P = np.zeros((128, 128), np.float32)
    for hl in range(2):
        for dd in range(DH):
            r = hl * 64 + dd
            if dd < 32:
                P[hl * 64 + dd + 32, r] = -1.0
            else:
                P[hl * 64 + dd - 32, r] = 1.0
    c["c_roperm"] = _bf16(P)
    return c


CONST_SPECS = {
    "c_ident": ([128, 128], BF16),
    "c_ident_f": ([128, 128], F32),
    "c_delta16": ([16, 128], BF16),
    "c_ones_row": ([1, 128], BF16),
    "c_ones_row_f": ([1, 128], F32),
    "c_scal": ([128, 8], F32),
    "c_ones_col": ([128, 1], BF16),
    "c_ones_col_f": ([128, 1], F32),
    "c_cmask": ([128, 4 * TT], BF16),
    "c_ropecos": ([128, T], BF16),
    "c_ropesin": ([128, T], BF16),
    "c_poscos_b": ([128, T], BF16),
    "c_possin_b": ([128, T], BF16),
    "c_roperm": ([128, 128], BF16),
}


class Ctx:
    def __init__(self, nc, p):
        self.nc = nc
        self.p = p
        self.cin = {}
        self.dram = {}
        self.b = {}
        self.uid = 0

    def name(self, base):
        self.uid += 1
        return f"{base}_{self.uid}"


def alloc_shared(ctx):
    nc = ctx.nc
    b = ctx.b
    b["xres"] = nc.alloc_sbuf_tensor("xres", [128, NC * T], F32)[:]
    b["arb"] = nc.alloc_sbuf_tensor("arb", [128, 61440], BF16)[:]
    b["arf"] = nc.alloc_sbuf_tensor("arf", [128, 1024], F32)[:]
    b["wf32"] = nc.alloc_sbuf_tensor("wf32", [128, 2 * WG], F32)[:]
    b["wbf"] = nc.alloc_sbuf_tensor("wbf", [128, 16 * WG], BF16)[:]
    b["rows"] = nc.alloc_sbuf_tensor("rows", [1, 2048], F32)[:]
    b["lrow"] = b["rows"][0:1, 0:1024]
    b["vt"] = b["arb"][:, 57632:57632 + NKB * 130]
    b["ebuf"] = b["arb"][:, 45088:45088 + 4 * TT]
    b["usb"] = b["arf"][:, 512:1024]
    for i in range(7):
        b[f"ps{i}"] = nc.alloc_psum_tensor(f"ps{i}", [128, 512], F32)[:]
    b["pst"] = nc.alloc_psum_tensor("pst", [128, 1024], BF16)[:]
    return b


def load_consts(ctx, names):
    nc, p = ctx.nc, ctx.p
    toks = []
    for n in names:
        shape, dt = CONST_SPECS[n]
        t = nc.alloc_sbuf_tensor(n + "_s", shape, dt)[:]
        ctx.cin[n] = t
        toks.append(p.dma("sync", t, ctx.dram[n], chan="misc"))
    return toks


def emit_ln(ctx, xn_off, xb_off=None):
    p, b, cin = ctx.p, ctx.b, ctx.cin
    xres = b["xres"]
    if xb_off is None:
        xb_off = xn_off + NC * T
    xn = b["arb"][:, xn_off:xn_off + NC * T]
    xb = b["arb"][:, xb_off:xb_off + NC * T]
    stat = b["ps0"]
    mtile = b["ps1"]
    rtile = b["ps2"]
    rows = b["rows"]
    tsc = b["arf"][:, 0:TT]
    onescol = cin["c_ones_col"]
    onesrow = cin["c_ones_row_f"]

    last = None
    prev_bcast_reads = ()
    prev_ev = ()
    for tt in range(NTT):
        tcopy = [p.op("vector", lambda v, c=c, tt=tt: v.tensor_copy(
            xb[:, c * T + tt * TT:c * T + tt * TT + TT],
            xres[:, c * T + tt * TT:c * T + tt * TT + TT]))
            for c in range(NC)]
        mmt = None
        for c in range(NC):
            sl = slice(c * T + tt * TT, c * T + tt * TT + TT)

            def mm(t, sl=sl, c=c):
                return t.matmul(stat[0:1, 0:TT], onescol[:, 0:1], xb[:, sl],
                                start=(c == 0), stop=(c == NC - 1))
            w = [tcopy[c]]
            if c == 0:
                w += list(prev_ev)
            mmt = p.op("tensor", mm, waits=tuple(w), track=(c == NC - 1))
        ev0 = p.op("scalar", lambda s: s.copy(rows[0:1, 0:TT],
                                              stat[0:1, 0:TT]), waits=(mmt,))
        tsq = [p.op("scalar", lambda s, c=c, tt=tt: s.activation(
            xb[:, c * T + tt * TT:c * T + tt * TT + TT],
            xb[:, c * T + tt * TT:c * T + tt * TT + TT], AF.Square),
            waits=(ev0,) if c == 0 else ())
            for c in range(NC)]
        mmt = None
        for c in range(NC):
            sl = slice(c * T + tt * TT, c * T + tt * TT + TT)

            def mm(t, sl=sl, c=c):
                return t.matmul(stat[0:1, 0:TT], onescol[:, 0:1], xb[:, sl],
                                start=(c == 0), stop=(c == NC - 1))
            w = [tsq[c]]
            if c == 0:
                w.append(ev0)
            mmt = p.op("tensor", mm, waits=tuple(w), track=(c == NC - 1))
        ev1 = p.op("scalar", lambda s: s.copy(rows[0:1, TT:2 * TT],
                                              stat[0:1, 0:TT]), waits=(mmt,))
        prev_ev = (ev1,)
        t_mu = p.op("scalar", lambda s: s.activation(
            rows[0:1, 2 * TT:3 * TT], rows[0:1, 0:TT], AF.Copy,
            scale=1.0 / D), waits=(ev1,))
        t_ms = p.op("scalar", lambda s: s.activation(
            rows[0:1, 3 * TT:4 * TT], rows[0:1, 2 * TT:3 * TT], AF.Square),
            waits=(t_mu,))
        t_var = p.op("vector", lambda v: v.scalar_tensor_tensor(
            rows[0:1, TT:2 * TT], rows[0:1, TT:2 * TT], 1.0 / D,
            rows[0:1, 3 * TT:4 * TT], ALU.mult, ALU.subtract), waits=(t_ms,))
        t_sd = p.op("scalar", lambda s: s.activation(
            rows[0:1, TT:2 * TT], rows[0:1, TT:2 * TT], AF.Sqrt,
            bias=cin["c_scal"][0:1, 0:1]), waits=(t_var,))
        t_rs = p.op("vector", lambda v: v.reciprocal(
            rows[0:1, TT:2 * TT], rows[0:1, TT:2 * TT]), waits=(t_sd,))
        tb1 = p.op("tensor", lambda t: t.matmul(
            mtile[:, 0:TT], onesrow[0:1, :], rows[0:1, 2 * TT:3 * TT],
            start=True, stop=True), waits=(t_rs,) + prev_bcast_reads)
        tb2 = p.op("tensor", lambda t: t.matmul(
            rtile[:, 0:TT], onesrow[0:1, :], rows[0:1, TT:2 * TT],
            start=True, stop=True))
        reads = []
        for c in range(NC):
            xsl = slice(c * T + tt * TT, c * T + tt * TT + TT)
            p.op("vector", lambda v, xsl=xsl: v.tensor_tensor(
                tsc, xres[:, xsl], mtile[:, 0:TT], ALU.subtract),
                waits=(tb1, tb2) if c == 0 else ())
            last = p.op("vector", lambda v, xsl=xsl: v.tensor_tensor(
                xn[:, xsl], tsc, rtile[:, 0:TT], ALU.mult))
            reads.append(last)
        prev_bcast_reads = (last,)
    return xn, last


class WStream:

    def __init__(self, ctx):
        self.ctx = ctx
        self.wslot_users = [(), ()]
        self.mm_last = [(), ()]
        self.par = 0
        self.ce_i = 0
        self.wcnt = 0
        self.bank_free = {}


def emit_linear(ctx, ws, cols_src, w_dram, n_in, n_out, out_cb,
                first_waits=(), col_off=0, bank_pair=("ps3", "ps4"),
                conv_engines=("vector", "scalar", "gpsimd"), G=None, nbuf=2,
                bf16_direct=False, tt_list=None):
    p, b = ctx.p, ctx.b
    nc_in = n_in // 128
    if G is None:
        G = WG
    assert nbuf * nc_in * G <= 16 * WG, "wbf overflow"
    if tt_list is None:
        tt_list = tuple(range(NTT))
    if nbuf == 1:
        ws.mm_last[0] = tuple(ws.mm_last[0]) + tuple(ws.mm_last[1])
    wf32, wbf = b["wf32"], b["wbf"]
    banks = (b[bank_pair[0]], b[bank_pair[1]])
    bank_free = ws.bank_free.setdefault(bank_pair, [(), ()])
    ngroups = (n_out + G - 1) // G
    for g in range(ngroups):
        if nbuf == 2:
            par = ws.par
            ws.par ^= 1
        else:
            par = 0
        cols = min(G, n_out - g * G)
        conv_toks = {}
        for dc in range(nc_in):
            wsl = w_dram[dc * 128:(dc + 1) * 128,
                         col_off + g * G:col_off + g * G + cols]
            dstb = wbf[:, (par * nc_in + dc) * G:(par * nc_in + dc) * G + cols]
            if bf16_direct:
                dtok = p.dma("sync", dstb, wsl,
                             waits=tuple(ws.mm_last[par]) if dc == 0 else (),
                             chan=f"wd{par}_{dc}")
                conv_toks[dc] = dtok
                continue
            slot = ws.wcnt % 2
            ws.wcnt += 1
            dst32 = wf32[:, slot * G:slot * G + cols]
            dtok = p.dma("sync", dst32, wsl, waits=ws.wslot_users[slot],
                         chan=f"wslot{slot}")
            eng = conv_engines[ws.ce_i % len(conv_engines)]
            ws.ce_i += 1

            def cv(e, dstb=dstb, dst32=dst32):
                if hasattr(e, "tensor_copy"):
                    return e.tensor_copy(dstb, dst32)
                return e.copy(dstb, dst32)
            ctok = p.op(eng, cv, waits=(dtok,) + tuple(ws.mm_last[par]))
            ws.wslot_users[slot] = (ctok,)
            conv_toks[dc] = ctok
        gsz = (cols + 127) // 128
        last_mm = None
        for oci in range(gsz):
            oc = g * (G // 128) + oci
            ocols = min(128, cols - oci * 128)
            bk = oc % 2
            pbank = banks[bk]
            for tt in tt_list:
                mmt = None
                for dc in range(nc_in):
                    lhsT = wbf[:, (par * nc_in + dc) * G + oci * 128:
                               (par * nc_in + dc) * G + oci * 128 + ocols]

                    def mm(t, lhsT=lhsT, dc=dc, tt=tt, ocols=ocols, pbank=pbank):
                        return t.matmul(pbank[0:ocols, 0:TT], lhsT,
                                        cols_src(dc, tt), start=(dc == 0),
                                        stop=(dc == nc_in - 1))
                    w = [conv_toks[dc]]
                    if dc == 0:
                        w += list(bank_free[bk]) + list(first_waits)
                        first_waits = ()
                    mmt = p.op("tensor", mm, waits=tuple(w),
                               track=(dc == nc_in - 1))
                etok = out_cb(oc, tt, pbank, mmt)
                bank_free[bk] = (etok,)
                last_mm = mmt
        ws.mm_last[par] = (last_mm,)
    return bank_free


def emit_vtok(ctx, v_src_pair, first_waits=()):
    p, b, cin = ctx.p, ctx.b, ctx.cin
    vt = b["vt"]
    pst = b["pst"]
    ident = cin["c_ident"]
    tok = None
    prev_ev = tuple(first_waits) + tuple(ctx.b.get("vt_last", ()))
    for kb in range(NKB):
        t1 = p.op("tensor", lambda t, kb=kb: t.transpose(
            pst[:, 0:128], v_src_pair[:, kb * 128:(kb + 1) * 128],
            ident[:, :]), waits=prev_ev)
        dst = vt[:, kb * 130:kb * 130 + 130].rearrange(
            "p (h c) -> p h c", h=2)[:, :, 0:64]
        srcv = pst[:, 0:128].rearrange("p (h c) -> p h c", h=2)
        e1 = p.op("scalar", lambda s, dst=dst, srcv=srcv: s.copy(dst, srcv),
                  waits=(t1,))
        prev_ev = (e1,)
        tok = e1
    ctx.b["vt_last"] = (tok,)
    return tok


def emit_attn_pair(ctx, pair, q_lhs, k_pair, y_dst, causal, scale,
                   extra_mask=None, first_waits=(), two_part=False):
    p, b, cin = ctx.p, ctx.b, ctx.cin
    sb = (b["ps0"], b["ps1"], b["ps2"], b["ps3"])
    ub = (b["ps4"], b["ps5"])
    scb = b["ps6"]
    vt = b["vt"]
    ebuf = b["arf"]
    eb = b["ebuf"]
    usb = b["usb"]
    cmask = cin.get("c_cmask")
    onesrow = cin["c_ones_row_f"]
    rrow = b["rows"]

    y_out_toks = []
    ufree = [(), ()]
    sfree = {(h, par): tuple(first_waits) for h in (0, 1) for par in (0, 1)}
    for qt in range(NTT):
        kb_list = list(range(NKB))
        if causal:
            kb_list = [kb for kb in kb_list if kb * 128 < (qt + 1) * TT]
        efree = {}
        utoks = [None, None]
        for ik, kb in enumerate(kb_list):
            parity = ik % 2
            for h in (0, 1):
                sbank = sb[h * 2 + parity]
                if two_part:
                    p.op("tensor", lambda t, h=h, kb=kb, qt=qt, sbank=sbank:
                         t.matmul(sbank[0:128, 0:TT], k_pair(h, kb, 0),
                                  q_lhs(h, qt, 0), start=True, stop=False),
                         waits=sfree[(h, parity)], track=False)
                    stok = p.op("tensor",
                                lambda t, h=h, kb=kb, qt=qt, sbank=sbank:
                                t.matmul(sbank[0:128, 0:TT], k_pair(h, kb, 1),
                                         q_lhs(h, qt, 1), start=False,
                                         stop=True))
                else:
                    stok = p.op("tensor",
                                lambda t, h=h, kb=kb, qt=qt, sbank=sbank:
                                t.matmul(sbank[0:128, 0:TT], k_pair(h, kb),
                                         q_lhs(h, qt), start=True, stop=True),
                                waits=sfree[(h, parity)])
                esl = eb[:, (h * 2 + parity) * TT:(h * 2 + parity) * TT + TT]
                etok = p.op("scalar", lambda s, esl=esl, sbank=sbank:
                            s.activation(esl, sbank[0:128, 0:TT], AF.Exp,
                                         scale=scale),
                            waits=(stok,) + efree.get((h, parity), ()))
                sfree[(h, parity)] = (etok,)
                mtok = etok
                if extra_mask is not None:
                    mtok = extra_mask(h, kb, qt, esl, etok)
                if causal and kb * 128 >= qt * TT:
                    j = kb - qt * 4
                    mtok = p.op("vector", lambda v, esl=esl, j=j: v.tensor_tensor(
                        esl, esl, cmask[:, j * TT:(j + 1) * TT], ALU.mult),
                        waits=(mtok,))
                first = (ik == 0)
                last = (ik == len(kb_list) - 1)
                vsl = vt[:, kb * 130 + h * 65:kb * 130 + h * 65 + 65]
                ut = p.op("tensor", lambda t, h=h, vsl=vsl, esl=esl, first=first,
                          last=last: t.matmul(ub[h][0:65, 0:TT], vsl, esl,
                                              start=first, stop=last),
                          waits=(mtok,) + (ufree[h] if first else ()))
                efree[(h, parity)] = (ut,)
                if last:
                    utoks[h] = ut
        for h in (0, 1):
            usl = usb
            ev = p.op("scalar", lambda s, usl=usl, h=h: s.copy(
                usl[0:65, :], ub[h][0:65, 0:TT]),
                waits=(utoks[h],) + tuple(y_out_toks[-1:]))
            lrow = b["lrow"][0:1, h * TT:h * TT + TT]
            l1 = p.op("scalar", lambda s, usl=usl, lrow=lrow: s.activation(
                lrow, usl[64:65, :], AF.Ln), waits=(ev,))
            l2 = p.op("scalar", lambda s, lrow=lrow: s.activation(
                lrow, lrow, AF.Exp, scale=-1.0), waits=(l1,))
            bt = p.op("tensor", lambda t, lrow=lrow: t.matmul(
                scb[0:64, 0:TT], onesrow[0:1, 0:64], lrow, start=True,
                stop=True), waits=(l2,) + tuple(y_out_toks[-1:]))
            yt = p.op("vector", lambda v, usl=usl, h=h, qt=qt: v.tensor_tensor(
                y_dst(h, qt), usl[0:64, :], scb[0:64, 0:TT], ALU.mult),
                waits=(bt,))
            ufree[h] = (ev,)
            y_out_toks.append(yt)
    return y_out_toks


def emit_mha(ctx, kind, upto="full"):
    p, b, cin, dram = ctx.p, ctx.b, ctx.cin, ctx.dram
    pfx = {"l0a": "w0a", "l1a": "w1a"}[kind]
    rope = (kind == "l1a")
    xn, t_xn = emit_ln(ctx, 0, xb_off=26624)
    if upto == "ln":
        return t_xn
    bqkv = ctx.nc.alloc_sbuf_tensor(ctx.name("bqkv"), [128, 32], F32)[:]
    bc_chan = ctx.name("bias")
    p.dma("sync", bqkv[:, 0:8], dram[pfx + "_bq"], chan=bc_chan)
    p.dma("sync", bqkv[:, 8:16], dram[pfx + "_bk"], chan=bc_chan)
    p.dma("sync", bqkv[:, 16:24], dram[pfx + "_bv"], chan=bc_chan)
    p.dma("sync", bqkv[:, 24:32], dram[pfx + "_bo"], chan=bc_chan)
    tb = p.dfence(bc_chan)

    QOF, KOF, VOF, TRIG, YOF = 16384, 18432, 20480, 22528, 26624
    qP = b["arb"][:, QOF:QOF + T]
    kP = b["arb"][:, KOF:KOF + T]
    vP = b["arb"][:, VOF:VOF + T]
    YB = b["arb"][:, YOF + 0:YOF + NC * T]
    ws = WStream(ctx)
    t_ones = p.op("vector", lambda v: v.memset(
        b["vt"][:].rearrange("p (k c) -> p k c", c=130)[:, :, 64:65], 1.0))
    t_ones2 = p.op("vector", lambda v: v.memset(
        b["vt"][:].rearrange("p (k c) -> p k c", c=130)[:, :, 129:130], 1.0))

    cos = b["arb"][:, TRIG:TRIG + T]
    sin = b["arb"][:, TRIG + T:TRIG + 2 * T]
    roperm = cin.get("c_roperm")
    prev_attn = (t_xn, tb)
    if rope:
        p.dma("sync", cos, dram["c_ropecos"], chan="trig")
        p.dma("sync", sin, dram["c_ropesin"], chan="trig")
        prev_attn = prev_attn + (p.dfence("trig"),)
    for pair in range(NC):
        toks = {"q": [], "k": [], "v": []}

        def mk_cb(dst, which, pair=pair):
            def cb(oc, tt, ps, mmtok):
                bc = bqkv[:, {"q": 0, "k": 8, "v": 16}[which] + pair:
                          {"q": 0, "k": 8, "v": 16}[which] + pair + 1]
                tok = p.op("scalar", lambda s, tt=tt, ps=ps: s.activation(
                    dst[:, tt * TT:tt * TT + TT], ps[0:128, 0:TT], AF.Identity,
                    bias=bc), waits=(mmtok,))
                toks[which].append(tok)
                return tok
            return cb
        for which, dst, wname in (("q", qP, "_wq"), ("k", kP, "_wk"),
                                  ("v", vP, "_wv")):
            emit_linear(ctx, ws, lambda dc, tt: xn[:, dc * T + tt * TT:
                                                   dc * T + tt * TT + TT],
                        dram[pfx + wname], D, 128, mk_cb(dst, which),
                        first_waits=prev_attn, col_off=pair * 128,
                        bank_pair=("ps3", "ps4"),
                        conv_engines=("vector", "scalar"))
            prev_attn = ()
        if rope:
            rope_prev = ()
            for which, buf in (("q", qP), ("k", kP)):
                for tt in range(NTT):
                    sl = slice(tt * TT, tt * TT + TT)
                    wt = (tuple(toks[which]) if tt == 0 else ()) + rope_prev
                    m1 = p.op("tensor", lambda t, sl=sl, buf=buf: t.matmul(
                        b["ps5"][:, 0:TT], roperm, buf[:, sl], start=True,
                        stop=True), waits=wt)
                    s1 = p.op("vector", lambda v, sl=sl, buf=buf: v.tensor_tensor(
                        b["arf"][:, 0:TT], b["ps5"][:, 0:TT], sin[:, sl],
                        ALU.mult), waits=(m1,))
                    rope_prev = (s1,)
                    s2 = p.op("vector", lambda v, sl=sl, buf=buf: v.tensor_tensor(
                        b["arf"][:, TT:2 * TT], buf[:, sl], cos[:, sl],
                        ALU.mult))
                    s3 = p.op("vector", lambda v, sl=sl, buf=buf: v.tensor_tensor(
                        buf[:, sl], b["arf"][:, 0:TT], b["arf"][:, TT:2 * TT],
                        ALU.add))
                    toks[which].append(s3)
        if upto == "qkv0":
            return toks["v"][-1]
        t_vt = emit_vtok(ctx, vP, first_waits=tuple(toks["v"]))
        if upto == "vtok0":
            return t_vt
        pre = tuple(toks["q"]) + tuple(toks["k"]) + (t_vt, t_ones, t_ones2)
        yt = emit_attn_pair(
            ctx, pair,
            q_lhs=lambda h, qt: qP[h * 64:h * 64 + 64, qt * TT:qt * TT + TT],
            k_pair=lambda h, kb: kP[h * 64:h * 64 + 64, kb * 128:kb * 128 + 128],
            y_dst=lambda h, qt, pair=pair: YB[
                h * 64:h * 64 + 64,
                pair * T + qt * TT:pair * T + qt * TT + TT],
            causal=False, scale=0.125, first_waits=pre)
        prev_attn = tuple(yt[-2:])
        if upto == "pair0":
            return yt[-1]
    if upto == "pairs":
        return prev_attn[-1]
    def ocb(oc, tt, ps, mmtok):
        bc = bqkv[:, 24 + oc:24 + oc + 1]
        sl = slice(oc * T + tt * TT, oc * T + tt * TT + TT)
        return p.op("vector", lambda v, sl=sl, ps=ps, bc=bc:
                    v.scalar_tensor_tensor(b["xres"][:, sl], ps[0:128, 0:TT],
                                           bc, b["xres"][:, sl], ALU.add,
                                           ALU.add), waits=(mmtok,))
    emit_linear(ctx, ws, lambda dc, tt: YB[:, dc * T + tt * TT:
                                           dc * T + tt * TT + TT],
                dram[pfx + "_wo"], D, D, ocb, first_waits=prev_attn,
                bank_pair=("ps3", "ps4"), conv_engines=("scalar", "gpsimd"))
    return p.fence("vector")




AOF = 16384


def emit_moe_gate(ctx, xn, t_xn):
    p, b, cin, dram = ctx.p, ctx.b, ctx.cin, ctx.dram
    arf = b["arf"]
    gl = arf[:, 512:640]
    CW = arf[:, 640:768]
    m1 = arf[:, 768:769]
    m2 = arf[:, 769:770]
    d12 = arf[:, 770:771]
    w1 = arf[:, 771:772]
    w2 = arf[:, 772:773]
    eq1 = arf[:, 776:784]
    tmp = arf[:, 784:792]
    eq2 = arf[:, 792:800]
    ps5 = b["ps5"]
    gw = b["wbf"][:, 0:64]
    gwf = b["wf32"][:, 0:64]
    ch = ctx.name("gw")
    dt0 = p.dma("sync", gwf.rearrange("p (c e) -> p c e", e=8),
                dram["moe_gate_w"].rearrange("(c p) e -> p c e", p=128),
                chan=ch)
    tcv = p.op("scalar", lambda s: s.copy(gw, gwf), waits=(dt0,))
    cw_tok = None
    prev_ev = ()
    for tb in range(NKB):
        mmt = None
        for dc in range(NC):
            def mm(t, tb=tb, dc=dc):
                return t.matmul(ps5[0:128, 0:8],
                                xn[:, dc * T + tb * 128:dc * T + (tb + 1) * 128],
                                gw[:, dc * 8:dc * 8 + 8],
                                start=(dc == 0), stop=(dc == NC - 1))
            w = [tcv, t_xn] if dc == 0 else []
            if dc == 0:
                w += list(prev_ev)
            mmt = p.op("tensor", mm, waits=tuple(w), track=(dc == NC - 1))
        ev = p.op("scalar", lambda s, tb=tb: s.copy(gl[:, tb * 8:(tb + 1) * 8],
                                                    ps5[0:128, 0:8]),
                  waits=(mmt,))
        prev_ev = (ev,)
        glt = gl[:, tb * 8:(tb + 1) * 8]
        t1 = p.op("vector", lambda v, glt=glt: v.tensor_reduce(
            m1, glt, AX.X, ALU.max), waits=(ev,))
        t2 = p.op("vector", lambda v, glt=glt: v.tensor_scalar(
            eq1, glt, m1[:, 0:1], None, ALU.is_ge))
        t3 = p.op("vector", lambda v, glt=glt: v.scalar_tensor_tensor(
            tmp, eq1, -1e30, glt, ALU.mult, ALU.add))
        t4 = p.op("vector", lambda v: v.tensor_reduce(m2, tmp, AX.X, ALU.max))
        t5 = p.op("vector", lambda v, glt=glt: v.tensor_scalar(
            eq2, tmp, m2[:, 0:1], None, ALU.is_ge))
        t6 = p.op("vector", lambda v: v.tensor_tensor(d12, m1, m2, ALU.subtract))
        t7 = p.op("scalar", lambda s: s.activation(w1, d12, AF.Sigmoid),
                  waits=(t6,))
        t8 = p.op("scalar", lambda s: s.activation(w2, d12, AF.Sigmoid,
                                                   scale=-1.0))
        t9 = p.op("vector", lambda v: v.tensor_scalar(
            eq1, eq1, w1[:, 0:1], None, ALU.mult), waits=(t8,))
        cw_tok = p.op("vector", lambda v, tb=tb: v.scalar_tensor_tensor(
            CW[:, tb * 8:(tb + 1) * 8], eq2, w2[:, 0:1], eq1, ALU.mult,
            ALU.add))
    return CW, cw_tok


def emit_cw_row(ctx, CW, e, cw_tok, prev=()):
    p, b, cin = ctx.p, ctx.b, ctx.cin
    rows = b["rows"]
    ps6 = b["ps6"]
    identf = cin["c_ident_f"]
    toks = []
    prev_t = prev
    for tb in range(NKB):
        t1 = p.op("tensor", lambda t, tb=tb: t.transpose(
            ps6[0:1, 0:128], CW[:, tb * 8 + e:tb * 8 + e + 1], identf),
            waits=(cw_tok,) + tuple(prev_t))
        e1 = p.op("scalar", lambda s, tb=tb: s.copy(
            rows[0:1, tb * 128:(tb + 1) * 128], ps6[0:1, 0:128]), waits=(t1,))
        prev_t = (e1,)
        toks.append(e1)
    return toks[-1]


def emit_moe_mlp(ctx, xn, wgu_dram, wd_dram, ws, first_waits, cw_row_tok,
                 weighted):
    p, b, cin = ctx.p, ctx.b, ctx.cin
    arb, arf = b["arb"], b["arf"]
    wbf = b["wbf"]
    wf32 = b["wf32"]
    sg = arf[:, 0:512]
    cwb = arf[:, 0:512]
    II = 22
    bankA, bankB = b["ps0"], b["ps1"]
    a_toks = []
    freeA = tuple(first_waits)
    freeB = ()
    sg_free = ()
    for ocp in range(II):
        par = ocp % 2
        conv = {}
        for half, coff in ((0, ocp * 128), (1, (II + ocp) * 128)):
            for dc in range(NC):
                slot = ws.wcnt % 2
                ws.wcnt += 1
                dst32 = wf32[:, slot * WG:slot * WG + 128]
                dtok = p.dma("sync", dst32,
                             wgu_dram[dc * 128:(dc + 1) * 128, coff:coff + 128],
                             waits=ws.wslot_users[slot], chan=f"wslot{slot}")
                eng = ("scalar", "gpsimd")[ws.ce_i % 2]
                ws.ce_i += 1
                dstb = wbf[:, (par * 16 + half * 8 + dc) * 128:
                           (par * 16 + half * 8 + dc) * 128 + 128]

                def cv(en, dstb=dstb, dst32=dst32):
                    if hasattr(en, "tensor_copy"):
                        return en.tensor_copy(dstb, dst32)
                    return en.copy(dstb, dst32)
                ctok = p.op(eng, cv, waits=(dtok,) + tuple(ws.mm_last[par]))
                ws.wslot_users[slot] = (ctok,)
                conv[(half, dc)] = ctok
        last_mm = None
        for tt in range(NTT):
            mg = None
            for dc in range(NC):
                lhsT = wbf[:, (par * 16 + dc) * 128:(par * 16 + dc) * 128 + 128]

                def mm(t, lhsT=lhsT, dc=dc, tt=tt):
                    return t.matmul(bankA[0:128, 0:TT], lhsT,
                                    xn[:, dc * T + tt * TT:dc * T + tt * TT + TT],
                                    start=(dc == 0), stop=(dc == NC - 1))
                w = [conv[(0, dc)]]
                if dc == 0:
                    w += list(freeA)
                mg = p.op("tensor", mm, waits=tuple(w), track=(dc == NC - 1))
            tsg = p.op("scalar", lambda s: s.activation(
                sg, bankA[0:128, 0:TT], AF.Silu), waits=(mg,) + sg_free)
            freeA = (tsg,)
            mu = None
            for dc in range(NC):
                lhsT = wbf[:, (par * 16 + 8 + dc) * 128:
                           (par * 16 + 8 + dc) * 128 + 128]

                def mm(t, lhsT=lhsT, dc=dc, tt=tt):
                    return t.matmul(bankB[0:128, 0:TT], lhsT,
                                    xn[:, dc * T + tt * TT:dc * T + tt * TT + TT],
                                    start=(dc == 0), stop=(dc == NC - 1))
                w = [conv[(1, dc)]]
                if dc == 0:
                    w += list(freeB)
                mu = p.op("tensor", mm, waits=tuple(w), track=(dc == NC - 1))
            asl = arb[:, AOF + ocp * T + tt * TT:AOF + ocp * T + tt * TT + TT]
            ta = p.op("vector", lambda v, asl=asl: v.tensor_tensor(
                asl, sg, bankB[0:128, 0:TT], ALU.mult), waits=(tsg, mu))
            freeB = (ta,)
            sg_free = (ta,)
            a_toks.append(ta)
            last_mm = mu
        ws.mm_last[par] = (last_mm,)

    rows = b["rows"]
    ps5 = b["ps5"]
    onesrow = cin["c_ones_row_f"]
    cw_state = {"tt": -1, "tok": (), "cwb_users": ()}

    def ocb(oc, tt, ps, mmtok):
        if not weighted:
            return p.op("vector", lambda v, oc=oc, tt=tt, ps=ps: v.tensor_tensor(
                b["xres"][:, oc * T + tt * TT:oc * T + tt * TT + TT],
                b["xres"][:, oc * T + tt * TT:oc * T + tt * TT + TT],
                ps[0:128, 0:TT], ALU.add), waits=(mmtok,))
        bt = p.op("tensor", lambda t, tt=tt: t.matmul(
            ps5[0:128, 0:TT], onesrow[0:1, :],
            rows[0:1, tt * TT:tt * TT + TT], start=True, stop=True),
            waits=(cw_row_tok,) + cw_state["tok"])
        ev = p.op("scalar", lambda s: s.copy(cwb, ps5[0:128, 0:TT]),
                  waits=(bt,) + cw_state["cwb_users"])
        cw_state["tok"] = (ev,)
        t1 = p.op("vector", lambda v, ps=ps: v.tensor_tensor(
            ps[0:128, 0:TT], ps[0:128, 0:TT], cwb, ALU.mult),
            waits=(mmtok, ev))
        t2 = p.op("vector", lambda v, oc=oc, tt=tt, ps=ps: v.tensor_tensor(
            b["xres"][:, oc * T + tt * TT:oc * T + tt * TT + TT],
            b["xres"][:, oc * T + tt * TT:oc * T + tt * TT + TT],
            ps[0:128, 0:TT], ALU.add))
        cw_state["cwb_users"] = (t1,)
        return t2

    emit_linear(ctx, ws, lambda dc, tt: arb[:, AOF + dc * T + tt * TT:
                                            AOF + dc * T + tt * TT + TT],
                wd_dram, 2816, D, ocb, first_waits=tuple(a_toks[-1:]),
                bank_pair=("ps3", "ps4"), conv_engines=("scalar", "gpsimd"),
                G=128, nbuf=1)
    return p.fence("vector")


def emit_moe(ctx, upto="full"):
    p, b, dram = ctx.p, ctx.b, ctx.dram
    xn, t_xn = emit_ln(ctx, 0, xb_off=AOF)
    if upto == "ln":
        return t_xn
    CW, cw_tok = emit_moe_gate(ctx, xn, t_xn)
    if upto == "gate":
        return cw_tok
    ws = WStream(ctx)
    seed = (cw_tok, p.fence("tensor"))
    ws.wslot_users = [seed, seed]
    ws.mm_last = [seed, seed]
    tok = emit_moe_mlp(ctx, xn, dram["moe_base_wgu"], dram["moe_base_wd"], ws,
                       (t_xn,), None, weighted=False)
    if upto == "base":
        return tok
    prev_cw = ()
    for e in range(8):
        crt = emit_cw_row(ctx, CW, e, cw_tok, prev=prev_cw)
        prev_cw = (crt,)
        wgu_e = dram["moe_exp_wgu"].rearrange("(e h) i -> e h i", e=8)[e]
        wd_e = dram["moe_exp_wd"].rearrange("(e i) h -> e i h", e=8)[e]
        tok = emit_moe_mlp(ctx, xn, wgu_e, wd_e, ws, (tok,), crt,
                           weighted=True)
        prev_cw = (tok, p.fence("tensor"))
    return tok


def build_sub_moe(upto="full"):
    nc = bass.Bass("TRN2", target_bir_lowering=False)
    p = Prog(nc)
    ctx = Ctx(nc, p)
    xin = nc.dram_tensor("xin", [D, T], F32, kind="ExternalInput").ap()
    xout = nc.dram_tensor("xout", [D, T], F32, kind="ExternalOutput").ap()
    ctx.dram["moe_gate_w"] = nc.dram_tensor("moe_gate_w", [D, 8], F32,
                                            kind="ExternalInput").ap()
    ctx.dram["moe_base_wgu"] = nc.dram_tensor("moe_base_wgu", [D, 5632], F32,
                                              kind="ExternalInput").ap()
    ctx.dram["moe_base_wd"] = nc.dram_tensor("moe_base_wd", [2816, D], F32,
                                             kind="ExternalInput").ap()
    ctx.dram["moe_exp_wgu"] = nc.dram_tensor("moe_exp_wgu", [8 * D, 5632], F32,
                                             kind="ExternalInput").ap()
    ctx.dram["moe_exp_wd"] = nc.dram_tensor("moe_exp_wd", [8 * 2816, D], F32,
                                            kind="ExternalInput").ap()
    cnames = ["c_ident", "c_ident_f", "c_ones_row", "c_ones_row_f",
              "c_ones_col", "c_scal"]
    for n in cnames:
        shape, dt = CONST_SPECS[n]
        ctx.dram[n] = nc.dram_tensor(n, shape, dt, kind="ExternalInput").ap()
    alloc_shared(ctx)
    load_consts(ctx, cnames)
    xres = ctx.b["xres"]
    tlds = [p.dma("gpsimd", xres[:, c * T:(c + 1) * T],
                  xin[c * 128:(c + 1) * 128, :], chan="xio") for c in range(NC)]
    startw = (p.dfence("xio"), p.dfence("misc"))
    for e in ("vector", "tensor", "scalar", "gpsimd"):
        p.op(e, None, waits=startw)
    tdone = emit_moe(ctx, upto)
    for c in range(NC):
        p.dma("gpsimd", xout[c * 128:(c + 1) * 128, :],
              xres[:, c * T:(c + 1) * T],
              waits=(tdone, p.fence("scalar"), p.fence("tensor"),
                     p.fence("vector")) if c == 0 else (), chan="xio")
    p.op("gpsimd", None, waits=(p.dfence("xio"),))
    return p.build()




def emit_quantize(ctx, w_dram, wq_dram, n_in, n_out, gslot, first_waits=()):
    p, b, cin = ctx.p, ctx.b, ctx.cin
    wf32 = b["wf32"]
    wbf = b["wbf"]
    arf = b["arf"]
    acc = arf[:, 768 + gslot:769 + gslot]
    gcol = arf[:, 772 + gslot:773 + gslot]
    ps6 = b["ps6"]
    onescol = cin["c_ones_col"]
    onesrow = cin["c_ones_row_f"]
    onescolf = cin["c_ones_col_f"]
    ncol = n_out // WG
    ntile = (n_in // 128) * ncol
    ch = ctx.name("qz")
    t_acc = p.op("vector", lambda v: v.memset(acc, 0.0), waits=first_waits)
    slot_free = [tuple(first_waits), tuple(first_waits)]
    red = arf[:, 774 + gslot:775 + gslot]
    for i in range(ntile):
        a, c0 = i // ncol, (i % ncol) * WG
        slot = i % 2
        dst = wf32[:, slot * WG:slot * WG + WG]
        dtok = p.dma("sync", dst,
                     w_dram[a * 128:(a + 1) * 128, c0:c0 + WG],
                     waits=slot_free[slot], chan=ch + str(slot))
        t1 = p.op("vector", lambda v, dst=dst: v.tensor_reduce(
            red, dst, AX.X, ALU.add, apply_absolute_value=True),
            waits=(dtok, t_acc))
        t_acc = p.op("vector", lambda v: v.tensor_tensor(acc, acc, red,
                                                         ALU.add))
        slot_free[slot] = (t_acc,)
    tmm = p.op("tensor", lambda t: t.matmul(ps6[0:1, 0:1], onescolf[:, 0:1],
                                            acc, start=True, stop=True),
               waits=(t_acc,))
    gl = b["rows"][0:1, 4:5]
    t2 = p.op("scalar", lambda s: s.activation(
        gl, ps6[0:1, 0:1], AF.Identity, scale=1.0 / (n_in * n_out),
        bias=cin["c_scal"][0:1, 0:1]), waits=(tmm,))
    t3 = p.op("vector", lambda v: v.reciprocal(gl, gl), waits=(t2,))
    t4 = p.op("tensor", lambda t: t.matmul(ps6[0:128, 1:2], onesrow[0:1, :],
                                           gl, start=True, stop=True),
               waits=(t3,))
    t5 = p.op("scalar", lambda s: s.copy(gcol, ps6[0:128, 1:2]), waits=(t4,))
    sgn = arf[:, 0:WG]
    ab = arf[:, 512:512 + WG]
    prev_q = ()
    slot_free = [(t5,), (t5,)]
    wq_free = [tuple(first_waits), tuple(first_waits)]
    for i in range(ntile):
        a, c0 = i // ncol, (i % ncol) * WG
        slot = i % 2
        dst = wf32[:, slot * WG:slot * WG + WG]
        dtok = p.dma("sync", dst,
                     w_dram[a * 128:(a + 1) * 128, c0:c0 + WG],
                     waits=slot_free[slot], chan=ch + str(slot))
        t6 = p.op("vector", lambda v, dst=dst: v.tensor_scalar(
            dst, dst, gcol, None, ALU.mult), waits=(dtok,) + prev_q)
        t7 = p.op("vector", lambda v, dst=dst: v.tensor_scalar(
            dst, dst, 1.0, -1.0, ALU.min, ALU.max))
        t8 = p.op("scalar", lambda s, dst=dst: s.activation(sgn, dst, AF.Sign),
                  waits=(t7,))
        t9 = p.op("scalar", lambda s, dst=dst: s.activation(ab, dst,
                                                            AF.Square),
                  waits=(t7,))
        t10 = p.op("vector", lambda v: v.tensor_scalar(
            ab, ab, 0.25, None, ALU.is_ge), waits=(t9,))
        qsl = wbf[:, slot * WG:slot * WG + WG]
        t11 = p.op("vector", lambda v, qsl=qsl: v.tensor_tensor(
            qsl, ab, sgn, ALU.mult), waits=(t8,) + wq_free[slot])
        prev_q = (t11,)
        stok = p.dma("gpsimd", wq_dram[a * 128:(a + 1) * 128, c0:c0 + WG],
                     qsl, waits=(t11,), chan=ch + "s" + str(slot))
        slot_free[slot] = (t11,)
        wq_free[slot] = (stok,)
    return p.dfence(ch + "s0"), p.dfence(ch + "s1")


def emit_bitffn(ctx, upto="full"):
    p, b, cin, dram = ctx.p, ctx.b, ctx.cin, ctx.dram
    arb = b["arb"]
    xn, t_xn = emit_ln(ctx, 0, xb_off=AOF)
    bcol = ctx.nc.alloc_sbuf_tensor(ctx.name("bb"), [128, 40], F32)[:]
    bch = ctx.name("bias")
    p.dma("sync", bcol[:, 0:32], dram["bit_b1"], chan=bch)
    p.dma("sync", bcol[:, 32:40], dram["bit_b2"], chan=bch)
    tb = p.dfence(bch)
    qt1, qt2 = ctx.b.get("bit_quant_toks", ((), ()))
    ws = WStream(ctx)
    seed = tuple(qt1) + tuple(qt2)
    ws.mm_last = [seed, seed]
    ws.wslot_users = [seed, seed]
    HOF = AOF
    TH = T // 2
    tok = (t_xn, tb)
    for th in range(2):
        hbuf = arb[:, HOF:HOF + 32 * TH]

        def hcb(oc, tt, ps, mmtok, th=th, hbuf=hbuf):
            sl = slice(oc * TH + (tt - th * 2) * TT, oc * TH + (tt - th * 2) * TT + TT)
            return p.op("scalar", lambda s, sl=sl, ps=ps, oc=oc: s.activation(
                hbuf[:, sl], ps[0:128, 0:TT], AF.Relu,
                bias=bcol[:, oc:oc + 1]), waits=(mmtok,))

        emit_linear(ctx, ws, lambda dc, tt, th=th: xn[
            :, dc * T + th * TH + (tt - th * 2) * TT:
            dc * T + th * TH + (tt - th * 2) * TT + TT],
            ctx.dram["bit_w1q"], D, 4096, hcb,
            first_waits=tok + tuple(qt1), bank_pair=("ps3", "ps4"),
            conv_engines=(), bf16_direct=True, G=WG, nbuf=2,
            tt_list=(th * 2, th * 2 + 1))

        def ocb(oc, tt, ps, mmtok, th=th):
            sl = slice(oc * T + th * TH + (tt - th * 2) * TT,
                       oc * T + th * TH + (tt - th * 2) * TT + TT)
            return p.op("vector", lambda v, sl=sl, ps=ps, oc=oc:
                        v.scalar_tensor_tensor(
                            b["xres"][:, sl], ps[0:128, 0:TT],
                            bcol[:, 32 + oc:33 + oc], b["xres"][:, sl],
                            ALU.add, ALU.add), waits=(mmtok,))

        emit_linear(ctx, ws, lambda dc, tt, th=th, hbuf=hbuf: hbuf[
            :, dc * TH + (tt - th * 2) * TT:dc * TH + (tt - th * 2) * TT + TT],
            ctx.dram["bit_w2q"], 4096, D, ocb,
            first_waits=tuple(qt2) + (p.fence("scalar"),),
            bank_pair=("ps3", "ps4"),
            conv_engines=(), bf16_direct=True, G=128, nbuf=1,
            tt_list=(th * 2, th * 2 + 1))
        tok = (p.fence("vector"),)
    return p.fence("vector")


def build_sub_bitffn(upto="full"):
    nc = bass.Bass("TRN2", target_bir_lowering=False)
    p = Prog(nc)
    ctx = Ctx(nc, p)
    xin = nc.dram_tensor("xin", [D, T], F32, kind="ExternalInput").ap()
    xout = nc.dram_tensor("xout", [D, T], F32, kind="ExternalOutput").ap()
    ctx.dram["bit_w1"] = nc.dram_tensor("bit_w1", [D, 4096], F32,
                                        kind="ExternalInput").ap()
    ctx.dram["bit_w2"] = nc.dram_tensor("bit_w2", [4096, D], F32,
                                        kind="ExternalInput").ap()
    ctx.dram["bit_b1"] = nc.dram_tensor("bit_b1", [128, 32], F32,
                                        kind="ExternalInput").ap()
    ctx.dram["bit_b2"] = nc.dram_tensor("bit_b2", [128, 8], F32,
                                        kind="ExternalInput").ap()
    ctx.dram["bit_w1q"] = nc.dram_tensor("bit_w1q", [D, 4096], BF16,
                                         kind="Internal").ap()
    ctx.dram["bit_w2q"] = nc.dram_tensor("bit_w2q", [4096, D], BF16,
                                         kind="Internal").ap()
    cnames = ["c_ident", "c_ones_row", "c_ones_row_f", "c_ones_col",
              "c_ones_col_f", "c_scal"]
    for n in cnames:
        shape, dt = CONST_SPECS[n]
        ctx.dram[n] = nc.dram_tensor(n, shape, dt, kind="ExternalInput").ap()
    alloc_shared(ctx)
    load_consts(ctx, cnames)
    xres = ctx.b["xres"]
    for c in range(NC):
        p.dma("gpsimd", xres[:, c * T:(c + 1) * T],
              xin[c * 128:(c + 1) * 128, :], chan="xio")
    startw = (p.dfence("xio"), p.dfence("misc"))
    for e in ("vector", "tensor", "scalar", "gpsimd"):
        p.op(e, None, waits=startw)
    q1 = emit_quantize(ctx, ctx.dram["bit_w1"], ctx.dram["bit_w1q"], D, 4096, 0)
    q2 = emit_quantize(ctx, ctx.dram["bit_w2"], ctx.dram["bit_w2q"], 4096, D, 1,
                       first_waits=(p.fence("vector"), p.fence("scalar")) + q1)
    ctx.b["bit_quant_toks"] = (q1, q2)
    tdone = emit_bitffn(ctx, upto)
    for c in range(NC):
        p.dma("gpsimd", xout[c * 128:(c + 1) * 128, :],
              xres[:, c * T:(c + 1) * T],
              waits=(tdone, p.fence("scalar"), p.fence("tensor"),
                     p.fence("vector")) if c == 0 else (), chan="xio")
    p.op("gpsimd", None, waits=(p.dfence("xio"),))
    return p.build()




QROF = 47136
QIOF = 49184
KROF = 51232
KIOF = 53280
CMOF = 55328


def emit_cosformer(ctx, upto="full"):
    p, b, cin, dram = ctx.p, ctx.b, ctx.cin, ctx.dram
    arb, arf = b["arb"], b["arf"]
    xn, t_xn = emit_ln(ctx, 0, xb_off=26624)
    QOF, KOF, VOF, TRIG = 16384, 18432, 20480, 22528
    qP = arb[:, QOF:QOF + T]
    kP = arb[:, KOF:KOF + T]
    vP = arb[:, VOF:VOF + T]
    YB = arb[:, 26624:26624 + NC * T]
    cos = arb[:, TRIG:TRIG + T]
    sin = arb[:, TRIG + T:TRIG + 2 * T]
    qre = arb[:, QROF:QROF + T]
    qim = arb[:, QIOF:QIOF + T]
    kre = arb[:, KROF:KROF + T]
    kim = arb[:, KIOF:KIOF + T]
    p.dma("sync", cos, dram["c_poscos_b"], chan="trig")
    p.dma("sync", sin, dram["c_possin_b"], chan="trig")
    t_trig = p.dfence("trig")
    dcol = ctx.nc.alloc_sbuf_tensor(ctx.name("dcol"), [128, 40], F32)[:]
    dch = ctx.name("delta")
    p.dma("sync", dcol[:, 0:8], dram["cb_delta"], chan=dch)
    t_d = p.dfence(dch)
    PI = math.pi
    t1 = p.op("vector", lambda v: v.tensor_scalar(
        dcol[:, 0:8], dcol[:, 0:8], 0.0, -2.0 * PI, ALU.min, ALU.max),
        waits=(t_d,))
    t2 = p.op("scalar", lambda s: s.activation(dcol[:, 24:32], dcol[:, 0:8],
                                               AF.Sin, scale=0.25), waits=(t1,))
    t3 = p.op("scalar", lambda s: s.activation(dcol[:, 32:40], dcol[:, 24:32],
                                               AF.Square), waits=(t2,))
    t4 = p.op("vector", lambda v: v.tensor_scalar(
        dcol[:, 32:40], dcol[:, 32:40], -1.0, None, ALU.mult), waits=(t3,))
    t5 = p.op("scalar", lambda s: s.activation(
        dcol[:, 32:40], dcol[:, 32:40], AF.Sqrt,
        bias=cin["c_scal"][:, 1:2]), waits=(t4,))
    t6 = p.op("vector", lambda v: v.tensor_tensor(
        dcol[:, 8:16], dcol[:, 24:32], dcol[:, 32:40], ALU.mult), waits=(t5,))
    t7 = p.op("vector", lambda v: v.tensor_scalar(
        dcol[:, 8:16], dcol[:, 8:16], 2.0, None, ALU.mult))
    t8 = p.op("scalar", lambda s: s.activation(
        dcol[:, 16:24], dcol[:, 24:32], AF.Square), waits=(t7,))
    t9 = p.op("vector", lambda v: v.tensor_scalar(
        dcol[:, 16:24], dcol[:, 16:24], -2.0, 1.0, ALU.mult, ALU.add),
        waits=(t8,))
    t10 = p.op("vector", lambda v: v.tensor_tensor(
        dcol[:, 24:32], dcol[:, 8:16], dcol[:, 16:24], ALU.mult),
        waits=(t9, t8))
    t11 = p.op("vector", lambda v: v.tensor_scalar(
        dcol[:, 24:32], dcol[:, 24:32], 2.0, None, ALU.mult))
    t12 = p.op("scalar", lambda s: s.activation(
        dcol[:, 32:40], dcol[:, 8:16], AF.Square), waits=(t11,))
    t13 = p.op("vector", lambda v: v.tensor_scalar(
        dcol[:, 32:40], dcol[:, 32:40], -2.0, 1.0, ALU.mult, ALU.add),
        waits=(t12,))
    t_dc = t13

    ws = WStream(ctx)
    t_ones = p.op("vector", lambda v: v.memset(
        b["vt"].rearrange("p (k c) -> p k c", c=130)[:, :, 64:65], 1.0))
    t_ones2 = p.op("vector", lambda v: v.memset(
        b["vt"].rearrange("p (k c) -> p k c", c=130)[:, :, 129:130], 1.0))
    prev_attn = (t_xn, t_trig, t_dc)
    for pair in range(NC):
        toks = {"q": [], "k": [], "v": []}
        mu_toks = {"q": [], "k": []}

        def mk_cb(dst, which):
            def cb(oc, tt, ps, mmtok):
                if which == "v":
                    tok = p.op("scalar", lambda s, tt=tt, ps=ps: s.copy(
                        dst[:, tt * TT:tt * TT + TT], ps[0:128, 0:TT]),
                        waits=(mmtok,))
                    toks[which].append(tok)
                    return tok
                tA = arf[:, 0:512]
                tB = b["usb"]
                o1 = p.op("scalar", lambda s, ps=ps: s.activation(
                    tA, ps[0:128, 0:TT], AF.Abs), waits=(mmtok,))
                o2 = p.op("scalar", lambda s: s.activation(
                    tA, tA, AF.Exp, scale=-1.0))
                o3 = p.op("scalar", lambda s: s.activation(
                    tA, tA, AF.Ln, bias=cin["c_scal"][:, 1:2]))
                o4 = p.op("scalar", lambda s, ps=ps: s.activation(
                    tB, ps[0:128, 0:TT], AF.Relu))
                o5 = p.op("vector", lambda v, tt=tt: v.tensor_tensor(
                    dst[:, tt * TT:tt * TT + TT], tA, tB, ALU.add),
                    waits=(o3, o4))
                toks[which].append(o5)
                mu_toks[which].append(o5)
                return o5
            return cb
        for which, dst, coff in (("q", qP, 0), ("k", kP, D),
                                 ("v", vP, 2 * D)):
            emit_linear(ctx, ws, lambda dc, tt: xn[:, dc * T + tt * TT:
                                                   dc * T + tt * TT + TT],
                        dram["cb_w_attn"], D, 128, mk_cb(dst, which),
                        first_waits=prev_attn, col_off=coff + pair * 128,
                        bank_pair=("ps3", "ps4"),
                        conv_engines=("vector", "scalar"))
            prev_attn = ()
        rot = []
        for tt in range(NTT):
            sl = slice(tt * TT, tt * TT + TT)
            w0 = (mu_toks["q"][-1], mu_toks["k"][-1]) if tt == 0 else ()
            rot.append(p.op("vector", lambda v, sl=sl: v.tensor_tensor(
                qre[:, sl], qP[:, sl], cos[:, sl], ALU.mult), waits=w0))
            rot.append(p.op("vector", lambda v, sl=sl: v.tensor_tensor(
                qim[:, sl], qP[:, sl], sin[:, sl], ALU.mult)))
            cpd = arf[:, 0:512]
            spd = b["usb"]
            p.op("vector", lambda v, sl=sl, pair=pair: v.tensor_scalar(
                cpd, sin[:, sl], dcol[:, 24 + pair:25 + pair], None, ALU.mult))
            p.op("vector", lambda v, sl=sl, pair=pair: v.scalar_tensor_tensor(
                cpd, cos[:, sl], dcol[:, 32 + pair:33 + pair], cpd,
                ALU.mult, ALU.subtract))
            p.op("vector", lambda v, sl=sl, pair=pair: v.tensor_scalar(
                spd, cos[:, sl], dcol[:, 24 + pair:25 + pair], None, ALU.mult))
            p.op("vector", lambda v, sl=sl, pair=pair: v.scalar_tensor_tensor(
                spd, sin[:, sl], dcol[:, 32 + pair:33 + pair], spd,
                ALU.mult, ALU.add))
            rot.append(p.op("vector", lambda v, sl=sl: v.tensor_tensor(
                kre[:, sl], kP[:, sl], cpd, ALU.mult)))
            rot.append(p.op("vector", lambda v, sl=sl: v.tensor_tensor(
                kim[:, sl], kP[:, sl], spd, ALU.mult)))
        if upto == "rot0":
            return rot[-1]
        t_vt = emit_vtok(ctx, vP, first_waits=tuple(toks["v"]))
        pre = tuple(rot[-4:]) + (t_vt, t_ones, t_ones2)

        def q_lhs2(h, qt, part):
            buf = qre if part == 0 else qim
            return buf[h * 64:h * 64 + 64, qt * TT:qt * TT + TT]

        def k_pair2(h, kb, part):
            buf = kre if part == 0 else kim
            return buf[h * 64:h * 64 + 64, kb * 128:kb * 128 + 128]

        yt = emit_attn_pair(
            ctx, pair,
            q_lhs=q_lhs2, k_pair=k_pair2,
            y_dst=lambda h, qt, pair=pair: YB[
                h * 64:h * 64 + 64,
                pair * T + qt * TT:pair * T + qt * TT + TT],
            causal=True, scale=0.125, first_waits=pre, two_part=True)
        prev_attn = tuple(yt[-2:])
    if upto == "pairs":
        return prev_attn[-1]

    def ocb(oc, tt, ps, mmtok):
        sl = slice(oc * T + tt * TT, oc * T + tt * TT + TT)
        return p.op("vector", lambda v, sl=sl, ps=ps: v.tensor_tensor(
            b["xres"][:, sl], b["xres"][:, sl], ps[0:128, 0:TT], ALU.add),
            waits=(mmtok,))
    emit_linear(ctx, ws, lambda dc, tt: YB[:, dc * T + tt * TT:
                                           dc * T + tt * TT + TT],
                dram["cb_w_proj"], D, D, ocb, first_waits=prev_attn,
                bank_pair=("ps3", "ps4"), conv_engines=("scalar", "gpsimd"))
    return p.fence("vector")


def build_sub_cosformer(upto="full"):
    nc = bass.Bass("TRN2", target_bir_lowering=False)
    p = Prog(nc)
    ctx = Ctx(nc, p)
    xin = nc.dram_tensor("xin", [D, T], F32, kind="ExternalInput").ap()
    xout = nc.dram_tensor("xout", [D, T], F32, kind="ExternalOutput").ap()
    ctx.dram["cb_w_attn"] = nc.dram_tensor("cb_w_attn", [D, 3 * D], F32,
                                           kind="ExternalInput").ap()
    ctx.dram["cb_w_proj"] = nc.dram_tensor("cb_w_proj", [D, D], F32,
                                           kind="ExternalInput").ap()
    ctx.dram["cb_delta"] = nc.dram_tensor("cb_delta", [128, 8], F32,
                                          kind="ExternalInput").ap()
    ctx.dram["c_poscos_b"] = nc.dram_tensor("c_poscos_b", [128, T], BF16,
                                            kind="ExternalInput").ap()
    ctx.dram["c_possin_b"] = nc.dram_tensor("c_possin_b", [128, T], BF16,
                                            kind="ExternalInput").ap()
    cnames = ["c_ident", "c_ones_row", "c_ones_row_f", "c_ones_col",
              "c_scal"]
    for n in cnames + ["c_cmask"]:
        shape, dt = CONST_SPECS[n]
        ctx.dram[n] = nc.dram_tensor(n, shape, dt, kind="ExternalInput").ap()
    alloc_shared(ctx)
    load_consts(ctx, cnames)
    ctx.cin["c_cmask"] = ctx.b["arb"][:, CMOF:CMOF + 2048]
    p.dma("sync", ctx.cin["c_cmask"], ctx.dram["c_cmask"], chan="misc")
    xres = ctx.b["xres"]
    for c in range(NC):
        p.dma("gpsimd", xres[:, c * T:(c + 1) * T],
              xin[c * 128:(c + 1) * 128, :], chan="xio")
    startw = (p.dfence("xio"), p.dfence("misc"))
    for e in ("vector", "tensor", "scalar", "gpsimd"):
        p.op(e, None, waits=startw)
    tdone = emit_cosformer(ctx, upto)
    for c in range(NC):
        p.dma("gpsimd", xout[c * 128:(c + 1) * 128, :],
              xres[:, c * T:(c + 1) * T],
              waits=(tdone, p.fence("scalar"), p.fence("tensor"),
                     p.fence("vector")) if c == 0 else (), chan="xio")
    p.op("gpsimd", None, waits=(p.dfence("xio"),))
    return p.build()




KW0, KW1 = 0, 8192
QW0, QW1 = 16384, 24576
VW0, VW1 = 32768, 34816
YSOF = 36864
SW = 16


def emit_hybrid(ctx, upto="full"):
    p, b, cin, dram = ctx.p, ctx.b, ctx.cin, ctx.dram
    nc = ctx.nc
    arb, arf = b["arb"], b["arf"]
    xres = b["xres"]
    xn, t_xn = emit_ln(ctx, 0, xb_off=26624)
    QOF, KOF, VOF = 16384, 18432, 20480
    qP = arb[:, QOF:QOF + T]
    kP = arb[:, KOF:KOF + T]
    vP = arb[:, VOF:VOF + T]
    YB = arb[:, 26624:26624 + NC * T]
    GOF, AOF2 = 47136, 55328
    gbuf = arb[:, GOF:GOF + 4 * T]
    abuf = arf[:, 300:556]
    bcol = arf[:, 0:44]
    b2row = arf[:, 44:60]
    bch = ctx.name("hbias")
    p.dma("sync", bcol[:, 0:24], dram["hy_bqkv"], chan=bch, waits=(t_xn,))
    p.dma("sync", bcol[:, 24:32], dram["hy_bo"], chan=bch)
    p.dma("sync", bcol[:, 32:36], dram["hy_bg1"], chan=bch)
    p.dma("sync", b2row, dram["hy_b2row"], chan=bch)
    tb_tok = p.dfence(bch)

    ws = WStream(ctx)
    t_ones = p.op("vector", lambda v: v.memset(
        b["vt"].rearrange("p (k c) -> p k c", c=130)[:, :, 64:65], 1.0))
    t_ones2 = p.op("vector", lambda v: v.memset(
        b["vt"].rearrange("p (k c) -> p k c", c=130)[:, :, 129:130], 1.0))
    cmask = ctx.cin["c_cmask"]

    qkv_dma = []
    prev_attn = (t_xn, tb_tok)
    for pair in range(NC):
        toks = {"q": [], "k": [], "v": []}

        def mk_cb(dst, which, pair=pair):
            def cb(oc, tt, ps, mmtok):
                bc = bcol[:, {"q": 0, "k": 8, "v": 16}[which] + pair:
                          {"q": 0, "k": 8, "v": 16}[which] + pair + 1]
                tok = p.op("scalar", lambda s, tt=tt, ps=ps: s.activation(
                    dst[:, tt * TT:tt * TT + TT], ps[0:128, 0:TT], AF.Identity,
                    bias=bc), waits=(mmtok,))
                toks[which].append(tok)
                return tok
            return cb
        for which, dst, coff in (("q", qP, 0), ("k", kP, D), ("v", vP, 2 * D)):
            emit_linear(ctx, ws, lambda dc, tt: xn[:, dc * T + tt * TT:
                                                   dc * T + tt * TT + TT],
                        dram["hy_wqkv"], D, 128, mk_cb(dst, which),
                        first_waits=prev_attn, col_off=coff + pair * 128,
                        bank_pair=("ps3", "ps4"),
                        conv_engines=("vector", "scalar"))
            prev_attn = ()
            dname = {"q": "hy_qd", "k": "hy_kd", "v": "hy_vd"}[which]
            qkv_dma.append(p.dma(
                "gpsimd", dram[dname][pair * 128:(pair + 1) * 128, :], dst,
                waits=(toks[which][-1],), chan="qkvst_" + which))
        t_vt = emit_vtok(ctx, vP, first_waits=tuple(toks["v"]))
        pre = tuple(toks["q"]) + tuple(toks["k"]) + (t_vt, t_ones, t_ones2)
        yt = emit_attn_pair(
            ctx, pair,
            q_lhs=lambda h, qt: qP[h * 64:h * 64 + 64, qt * TT:qt * TT + TT],
            k_pair=lambda h, kb: kP[h * 64:h * 64 + 64, kb * 128:kb * 128 + 128],
            y_dst=lambda h, qt, pair=pair: YB[
                h * 64:h * 64 + 64,
                pair * T + qt * TT:pair * T + qt * TT + TT],
            causal=True, scale=0.125, first_waits=pre)
        prev_attn = tuple(yt[-2:])
        stage = b["ebuf"][:, 0:128]
        pst = b["pst"]
        ident = cin["c_ident"]
        sp_prev = prev_attn
        for which, buf, dnm in (("q", qP, "hy_qt"), ("k", kP, "hy_kt"),
                                ("v", vP, "hy_vt")):
            for tb in range(NKB):
                tr = p.op("tensor", lambda te, buf=buf, tb=tb: te.transpose(
                    pst[:, 128:256], buf[:, tb * 128:(tb + 1) * 128],
                    ident), waits=sp_prev)
                ev = p.op("scalar", lambda s: s.copy(stage, pst[:, 128:256]),
                          waits=(tr,))
                dmt = p.dma("gpsimd",
                            dram[dnm][tb * 128:(tb + 1) * 128,
                                      pair * 128:(pair + 1) * 128],
                            stage, waits=(ev,), chan="qkvtr")
                sp_prev = (dmt,)
        prev_attn = prev_attn + (sp_prev[-1],) + tuple(qkv_dma[-3:])
    t_qkvt = p.dfence("qkvtr")
    ysp = []
    for c in range(NC):
        ysp.append(p.dma("gpsimd", dram["hy_yd"][c * 128:(c + 1) * 128, :],
                         YB[:, c * T:(c + 1) * T],
                         waits=prev_attn if c == 0 else (), chan="ysp"))
    t_ysp = p.dfence("ysp")

    gtoks = []

    def gcb(oc, tt, ps, mmtok):
        tok = p.op("scalar", lambda s, oc=oc, tt=tt, ps=ps: s.activation(
            gbuf[:, oc * T + tt * TT:oc * T + tt * TT + TT], ps[0:128, 0:TT],
            AF.Gelu, bias=bcol[:, 32 + oc:33 + oc]), waits=(mmtok,))
        gtoks.append(tok)
        return tok
    emit_linear(ctx, ws, lambda dc, tt: xn[:, dc * T + tt * TT:
                                           dc * T + tt * TT + TT],
                dram["hy_gw1"], D, 512, gcb, first_waits=prev_attn,
                bank_pair=("ps3", "ps4"), conv_engines=("vector", "scalar"))
    gw2 = arb[:, 57632:57696]
    gw2f = arf[:, 64:128]
    gch = ctx.name("gw2")
    dt0 = p.dma("sync", gw2f.rearrange("p (c e) -> p c e", e=16),
                dram["hy_gw2"].rearrange("(c p) e -> p c e", p=128), chan=gch,
                waits=prev_attn)
    tcv = p.op("scalar", lambda s: s.copy(gw2, gw2f), waits=(dt0,))
    ps5 = b["ps5"]
    a_toks = []
    prev_ev = ()
    for tb in range(NKB):
        mmt = None
        for gc in range(4):
            def mm(t, tb=tb, gc=gc):
                return t.matmul(ps5[0:128, 0:16],
                                gbuf[:, gc * T + tb * 128:gc * T + (tb + 1) * 128],
                                gw2[:, gc * 16:gc * 16 + 16],
                                start=(gc == 0), stop=(gc == 3))
            w = [tcv, gtoks[-1]] if gc == 0 else []
            if gc == 0:
                w += list(prev_ev)
            mmt = p.op("tensor", mm, waits=tuple(w), track=(gc == 3))
        t1 = p.op("vector", lambda v, tb=tb: v.tensor_tensor(
            arf[:, 256:272], ps5[0:128, 0:16], b2row[:, 0:16], ALU.add),
            waits=(mmt, tb_tok))
        t2 = p.op("scalar", lambda s, tb=tb: s.activation(
            abuf[:, tb * 16:(tb + 1) * 16], arf[:, 256:272], AF.Sigmoid),
            waits=(t1,))
        prev_ev = (t2,)
        a_toks.append(t2)
    if upto == "gate":
        return a_toks[-1]

    xsp = []
    for c in range(NC):
        xsp.append(p.dma("gpsimd", dram["hy_xd"][c * 128:(c + 1) * 128, :],
                         xres[:, c * T:(c + 1) * T], chan="xsp"))
    t_xsp = p.dfence("xsp")
    Rt = xres[:, 0:512]
    Gs = xres[:, 512:1024]
    QG = xres[:, 1024:1536]
    Mb = xres[:, 1536:2048]
    lam = xres[:, 2048:2560]
    wst = xres[:, 2560:3072]
    sch = ctx.name("scan")
    d1 = p.dma("sync", lam, dram["hy_lam"], waits=(t_xsp,), chan=sch)
    d2 = p.dma("sync", Gs, dram["hy_gam"], chan=sch)
    d3 = p.dma("sync", wst, dram["hy_wst"], chan=sch)
    t_sc = p.dfence(sch)
    tR = p.op("scalar", lambda s: s.activation(Rt, lam, AF.Sigmoid,
                                               scale=-1.0), waits=(t_sc,))
    wstb = arb[:, 57696:58208]
    t_wstb = p.op("vector", lambda v: v.tensor_copy(wstb, wst), waits=(t_sc,))
    tG = p.op("vector", lambda v: v.tensor_scalar(Gs, Gs, 0.125, None,
                                                  ALU.mult), waits=(t_sc,))
    zts = []
    for off, ln in ((KW0, 8192), (KW1, 8192), (QW0, 8192), (QW1, 8192),
                    (VW0, 2048), (VW1, 2048)):
        zts.append(p.op("vector", lambda v, off=off, ln=ln: v.memset(
            arb[0:16, off:off + ln], 0.0), waits=(t_ysp,) if not zts else ()))
    t_z = zts[-1]
    ybuf = arb[:, YSOF:YSOF + T * 8]

    def refill(t0, buf_par, wait):
        toks = []
        kwo = (KW0, KW1)[buf_par]
        qwo = (QW0, QW1)[buf_par]
        vwo = (VW0, VW1)[buf_par]
        for nm, wo, q in (("hy_kt", kwo, "scalar"), ("hy_qt", qwo, "gpsimd")):
            dd = arb[0:16, wo:wo + SW * 512].rearrange(
                "p (dt f) -> p dt f", dt=SW)
            sv = dram[nm].rearrange("t (h j) -> h t j", h=16)
            for g in range(8):
                toks.append(p.dma(
                    q, dd[2 * g:2 * g + 2, :, g * 64:g * 64 + 64],
                    sv[2 * g:2 * g + 2, t0:t0 + SW, :],
                    waits=wait if g == 0 else (),
                    chan=f"rf_{nm}_{buf_par}"))
        dd = arb[0:16, vwo:vwo + SW * 128].rearrange(
            "p (dt f) -> p dt f", dt=SW)
        sv = dram["hy_vt"].rearrange("t (h i) -> h t i", h=16)
        toks.append(p.dma("gpsimd", dd[0:16:2, :, 0:64],
                          sv[0:16:2, t0:t0 + SW, :], waits=wait,
                          chan=f"rf_v_{buf_par}"))
        toks.append(p.dma("gpsimd", dd[1:16:2, :, 64:128],
                          sv[1:16:2, t0:t0 + SW, :],
                          chan=f"rf_v_{buf_par}"))
        return (p.dfence(f"rf_hy_kt_{buf_par}"), p.dfence(f"rf_hy_qt_{buf_par}"),
                p.dfence(f"rf_v_{buf_par}"))

    d16 = cin["c_delta16"]
    Zb = b["ps0"]
    Qb = b["ps1"]
    win_tok = [None, None]
    win_tok[0] = refill(0, 0, (t_z, t_qkvt))
    win_tok[1] = refill(SW, 1, (t_z,))
    dec_tok = None
    mmh_tok = None
    m_tok = ()
    qg_tok = ()
    mmq_tok = None
    y_toks = []
    win_free = [(), ()]
    for t in range(T):
        par = (t // SW) % 2
        kwo, qwo, vwo = (KW0, KW1)[par], (QW0, QW1)[par], (VW0, VW1)[par]
        dt = t % SW
        if dt == 0 and t > 0:
            opar = par ^ 1
            nxt = t + SW
            if nxt < T:
                win_tok[opar] = refill(nxt, opar, win_free[opar])
        if t > 0:
            dec_tok = p.op("vector", lambda v: v.tensor_tensor(
                Zb[0:128, 0:TT], Zb[0:128, 0:TT], Rt, ALU.mult),
                waits=(mmh_tok, tR) + m_tok)
        vsl = arb[0:16, vwo + dt * 128:vwo + dt * 128 + 128]
        ksl = arb[0:16, kwo + dt * 512:kwo + dt * 512 + 512]
        qsl = arb[0:16, qwo + dt * 512:qwo + dt * 512 + 512]
        w = [win_tok[par][0], win_tok[par][2]] if dt == 0 else []
        if t > 0:
            w.append(dec_tok)
        mmh_tok = p.op("tensor", lambda te, vsl=vsl, ksl=ksl, t=t: te.matmul(
            Zb[0:128, 0:TT], vsl, ksl, start=(t == 0), stop=(t == T - 1),
            skip_group_check=True), waits=tuple(w))
        w2 = [win_tok[par][1]] if dt == 0 else []
        w2 += list(qg_tok)
        mmq_tok = p.op("tensor", lambda te, qsl=qsl: te.matmul(
            Qb[0:128, 0:TT], d16, qsl, start=True, stop=True),
            waits=tuple(w2))
        qg_tok = (p.op("vector", lambda v: v.tensor_tensor(
            QG, Qb[0:128, 0:TT], Gs, ALU.mult), waits=(mmq_tok, tG)),)
        m_tok = (p.op("vector", lambda v: v.tensor_tensor(
            Mb, Zb[0:128, 0:TT], QG, ALU.mult), waits=(mmh_tok,) + qg_tok),)
        def _red(v, t=t):
            with ctx.nc.allow_low_precision(reason="bf16 y_stp store"):
                return v.tensor_reduce(
                    ybuf[:, t * 8:t * 8 + 8],
                    Mb.rearrange("p (hh j) -> p hh j", hh=8), AX.X, ALU.add)
        yv = p.op("vector", _red, waits=m_tok)
        y_toks.append(yv)
        if dt == SW - 1:
            win_free[par] = (mmh_tok, mmq_tok)
    t_scan = y_toks[-1]
    if upto == "scan":
        return t_scan

    psA, psB = b["ps2"], b["ps3"]
    wtoks = []
    prevA = ()
    prevB = ()
    for pair in range(NC):
        for tt in range(NTT):
            qsl = arb[:, 53248 + (tt % 2) * 2048:53248 + (tt % 2) * 2048 + TT]
            dq = p.dma("sync", qsl,
                       dram["hy_qd"][pair * 128:(pair + 1) * 128,
                                     tt * TT:tt * TT + TT],
                       waits=prevA + prevB + ((a_toks[-1],) if pair == 0
                                              and tt == 0 else ()),
                       chan=f"wq{tt % 2}")
            mA = p.op("tensor", lambda te, qsl=qsl, pair=pair: te.matmul(
                psA[0:64, 0:TT], wstb[0:64, pair * 64:pair * 64 + 64],
                qsl[0:64, :], start=True, stop=True),
                waits=(dq, t_wstb) + prevA)
            mB = p.op("tensor", lambda te, qsl=qsl, pair=pair: te.matmul(
                psB[0:64, 0:TT], wstb[64:128, pair * 64:pair * 64 + 64],
                qsl[64:128, :], start=True, stop=True),
                waits=(dq,) + prevB)
            yA = ybuf[0:64, :].rearrange("p (t hh) -> p t hh", hh=8)[
                :, tt * TT:tt * TT + TT, pair:pair + 1]
            yB = ybuf[64:128, :].rearrange("p (t hh) -> p t hh", hh=8)[
                :, tt * TT:tt * TT + TT, pair:pair + 1]
            wA = p.op("vector", lambda v, yA=yA: v.tensor_tensor(
                yA, yA, psA[0:64, 0:TT].rearrange("p (t o) -> p t o", o=1),
                ALU.add), waits=(mA, t_scan))
            wB = p.op("vector", lambda v, yB=yB: v.tensor_tensor(
                yB, yB, psB[0:64, 0:TT].rearrange("p (t o) -> p t o", o=1),
                ALU.add), waits=(mB,))
            prevA = (wA,)
            prevB = (wB,)
            wtoks += [wA, wB]
    t_w = wtoks[-1]
    if upto == "wstat":
        return t_w

    xrl = []
    for c in range(NC):
        xrl.append(p.dma("gpsimd", xres[:, c * T:(c + 1) * T],
                         dram["hy_xd"][c * 128:(c + 1) * 128, :],
                         waits=(t_w, p.fence("vector")) if c == 0 else (),
                         chan="xsp"))
    t_xrl = p.dfence("xsp")
    onesrow = cin["c_ones_row_f"]
    identf = cin["c_ident_f"]
    ps6 = b["ps6"]
    rows = b["rows"]
    YC = arb[:, 26624:26624 + NC * T]
    scb = b["ps5"]
    prev_cmb = ()
    for pair in range(NC):
        ysl = arb[:, 16384 + (pair % 2) * T:16384 + (pair % 2) * T + T]
        dy = p.dma("sync", ysl,
                   dram["hy_yd"][pair * 128:(pair + 1) * 128, :],
                   waits=(t_ysp,) + prev_cmb, chan=f"yrl{pair % 2}")
        for hl in (0, 1):
            h = 2 * pair + hl
            art = ()
            for tb in range(NKB):
                t1 = p.op("tensor", lambda te, tb=tb, h=h: te.transpose(
                    ps6[0:1, 0:128], abuf[:, tb * 16 + h:tb * 16 + h + 1],
                    identf), waits=(a_toks[-1],) + art)
                e1 = p.op("scalar", lambda s, tb=tb: s.copy(
                    rows[0:1, tb * 128:(tb + 1) * 128], ps6[0:1, 0:128]),
                    waits=(t1,))
                art = (e1,)
            for tt in range(NTT):
                bt = p.op("tensor", lambda te, tt=tt: te.matmul(
                    scb[0:64, 0:TT], onesrow[0:1, 0:64],
                    rows[0:1, tt * TT:tt * TT + TT], start=True, stop=True),
                    waits=art + prev_cmb)
                ysm = ysl[hl * 64:hl * 64 + 64, tt * TT:tt * TT + TT]
                yst = ybuf[hl * 64:hl * 64 + 64, :].rearrange(
                    "p (t hh) -> p t hh", hh=8)[:, tt * TT:tt * TT + TT,
                                                pair:pair + 1]
                dst = YC[hl * 64:hl * 64 + 64,
                         pair * T + tt * TT:pair * T + tt * TT + TT]
                c1 = p.op("vector", lambda v, ysm=ysm, yst=yst, dst=dst:
                          v.tensor_tensor(
                              dst, ysm,
                              yst.rearrange("p t o -> p (t o)"), ALU.subtract),
                          waits=(dy, bt))
                c2 = p.op("vector", lambda v, dst=dst: v.tensor_tensor(
                    dst, dst, scb[0:64, 0:TT], ALU.mult))
                c3 = p.op("vector", lambda v, yst=yst, dst=dst: v.tensor_tensor(
                    dst, dst, yst.rearrange("p t o -> p (t o)"), ALU.add))
                prev_cmb = (c3,)
    t_cmb = prev_cmb[0]
    if upto == "combine":
        return t_cmb

    def ocb(oc, tt, ps, mmtok):
        bc = bcol[:, 24 + oc:24 + oc + 1]
        sl = slice(oc * T + tt * TT, oc * T + tt * TT + TT)
        return p.op("vector", lambda v, sl=sl, ps=ps, bc=bc:
                    v.scalar_tensor_tensor(xres[:, sl], ps[0:128, 0:TT],
                                           bc, xres[:, sl], ALU.add, ALU.add),
                    waits=(mmtok,))
    emit_linear(ctx, ws, lambda dc, tt: YC[:, dc * T + tt * TT:
                                           dc * T + tt * TT + TT],
                dram["hy_wo"], D, D, ocb,
                first_waits=(t_cmb, t_xrl), bank_pair=("ps3", "ps4"),
                conv_engines=("scalar", "gpsimd"))
    return p.fence("vector")


def build_sub_hybrid(upto="full"):
    nc = bass.Bass("TRN2", target_bir_lowering=False)
    p = Prog(nc)
    ctx = Ctx(nc, p)
    xin = nc.dram_tensor("xin", [D, T], F32, kind="ExternalInput").ap()
    xout = nc.dram_tensor("xout", [D, T], F32, kind="ExternalOutput").ap()
    for nm, shape in (("hy_wqkv", [D, 3 * D]), ("hy_wo", [D, D]),
                      ("hy_gw1", [D, 512]), ("hy_gw2", [512, 16]),
                      ("hy_bqkv", [128, 24]), ("hy_bo", [128, 8]),
                      ("hy_bg1", [128, 4]), ("hy_b2row", [128, 16]),
                      ("hy_lam", [128, 512]), ("hy_gam", [128, 512]),
                      ("hy_wst", [128, 512])):
        ctx.dram[nm] = nc.dram_tensor(nm, shape, F32,
                                      kind="ExternalInput").ap()
    for nm, shape in (("hy_qd", [D, T]), ("hy_kd", [D, T]), ("hy_vd", [D, T]),
                      ("hy_yd", [D, T]), ("hy_qt", [T, D]), ("hy_kt", [T, D]),
                      ("hy_vt", [T, D])):
        ctx.dram[nm] = nc.dram_tensor(nm, shape, BF16, kind="Internal").ap()
    ctx.dram["hy_xd"] = nc.dram_tensor("hy_xd", [D, T], F32,
                                       kind="Internal").ap()
    cnames = ["c_ident", "c_ident_f", "c_ones_row", "c_ones_row_f",
              "c_ones_col", "c_scal", "c_delta16"]
    for n in cnames + ["c_cmask"]:
        shape, dt = CONST_SPECS[n]
        ctx.dram[n] = nc.dram_tensor(n, shape, dt, kind="ExternalInput").ap()
    alloc_shared(ctx)
    load_consts(ctx, cnames)
    ctx.cin["c_cmask"] = ctx.b["arb"][:, 55584:55584 + 2048]
    p.dma("sync", ctx.cin["c_cmask"], ctx.dram["c_cmask"], chan="misc")
    xres = ctx.b["xres"]
    for c in range(NC):
        p.dma("gpsimd", xres[:, c * T:(c + 1) * T],
              xin[c * 128:(c + 1) * 128, :], chan="xio")
    startw = (p.dfence("xio"), p.dfence("misc"))
    for e in ("vector", "tensor", "scalar", "gpsimd"):
        p.op(e, None, waits=startw)
    tdone = emit_hybrid(ctx, upto)
    for c in range(NC):
        p.dma("gpsimd", xout[c * 128:(c + 1) * 128, :],
              xres[:, c * T:(c + 1) * T],
              waits=(tdone, p.fence("scalar"), p.fence("tensor"),
                     p.fence("vector")) if c == 0 else (), chan="xio")
    p.op("gpsimd", None, waits=(p.dfence("xio"),))
    return p.build()


def build_sub_mha(kind, upto="full"):
    nc = bass.Bass("TRN2", target_bir_lowering=False)
    p = Prog(nc)
    ctx = Ctx(nc, p)
    pfx = {"l0a": "w0a", "l1a": "w1a"}[kind]
    xin = nc.dram_tensor("xin", [D, T], F32, kind="ExternalInput").ap()
    xout = nc.dram_tensor("xout", [D, T], F32, kind="ExternalOutput").ap()
    for wname in ("_wq", "_wk", "_wv", "_wo"):
        ctx.dram[pfx + wname] = nc.dram_tensor(pfx + wname, [D, D], F32,
                                               kind="ExternalInput").ap()
    for bname in ("_bq", "_bk", "_bv", "_bo"):
        ctx.dram[pfx + bname] = nc.dram_tensor(pfx + bname, [128, 8], F32,
                                               kind="ExternalInput").ap()
    cnames = ["c_ident", "c_ones_row", "c_ones_row_f", "c_ones_col", "c_scal"]
    lnames = list(cnames)
    if kind == "l1a":
        cnames += ["c_ropecos", "c_ropesin", "c_roperm"]
        lnames += ["c_roperm"]
    for n in cnames:
        shape, dt = CONST_SPECS[n]
        ctx.dram[n] = nc.dram_tensor(n, shape, dt, kind="ExternalInput").ap()
    alloc_shared(ctx)
    load_consts(ctx, lnames)
    xres = ctx.b["xres"]
    tlds = [p.dma("gpsimd", xres[:, c * T:(c + 1) * T],
                  xin[c * 128:(c + 1) * 128, :], chan="xio") for c in range(NC)]
    startw = (p.dfence("xio"), p.dfence("misc"))
    for e in ("vector", "tensor", "scalar", "gpsimd"):
        p.op(e, None, waits=startw)
    tdone = emit_mha(ctx, kind, upto)
    for c in range(NC):
        p.dma("gpsimd", xout[c * 128:(c + 1) * 128, :],
              xres[:, c * T:(c + 1) * T],
              waits=(tdone, p.fence("scalar"), p.fence("tensor"))
              if c == 0 else (), chan="xio")
    p.op("gpsimd", None, waits=(p.dfence("xio"),))
    return p.build()


def prep_mha_weights(kind, inputs):
    pfx = {"l0a": "w0a", "l1a": "w1a"}[kind]
    ipfx = {"l0a": "L0_attn_a", "l1a": "L1_attn_a"}[kind]
    d = {}
    if kind == "l0a":
        for nm in ("Wq", "Wk", "Wv"):
            w = np.asarray(inputs[f"{ipfx}_{nm}"])
            d[pfx + "_w" + nm[1].lower()] = _f32(w.transpose(2, 0, 1).reshape(D, D))
        for nm in ("bq", "bk", "bv"):
            d[pfx + "_" + nm] = _bias_cols(np.asarray(inputs[f"{ipfx}_{nm}"]).reshape(-1))
    else:
        for nm in ("Wq", "Wk", "Wv"):
            w = np.asarray(inputs[f"{ipfx}_{nm}"])
            d[pfx + "_w" + nm[1].lower()] = _f32(w.T)
        for nm in ("bq", "bk", "bv"):
            d[pfx + "_" + nm] = _bias_cols(np.asarray(inputs[f"{ipfx}_{nm}"]))
    wo = np.asarray(inputs[f"{ipfx}_Wo"])
    d[pfx + "_wo"] = _f32(wo.T)
    d[pfx + "_bo"] = _bias_cols(np.asarray(inputs[f"{ipfx}_bo"]))
    return d


def kernel(**inputs):
    raise NotImplementedError


def _prep_all(inputs):
    assert inputs["L0_attn_a_Wq"] is not None and inputs["L0_attn_a_Wk"] \
        is not None and inputs["L0_attn_a_Wv"] is not None
    assert inputs["L1_attn_a_Wq"] is not None and inputs["L1_attn_a_Wk"] \
        is not None and inputs["L1_attn_a_Wv"] is not None
    wd = {}
    wd.update(prep_mha_weights("l0a", inputs))
    wd.update(prep_mha_weights("l1a", inputs))
    wd.update({
        "moe_gate_w": _f32(inputs["L0_ffn_gate_w"]),
        "moe_base_wgu": _f32(inputs["L0_ffn_base_wgu"]),
        "moe_base_wd": _f32(inputs["L0_ffn_base_wd"]),
        "moe_exp_wgu": _f32(np.asarray(inputs["L0_ffn_exp_wgu"]).reshape(
            8 * 1024, 5632)),
        "moe_exp_wd": _f32(np.asarray(inputs["L0_ffn_exp_wd"]).reshape(
            8 * 2816, 1024)),
        "cb_w_attn": _f32(np.asarray(inputs["L0_attn_b_w_attn"]).T),
        "cb_w_proj": _f32(np.asarray(inputs["L0_attn_b_w_proj"]).T),
        "bit_w1": _f32(np.asarray(inputs["L1_ffn_W1"]).T),
        "bit_w2": _f32(np.asarray(inputs["L1_ffn_W2"]).T),
        "bit_b1": _bias_cols(np.asarray(inputs["L1_ffn_b1"])),
        "bit_b2": _bias_cols(np.asarray(inputs["L1_ffn_b2"])),
        "hy_wqkv": _f32(inputs["L1_attn_b_Wqkv_w"]),
        "hy_wo": _f32(inputs["L1_attn_b_out_w"]),
        "hy_gw1": _f32(inputs["L1_attn_b_gate_w1"]),
        "hy_gw2": _f32(inputs["L1_attn_b_gate_w2"]),
        "hy_bqkv": _bias_cols(np.asarray(inputs["L1_attn_b_Wqkv_b"])),
        "hy_bo": _bias_cols(np.asarray(inputs["L1_attn_b_out_b"])),
        "hy_bg1": _bias_cols(np.asarray(inputs["L1_attn_b_gate_b1"])),
        "hy_b2row": _f32(np.tile(np.asarray(
            inputs["L1_attn_b_gate_b2"])[None, :], (128, 1))),
    })
    delta = np.asarray(inputs["L0_attn_b_delta"])
    dcol = np.zeros((128, 8), np.float32)
    for h in range(NH):
        dcol[(h % 2) * 64:(h % 2) * 64 + 64, h // 2] = delta[h]
    wd["cb_delta"] = dcol
    lam = np.asarray(inputs["L1_attn_b_Lambda_raw"])
    gam = np.asarray(inputs["L1_attn_b_Gamma"])
    wst = np.asarray(inputs["L1_attn_b_W_static"])
    lamL = np.zeros((128, 512), np.float32)
    gamL = np.zeros((128, 512), np.float32)
    wstL = np.zeros((128, 512), np.float32)
    for h in range(NH):
        hh, hl = h // 2, h % 2
        lamL[hl * 64:(hl + 1) * 64, hh * 64:(hh + 1) * 64] = lam[h]
        gamL[hl * 64:(hl + 1) * 64, hh * 64:(hh + 1) * 64] = gam[h]
        wstL[hl * 64:(hl + 1) * 64, hh * 64:(hh + 1) * 64] = wst[h].T
    wd["hy_lam"] = lamL
    wd["hy_gam"] = gamL
    wd["hy_wst"] = wstL
    wd.update(host_constants())
    return wd


_SEQ = [("l0a", lambda: build_sub_mha("l0a")),
        ("moe", build_sub_moe),
        ("cos", build_sub_cosformer),
        ("moe", build_sub_moe),
        ("l1a", lambda: build_sub_mha("l1a")),
        ("bit", build_sub_bitffn),
        ("hyb", build_sub_hybrid),
        ("bit", build_sub_bitffn)]


def _declared_inputs(nc):
    names = set()
    for alloc in nc.m.functions[0].allocations:
        if isinstance(alloc, mybir.MemoryLocationSet) and                 alloc.kind == "ExternalInput":
            names.add(alloc.memorylocations[0].name)
    names.discard("partition_id")
    return names


def kernel(**inputs):
    wd = _prep_all(inputs)
    x = np.asarray(inputs["L0_attn_a_hidden_state"])
    xs = [_f32(x[b].T) for b in range(8)]
    for key, builder in _SEQ:
        if key not in _CACHE:
            _CACHE[key] = builder()
        nc = _CACHE[key]
        decl = _declared_inputs(nc)
        in_maps = []
        for b in range(8):
            m = {"xin": xs[b]}
            m.update({k: v for k, v in wd.items() if k in decl})
            in_maps.append(m)
        res = run_bass_kernel_spmd(nc, in_maps, core_ids=list(range(8)))
        xs = [np.ascontiguousarray(res.results[b]["xout"])
              for b in range(8)]
    out = np.stack([xs[b].T for b in range(8)], axis=0)
    return np.ascontiguousarray(out, dtype=np.float32)


# version 05
# speedup vs baseline: 1.2121x; 1.0391x over previous
import math
import numpy as np
import ml_dtypes
import concourse.bass as bass
import concourse.mybir as mybir
from concourse.bass_utils import run_bass_kernel_spmd

F32 = mybir.dt.float32
BF16 = mybir.dt.bfloat16
AF = mybir.ActivationFunctionType
ALU = mybir.AluOpType
AX = mybir.AxisListType

T = 2048
D = 1024
NH = 16
DH = 64
NC = D // 128
TT = 512
NTT = T // TT
NKB = T // 128
WG = 256

_CACHE = {}


class Sem:
    __slots__ = ("name", "n", "h")

    def __init__(self, name):
        self.name = name
        self.n = 0
        self.h = None


class Prog:
    ENGINES = ("sync", "tensor", "vector", "scalar", "gpsimd")

    def __init__(self, nc):
        self.nc = nc
        self.ops = {e: [] for e in self.ENGINES}
        self.sems = []
        self.esem = {e: self.sem("es_" + e) for e in self.ENGINES}
        self.dchans = {}

    def dchan(self, name):
        if name not in self.dchans:
            self.dchans[name] = self.sem("dc_" + name)
        return self.dchans[name]

    def sem(self, name):
        s = Sem(name)
        self.sems.append(s)
        return s

    AUTO_SELF = ("vector", "scalar", "gpsimd")

    def op(self, engine, fn, waits=(), track=True):
        tok = None
        incs = ()
        waits = [w for w in waits if w]
        s = self.esem[engine]
        if engine in self.AUTO_SELF and fn is not None and s.n > 0:
            waits.append((s, s.n))
        if track:
            s.n += 1
            tok = (s, s.n)
            incs = ((s, 1),)
        self.ops[engine].append((fn, tuple(waits), incs))
        return tok

    def dma(self, queue, out, in_, waits=(), chan="misc", **kw):
        s = self.dchan(chan)
        s.n += 16
        tok = (s, s.n)

        def fn(eng):
            return eng.dma_start(out=out, in_=in_, **kw)
        self.ops[queue].append((fn, tuple(w for w in waits if w), ((s, 16),)))
        return tok

    def fence(self, engine):
        s = self.esem[engine]
        return (s, s.n)

    def dfence(self, chan):
        s = self.dchan(chan)
        return (s, s.n)

    def build(self):
        import contextlib
        nc = self.nc
        with contextlib.ExitStack() as stack:
            for s in self.sems:
                s.h = stack.enter_context(nc.semaphore(s.name))
            block = stack.enter_context(nc.Block())
            for ename in self.ENGINES:
                ops = self.ops[ename]
                if not ops:
                    continue

                def body(eng, ops=ops):
                    for fn, waits, incs in ops:
                        for s, v in waits:
                            eng.wait_ge(s.h, v)
                        instr = fn(eng) if fn is not None else None
                        if incs:
                            (s0, k0), rest = incs[0], incs[1:]
                            if instr is not None:
                                instr.then_inc(s0.h, k0)
                            else:
                                eng.nop().then_inc(s0.h, k0)
                            for s, k in rest:
                                eng.nop().then_inc(s.h, k)
                getattr(block, ename)(body)
        return nc


def _bf16(x):
    return np.ascontiguousarray(
        np.asarray(x, dtype=np.float32).astype(ml_dtypes.bfloat16))


def _f32(x):
    return np.ascontiguousarray(x, dtype=np.float32)


def _bias_cols(b):
    n = b.shape[0]
    return _f32(b.reshape(n // 128, 128).T)


def host_constants():
    c = {}
    c["c_ident"] = _bf16(np.eye(128))
    c["c_ident_f"] = _f32(np.eye(128))
    d16 = np.zeros((16, 128), np.float32)
    for h in range(16):
        d16[h, (h % 2) * 64:(h % 2) * 64 + 64] = 1.0
    c["c_delta16"] = _bf16(d16)
    c["c_ones_row"] = _bf16(np.ones((1, 128)))
    c["c_ones_row_f"] = _f32(np.ones((1, 128)))
    sc = np.zeros((128, 8), np.float32)
    sc[:, 0] = 1e-5
    sc[:, 1] = 1.0
    c["c_scal"] = _f32(sc)
    c["c_ones_col"] = _bf16(np.ones((128, 1)))
    c["c_ones_col_f"] = _f32(np.ones((128, 1)))
    masks = np.zeros((4, 128, TT), np.float32)
    fq = np.arange(TT)[None, :]
    pp = np.arange(128)[:, None]
    for j in range(4):
        masks[j] = (fq >= j * 128 + pp).astype(np.float32)
    c["c_cmask"] = _bf16(masks.transpose(1, 0, 2).reshape(128, 4 * TT))
    inv = 1.0 / (10000.0 ** (np.arange(0, DH, 2) / DH))
    fr = np.arange(T)[:, None] * inv[None, :]
    emb = np.concatenate([fr, fr], axis=-1)
    cos = np.cos(emb).T
    sin = np.sin(emb).T
    c["c_ropecos"] = _bf16(np.concatenate([cos, cos], 0))
    c["c_ropesin"] = _bf16(np.concatenate([sin, sin], 0))
    inv0 = 1.0 / (10000.0 ** (np.arange(DH) / DH))
    fr0 = np.arange(T)[:, None] * inv0[None, :]
    c["c_poscos_b"] = _bf16(np.concatenate([np.cos(fr0).T, np.cos(fr0).T], 0))
    c["c_possin_b"] = _bf16(np.concatenate([np.sin(fr0).T, np.sin(fr0).T], 0))
    P = np.zeros((128, 128), np.float32)
    for hl in range(2):
        for dd in range(DH):
            r = hl * 64 + dd
            if dd < 32:
                P[hl * 64 + dd + 32, r] = -1.0
            else:
                P[hl * 64 + dd - 32, r] = 1.0
    c["c_roperm"] = _bf16(P)
    return c


CONST_SPECS = {
    "c_ident": ([128, 128], BF16),
    "c_ident_f": ([128, 128], F32),
    "c_delta16": ([16, 128], BF16),
    "c_ones_row": ([1, 128], BF16),
    "c_ones_row_f": ([1, 128], F32),
    "c_scal": ([128, 8], F32),
    "c_ones_col": ([128, 1], BF16),
    "c_ones_col_f": ([128, 1], F32),
    "c_cmask": ([128, 4 * TT], BF16),
    "c_ropecos": ([128, T], BF16),
    "c_ropesin": ([128, T], BF16),
    "c_poscos_b": ([128, T], BF16),
    "c_possin_b": ([128, T], BF16),
    "c_roperm": ([128, 128], BF16),
}


class Ctx:
    def __init__(self, nc, p):
        self.nc = nc
        self.p = p
        self.cin = {}
        self.dram = {}
        self.b = {}
        self.uid = 0
        self.ws = None

    def name(self, base):
        self.uid += 1
        return f"{base}_{self.uid}"


def get_ws(ctx):
    if ctx.ws is None:
        ctx.ws = WStream(ctx)
    return ctx.ws

    def name(self, base):
        self.uid += 1
        return f"{base}_{self.uid}"


def alloc_shared(ctx):
    nc = ctx.nc
    b = ctx.b
    b["xres"] = nc.alloc_sbuf_tensor("xres", [128, NC * T], F32)[:]
    b["arb"] = nc.alloc_sbuf_tensor("arb", [128, 61440], BF16)[:]
    b["arf"] = nc.alloc_sbuf_tensor("arf", [128, 1024], F32)[:]
    b["wf32"] = nc.alloc_sbuf_tensor("wf32", [128, 2 * WG], F32)[:]
    b["wbf"] = nc.alloc_sbuf_tensor("wbf", [128, 16 * WG], BF16)[:]
    b["rows"] = nc.alloc_sbuf_tensor("rows", [1, 2048], F32)[:]
    b["bias"] = nc.alloc_sbuf_tensor("bias", [128, 40], F32)[:]
    b["lrow"] = b["rows"][0:1, 0:1024]
    b["vt"] = b["arb"][:, 57632:57632 + NKB * 130]
    b["ebuf"] = b["arb"][:, 45088:45088 + 4 * TT]
    b["usb"] = b["arf"][:, 512:1024]
    for i in range(7):
        b[f"ps{i}"] = nc.alloc_psum_tensor(f"ps{i}", [128, 512], F32)[:]
    b["pst"] = nc.alloc_psum_tensor("pst", [128, 1024], BF16)[:]
    return b


def load_consts(ctx, names):
    nc, p = ctx.nc, ctx.p
    toks = []
    for n in names:
        shape, dt = CONST_SPECS[n]
        t = nc.alloc_sbuf_tensor(n + "_s", shape, dt)[:]
        ctx.cin[n] = t
        toks.append(p.dma("sync", t, ctx.dram[n], chan="misc"))
    return toks


def emit_ln(ctx, xn_off, xb_off=None):
    p, b, cin = ctx.p, ctx.b, ctx.cin
    xres = b["xres"]
    if xb_off is None:
        xb_off = xn_off + NC * T
    xn = b["arb"][:, xn_off:xn_off + NC * T]
    xb = b["arb"][:, xb_off:xb_off + NC * T]
    stat = b["ps0"]
    mtile = b["ps1"]
    rtile = b["ps2"]
    rows = b["rows"]
    tsc = b["arf"][:, 0:TT]
    onescol = cin["c_ones_col"]
    onesrow = cin["c_ones_row_f"]

    last = None
    prev_bcast_reads = ()
    prev_ev = ()
    for tt in range(NTT):
        tcopy = [p.op("vector", lambda v, c=c, tt=tt: v.tensor_copy(
            xb[:, c * T + tt * TT:c * T + tt * TT + TT],
            xres[:, c * T + tt * TT:c * T + tt * TT + TT]))
            for c in range(NC)]
        mmt = None
        for c in range(NC):
            sl = slice(c * T + tt * TT, c * T + tt * TT + TT)

            def mm(t, sl=sl, c=c):
                return t.matmul(stat[0:1, 0:TT], onescol[:, 0:1], xb[:, sl],
                                start=(c == 0), stop=(c == NC - 1))
            w = [tcopy[c]]
            if c == 0:
                w += list(prev_ev)
            mmt = p.op("tensor", mm, waits=tuple(w), track=(c == NC - 1))
        ev0 = p.op("scalar", lambda s: s.copy(rows[0:1, 0:TT],
                                              stat[0:1, 0:TT]), waits=(mmt,))
        tsq = [p.op("scalar", lambda s, c=c, tt=tt: s.activation(
            xb[:, c * T + tt * TT:c * T + tt * TT + TT],
            xb[:, c * T + tt * TT:c * T + tt * TT + TT], AF.Square),
            waits=(ev0,) if c == 0 else ())
            for c in range(NC)]
        mmt = None
        for c in range(NC):
            sl = slice(c * T + tt * TT, c * T + tt * TT + TT)

            def mm(t, sl=sl, c=c):
                return t.matmul(stat[0:1, 0:TT], onescol[:, 0:1], xb[:, sl],
                                start=(c == 0), stop=(c == NC - 1))
            w = [tsq[c]]
            if c == 0:
                w.append(ev0)
            mmt = p.op("tensor", mm, waits=tuple(w), track=(c == NC - 1))
        ev1 = p.op("scalar", lambda s: s.copy(rows[0:1, TT:2 * TT],
                                              stat[0:1, 0:TT]), waits=(mmt,))
        prev_ev = (ev1,)
        t_mu = p.op("scalar", lambda s: s.activation(
            rows[0:1, 2 * TT:3 * TT], rows[0:1, 0:TT], AF.Copy,
            scale=1.0 / D), waits=(ev1,))
        t_ms = p.op("scalar", lambda s: s.activation(
            rows[0:1, 3 * TT:4 * TT], rows[0:1, 2 * TT:3 * TT], AF.Square),
            waits=(t_mu,))
        t_var = p.op("vector", lambda v: v.scalar_tensor_tensor(
            rows[0:1, TT:2 * TT], rows[0:1, TT:2 * TT], 1.0 / D,
            rows[0:1, 3 * TT:4 * TT], ALU.mult, ALU.subtract), waits=(t_ms,))
        t_sd = p.op("scalar", lambda s: s.activation(
            rows[0:1, TT:2 * TT], rows[0:1, TT:2 * TT], AF.Sqrt,
            bias=cin["c_scal"][0:1, 0:1]), waits=(t_var,))
        t_rs = p.op("vector", lambda v: v.reciprocal(
            rows[0:1, TT:2 * TT], rows[0:1, TT:2 * TT]), waits=(t_sd,))
        tb1 = p.op("tensor", lambda t: t.matmul(
            mtile[:, 0:TT], onesrow[0:1, :], rows[0:1, 2 * TT:3 * TT],
            start=True, stop=True), waits=(t_rs,) + prev_bcast_reads)
        tb2 = p.op("tensor", lambda t: t.matmul(
            rtile[:, 0:TT], onesrow[0:1, :], rows[0:1, TT:2 * TT],
            start=True, stop=True))
        reads = []
        for c in range(NC):
            xsl = slice(c * T + tt * TT, c * T + tt * TT + TT)
            p.op("vector", lambda v, xsl=xsl: v.tensor_tensor(
                tsc, xres[:, xsl], mtile[:, 0:TT], ALU.subtract),
                waits=(tb1, tb2) if c == 0 else ())
            last = p.op("vector", lambda v, xsl=xsl: v.tensor_tensor(
                xn[:, xsl], tsc, rtile[:, 0:TT], ALU.mult))
            reads.append(last)
        prev_bcast_reads = (last,)
    return xn, last


class WStream:

    def __init__(self, ctx):
        self.ctx = ctx
        self.wslot_users = [(), ()]
        self.mm_last = [(), ()]
        self.par = 0
        self.ce_i = 0
        self.wcnt = 0
        self.bank_free = {}


def emit_linear(ctx, ws, cols_src, w_dram, n_in, n_out, out_cb,
                first_waits=(), col_off=0, bank_pair=("ps3", "ps4"),
                conv_engines=("vector", "scalar", "gpsimd"), G=None, nbuf=2,
                bf16_direct=False, tt_list=None):
    p, b = ctx.p, ctx.b
    nc_in = n_in // 128
    if G is None:
        G = WG
    assert nbuf * nc_in * G <= 16 * WG, "wbf overflow"
    if tt_list is None:
        tt_list = tuple(range(NTT))
    if nbuf == 1:
        ws.mm_last[0] = tuple(ws.mm_last[0]) + tuple(ws.mm_last[1])
    wf32, wbf = b["wf32"], b["wbf"]
    banks = (b[bank_pair[0]], b[bank_pair[1]])
    bank_free = ws.bank_free.setdefault(bank_pair, [(), ()])
    ngroups = (n_out + G - 1) // G
    for g in range(ngroups):
        if nbuf == 2:
            par = ws.par
            ws.par ^= 1
        else:
            par = 0
        cols = min(G, n_out - g * G)
        conv_toks = {}
        for dc in range(nc_in):
            wsl = w_dram[dc * 128:(dc + 1) * 128,
                         col_off + g * G:col_off + g * G + cols]
            dstb = wbf[:, (par * nc_in + dc) * G:(par * nc_in + dc) * G + cols]
            if bf16_direct:
                dtok = p.dma("sync", dstb, wsl,
                             waits=tuple(ws.mm_last[par]) if dc == 0 else (),
                             chan=f"wd{par}_{dc}")
                conv_toks[dc] = dtok
                continue
            slot = ws.wcnt % 2
            ws.wcnt += 1
            dst32 = wf32[:, slot * G:slot * G + cols]
            dtok = p.dma("sync", dst32, wsl, waits=ws.wslot_users[slot],
                         chan=f"wslot{slot}")
            eng = conv_engines[ws.ce_i % len(conv_engines)]
            ws.ce_i += 1

            def cv(e, dstb=dstb, dst32=dst32):
                if hasattr(e, "tensor_copy"):
                    return e.tensor_copy(dstb, dst32)
                return e.copy(dstb, dst32)
            ctok = p.op(eng, cv, waits=(dtok,) + tuple(ws.mm_last[par]))
            ws.wslot_users[slot] = (ctok,)
            conv_toks[dc] = ctok
        gsz = (cols + 127) // 128
        last_mm = None
        for oci in range(gsz):
            oc = g * (G // 128) + oci
            ocols = min(128, cols - oci * 128)
            bk = oc % 2
            pbank = banks[bk]
            for tt in tt_list:
                mmt = None
                for dc in range(nc_in):
                    lhsT = wbf[:, (par * nc_in + dc) * G + oci * 128:
                               (par * nc_in + dc) * G + oci * 128 + ocols]

                    def mm(t, lhsT=lhsT, dc=dc, tt=tt, ocols=ocols, pbank=pbank):
                        return t.matmul(pbank[0:ocols, 0:TT], lhsT,
                                        cols_src(dc, tt), start=(dc == 0),
                                        stop=(dc == nc_in - 1))
                    w = [conv_toks[dc]]
                    if dc == 0:
                        w += list(bank_free[bk]) + list(first_waits)
                        first_waits = ()
                    mmt = p.op("tensor", mm, waits=tuple(w),
                               track=(dc == nc_in - 1))
                etok = out_cb(oc, tt, pbank, mmt)
                bank_free[bk] = (etok,)
                last_mm = mmt
        ws.mm_last[par] = (last_mm,)
    return bank_free


def emit_vtok(ctx, v_src_pair, first_waits=()):
    p, b, cin = ctx.p, ctx.b, ctx.cin
    vt = b["vt"]
    pst = b["pst"]
    ident = cin["c_ident"]
    tok = None
    prev_ev = tuple(first_waits) + tuple(ctx.b.get("vt_last", ()))
    for kb in range(NKB):
        t1 = p.op("tensor", lambda t, kb=kb: t.transpose(
            pst[:, 0:128], v_src_pair[:, kb * 128:(kb + 1) * 128],
            ident[:, :]), waits=prev_ev)
        dst = vt[:, kb * 130:kb * 130 + 130].rearrange(
            "p (h c) -> p h c", h=2)[:, :, 0:64]
        srcv = pst[:, 0:128].rearrange("p (h c) -> p h c", h=2)
        e1 = p.op("scalar", lambda s, dst=dst, srcv=srcv: s.copy(dst, srcv),
                  waits=(t1,))
        prev_ev = (e1,)
        tok = e1
    ctx.b["vt_last"] = (tok,)
    return tok


def emit_attn_pair(ctx, pair, q_lhs, k_pair, y_dst, causal, scale,
                   extra_mask=None, first_waits=(), two_part=False):
    p, b, cin = ctx.p, ctx.b, ctx.cin
    sb = (b["ps0"], b["ps1"], b["ps2"], b["ps3"])
    ub = (b["ps4"], b["ps5"])
    scb = b["ps6"]
    vt = b["vt"]
    ebuf = b["arf"]
    eb = b["ebuf"]
    usb = b["usb"]
    cmask = cin.get("c_cmask")
    onesrow = cin["c_ones_row_f"]
    rrow = b["rows"]

    y_out_toks = []
    ufree = [(), ()]
    sfree = {(h, par): tuple(first_waits) for h in (0, 1) for par in (0, 1)}
    for qt in range(NTT):
        kb_list = list(range(NKB))
        if causal:
            kb_list = [kb for kb in kb_list if kb * 128 < (qt + 1) * TT]
        efree = {}
        utoks = [None, None]
        for ik, kb in enumerate(kb_list):
            parity = ik % 2
            for h in (0, 1):
                sbank = sb[h * 2 + parity]
                if two_part:
                    p.op("tensor", lambda t, h=h, kb=kb, qt=qt, sbank=sbank:
                         t.matmul(sbank[0:128, 0:TT], k_pair(h, kb, 0),
                                  q_lhs(h, qt, 0), start=True, stop=False),
                         waits=sfree[(h, parity)], track=False)
                    stok = p.op("tensor",
                                lambda t, h=h, kb=kb, qt=qt, sbank=sbank:
                                t.matmul(sbank[0:128, 0:TT], k_pair(h, kb, 1),
                                         q_lhs(h, qt, 1), start=False,
                                         stop=True))
                else:
                    stok = p.op("tensor",
                                lambda t, h=h, kb=kb, qt=qt, sbank=sbank:
                                t.matmul(sbank[0:128, 0:TT], k_pair(h, kb),
                                         q_lhs(h, qt), start=True, stop=True),
                                waits=sfree[(h, parity)])
                esl = eb[:, (h * 2 + parity) * TT:(h * 2 + parity) * TT + TT]
                etok = p.op("scalar", lambda s, esl=esl, sbank=sbank:
                            s.activation(esl, sbank[0:128, 0:TT], AF.Exp,
                                         scale=scale),
                            waits=(stok,) + efree.get((h, parity), ()))
                sfree[(h, parity)] = (etok,)
                mtok = etok
                if extra_mask is not None:
                    mtok = extra_mask(h, kb, qt, esl, etok)
                if causal and kb * 128 >= qt * TT:
                    j = kb - qt * 4
                    mtok = p.op("vector", lambda v, esl=esl, j=j: v.tensor_tensor(
                        esl, esl, cmask[:, j * TT:(j + 1) * TT], ALU.mult),
                        waits=(mtok,))
                first = (ik == 0)
                last = (ik == len(kb_list) - 1)
                vsl = vt[:, kb * 130 + h * 65:kb * 130 + h * 65 + 65]
                ut = p.op("tensor", lambda t, h=h, vsl=vsl, esl=esl, first=first,
                          last=last: t.matmul(ub[h][0:65, 0:TT], vsl, esl,
                                              start=first, stop=last),
                          waits=(mtok,) + (ufree[h] if first else ()))
                efree[(h, parity)] = (ut,)
                if last:
                    utoks[h] = ut
        for h in (0, 1):
            usl = usb
            ev = p.op("scalar", lambda s, usl=usl, h=h: s.copy(
                usl[0:65, :], ub[h][0:65, 0:TT]),
                waits=(utoks[h],) + tuple(y_out_toks[-1:]))
            lrow = b["lrow"][0:1, h * TT:h * TT + TT]
            l1 = p.op("scalar", lambda s, usl=usl, lrow=lrow: s.activation(
                lrow, usl[64:65, :], AF.Ln), waits=(ev,))
            l2 = p.op("scalar", lambda s, lrow=lrow: s.activation(
                lrow, lrow, AF.Exp, scale=-1.0), waits=(l1,))
            bt = p.op("tensor", lambda t, lrow=lrow: t.matmul(
                scb[0:64, 0:TT], onesrow[0:1, 0:64], lrow, start=True,
                stop=True), waits=(l2,) + tuple(y_out_toks[-1:]))
            yt = p.op("vector", lambda v, usl=usl, h=h, qt=qt: v.tensor_tensor(
                y_dst(h, qt), usl[0:64, :], scb[0:64, 0:TT], ALU.mult),
                waits=(bt,))
            ufree[h] = (ev,)
            y_out_toks.append(yt)
    return y_out_toks


def emit_mha(ctx, kind, upto="full"):
    p, b, cin, dram = ctx.p, ctx.b, ctx.cin, ctx.dram
    pfx = {"l0a": "w0a", "l1a": "w1a"}[kind]
    rope = (kind == "l1a")
    xn, t_xn = emit_ln(ctx, 0, xb_off=26624)
    if upto == "ln":
        return t_xn
    bqkv = ctx.b["bias"][:, 0:32]
    bc_chan = ctx.name("bias")
    p.dma("sync", bqkv[:, 0:8], dram[pfx + "_bq"], chan=bc_chan)
    p.dma("sync", bqkv[:, 8:16], dram[pfx + "_bk"], chan=bc_chan)
    p.dma("sync", bqkv[:, 16:24], dram[pfx + "_bv"], chan=bc_chan)
    p.dma("sync", bqkv[:, 24:32], dram[pfx + "_bo"], chan=bc_chan)
    tb = p.dfence(bc_chan)

    QOF, KOF, VOF, TRIG, YOF = 16384, 18432, 20480, 22528, 26624
    qP = b["arb"][:, QOF:QOF + T]
    kP = b["arb"][:, KOF:KOF + T]
    vP = b["arb"][:, VOF:VOF + T]
    YB = b["arb"][:, YOF + 0:YOF + NC * T]
    ws = get_ws(ctx)
    t_ones = p.op("vector", lambda v: v.memset(
        b["vt"][:].rearrange("p (k c) -> p k c", c=130)[:, :, 64:65], 1.0))
    t_ones2 = p.op("vector", lambda v: v.memset(
        b["vt"][:].rearrange("p (k c) -> p k c", c=130)[:, :, 129:130], 1.0))

    cos = b["arb"][:, TRIG:TRIG + T]
    sin = b["arb"][:, TRIG + T:TRIG + 2 * T]
    prev_attn = (t_xn, tb)
    roperm = cin.get("c_roperm")
    if rope:
        if roperm is None:
            roperm = b["arb"][:, 59712:59840]
            p.dma("sync", roperm, dram["c_roperm"], chan="trig")
        p.dma("sync", cos, dram["c_ropecos"], chan="trig")
        p.dma("sync", sin, dram["c_ropesin"], chan="trig")
        prev_attn = prev_attn + (p.dfence("trig"),)
    for pair in range(NC):
        toks = {"q": [], "k": [], "v": []}

        def mk_cb(dst, which, pair=pair):
            def cb(oc, tt, ps, mmtok):
                bc = bqkv[:, {"q": 0, "k": 8, "v": 16}[which] + pair:
                          {"q": 0, "k": 8, "v": 16}[which] + pair + 1]
                tok = p.op("scalar", lambda s, tt=tt, ps=ps: s.activation(
                    dst[:, tt * TT:tt * TT + TT], ps[0:128, 0:TT], AF.Identity,
                    bias=bc), waits=(mmtok,))
                toks[which].append(tok)
                return tok
            return cb
        for which, dst, wname in (("q", qP, "_wq"), ("k", kP, "_wk"),
                                  ("v", vP, "_wv")):
            emit_linear(ctx, ws, lambda dc, tt: xn[:, dc * T + tt * TT:
                                                   dc * T + tt * TT + TT],
                        dram[pfx + wname], D, 128, mk_cb(dst, which),
                        first_waits=prev_attn, col_off=pair * 128,
                        bank_pair=("ps3", "ps4"),
                        conv_engines=("vector", "scalar"))
            prev_attn = ()
        if rope:
            rope_prev = ()
            for which, buf in (("q", qP), ("k", kP)):
                for tt in range(NTT):
                    sl = slice(tt * TT, tt * TT + TT)
                    wt = (tuple(toks[which]) if tt == 0 else ()) + rope_prev
                    m1 = p.op("tensor", lambda t, sl=sl, buf=buf: t.matmul(
                        b["ps5"][:, 0:TT], roperm, buf[:, sl], start=True,
                        stop=True), waits=wt)
                    s1 = p.op("vector", lambda v, sl=sl, buf=buf: v.tensor_tensor(
                        b["arf"][:, 0:TT], b["ps5"][:, 0:TT], sin[:, sl],
                        ALU.mult), waits=(m1,))
                    rope_prev = (s1,)
                    s2 = p.op("vector", lambda v, sl=sl, buf=buf: v.tensor_tensor(
                        b["arf"][:, TT:2 * TT], buf[:, sl], cos[:, sl],
                        ALU.mult))
                    s3 = p.op("vector", lambda v, sl=sl, buf=buf: v.tensor_tensor(
                        buf[:, sl], b["arf"][:, 0:TT], b["arf"][:, TT:2 * TT],
                        ALU.add))
                    toks[which].append(s3)
        if upto == "qkv0":
            return toks["v"][-1]
        t_vt = emit_vtok(ctx, vP, first_waits=tuple(toks["v"]))
        if upto == "vtok0":
            return t_vt
        pre = tuple(toks["q"]) + tuple(toks["k"]) + (t_vt, t_ones, t_ones2)
        yt = emit_attn_pair(
            ctx, pair,
            q_lhs=lambda h, qt: qP[h * 64:h * 64 + 64, qt * TT:qt * TT + TT],
            k_pair=lambda h, kb: kP[h * 64:h * 64 + 64, kb * 128:kb * 128 + 128],
            y_dst=lambda h, qt, pair=pair: YB[
                h * 64:h * 64 + 64,
                pair * T + qt * TT:pair * T + qt * TT + TT],
            causal=False, scale=0.125, first_waits=pre)
        prev_attn = tuple(yt[-2:])
        if upto == "pair0":
            return yt[-1]
    if upto == "pairs":
        return prev_attn[-1]
    def ocb(oc, tt, ps, mmtok):
        bc = bqkv[:, 24 + oc:24 + oc + 1]
        sl = slice(oc * T + tt * TT, oc * T + tt * TT + TT)
        return p.op("vector", lambda v, sl=sl, ps=ps, bc=bc:
                    v.scalar_tensor_tensor(b["xres"][:, sl], ps[0:128, 0:TT],
                                           bc, b["xres"][:, sl], ALU.add,
                                           ALU.add), waits=(mmtok,))
    emit_linear(ctx, ws, lambda dc, tt: YB[:, dc * T + tt * TT:
                                           dc * T + tt * TT + TT],
                dram[pfx + "_wo"], D, D, ocb, first_waits=prev_attn,
                bank_pair=("ps3", "ps4"), conv_engines=("scalar", "gpsimd"))
    return p.fence("vector")




AOF = 16384


def emit_moe_gate(ctx, xn, t_xn):
    p, b, cin, dram = ctx.p, ctx.b, ctx.cin, ctx.dram
    arf = b["arf"]
    gl = arf[:, 512:640]
    CW = arf[:, 640:768]
    m1 = arf[:, 768:769]
    m2 = arf[:, 769:770]
    d12 = arf[:, 770:771]
    w1 = arf[:, 771:772]
    w2 = arf[:, 772:773]
    eq1 = arf[:, 776:784]
    tmp = arf[:, 784:792]
    eq2 = arf[:, 792:800]
    ps5 = b["ps5"]
    gw = b["wbf"][:, 0:64]
    gwf = b["wf32"][:, 0:64]
    ch = ctx.name("gw")
    dt0 = p.dma("sync", gwf.rearrange("p (c e) -> p c e", e=8),
                dram["moe_gate_w"].rearrange("(c p) e -> p c e", p=128),
                chan=ch)
    tcv = p.op("scalar", lambda s: s.copy(gw, gwf), waits=(dt0,))
    cw_tok = None
    prev_ev = ()
    for tb in range(NKB):
        mmt = None
        for dc in range(NC):
            def mm(t, tb=tb, dc=dc):
                return t.matmul(ps5[0:128, 0:8],
                                xn[:, dc * T + tb * 128:dc * T + (tb + 1) * 128],
                                gw[:, dc * 8:dc * 8 + 8],
                                start=(dc == 0), stop=(dc == NC - 1))
            w = [tcv, t_xn] if dc == 0 else []
            if dc == 0:
                w += list(prev_ev)
            mmt = p.op("tensor", mm, waits=tuple(w), track=(dc == NC - 1))
        ev = p.op("scalar", lambda s, tb=tb: s.copy(gl[:, tb * 8:(tb + 1) * 8],
                                                    ps5[0:128, 0:8]),
                  waits=(mmt,))
        prev_ev = (ev,)
        glt = gl[:, tb * 8:(tb + 1) * 8]
        t1 = p.op("vector", lambda v, glt=glt: v.tensor_reduce(
            m1, glt, AX.X, ALU.max), waits=(ev,))
        t2 = p.op("vector", lambda v, glt=glt: v.tensor_scalar(
            eq1, glt, m1[:, 0:1], None, ALU.is_ge))
        t3 = p.op("vector", lambda v, glt=glt: v.scalar_tensor_tensor(
            tmp, eq1, -1e30, glt, ALU.mult, ALU.add))
        t4 = p.op("vector", lambda v: v.tensor_reduce(m2, tmp, AX.X, ALU.max))
        t5 = p.op("vector", lambda v, glt=glt: v.tensor_scalar(
            eq2, tmp, m2[:, 0:1], None, ALU.is_ge))
        t6 = p.op("vector", lambda v: v.tensor_tensor(d12, m1, m2, ALU.subtract))
        t7 = p.op("scalar", lambda s: s.activation(w1, d12, AF.Sigmoid),
                  waits=(t6,))
        t8 = p.op("scalar", lambda s: s.activation(w2, d12, AF.Sigmoid,
                                                   scale=-1.0))
        t9 = p.op("vector", lambda v: v.tensor_scalar(
            eq1, eq1, w1[:, 0:1], None, ALU.mult), waits=(t8,))
        cw_tok = p.op("vector", lambda v, tb=tb: v.scalar_tensor_tensor(
            CW[:, tb * 8:(tb + 1) * 8], eq2, w2[:, 0:1], eq1, ALU.mult,
            ALU.add))
    return CW, cw_tok


def emit_cw_row(ctx, CW, e, cw_tok, prev=()):
    p, b, cin = ctx.p, ctx.b, ctx.cin
    rows = b["rows"]
    ps6 = b["ps6"]
    identf = cin["c_ident_f"]
    toks = []
    prev_t = prev
    for tb in range(NKB):
        t1 = p.op("tensor", lambda t, tb=tb: t.transpose(
            ps6[0:1, 0:128], CW[:, tb * 8 + e:tb * 8 + e + 1], identf),
            waits=(cw_tok,) + tuple(prev_t))
        e1 = p.op("scalar", lambda s, tb=tb: s.copy(
            rows[0:1, tb * 128:(tb + 1) * 128], ps6[0:1, 0:128]), waits=(t1,))
        prev_t = (e1,)
        toks.append(e1)
    return toks[-1]


def emit_moe_mlp(ctx, xn, wgu_dram, wd_dram, ws, first_waits, cw_row_tok,
                 weighted):
    p, b, cin = ctx.p, ctx.b, ctx.cin
    arb, arf = b["arb"], b["arf"]
    wbf = b["wbf"]
    wf32 = b["wf32"]
    sg = arf[:, 0:512]
    cwb = arf[:, 0:512]
    II = 22
    bankA, bankB = b["ps0"], b["ps1"]
    a_toks = []
    freeA = tuple(first_waits)
    freeB = ()
    sg_free = ()
    for ocp in range(II):
        par = ocp % 2
        conv = {}
        for half, coff in ((0, ocp * 128), (1, (II + ocp) * 128)):
            for dc in range(NC):
                slot = ws.wcnt % 2
                ws.wcnt += 1
                dst32 = wf32[:, slot * WG:slot * WG + 128]
                dtok = p.dma("sync", dst32,
                             wgu_dram[dc * 128:(dc + 1) * 128, coff:coff + 128],
                             waits=ws.wslot_users[slot], chan=f"wslot{slot}")
                eng = ("scalar", "gpsimd")[ws.ce_i % 2]
                ws.ce_i += 1
                dstb = wbf[:, (par * 16 + half * 8 + dc) * 128:
                           (par * 16 + half * 8 + dc) * 128 + 128]

                def cv(en, dstb=dstb, dst32=dst32):
                    if hasattr(en, "tensor_copy"):
                        return en.tensor_copy(dstb, dst32)
                    return en.copy(dstb, dst32)
                ctok = p.op(eng, cv, waits=(dtok,) + tuple(ws.mm_last[par]))
                ws.wslot_users[slot] = (ctok,)
                conv[(half, dc)] = ctok
        last_mm = None
        for tt in range(NTT):
            mg = None
            for dc in range(NC):
                lhsT = wbf[:, (par * 16 + dc) * 128:(par * 16 + dc) * 128 + 128]

                def mm(t, lhsT=lhsT, dc=dc, tt=tt):
                    return t.matmul(bankA[0:128, 0:TT], lhsT,
                                    xn[:, dc * T + tt * TT:dc * T + tt * TT + TT],
                                    start=(dc == 0), stop=(dc == NC - 1))
                w = [conv[(0, dc)]]
                if dc == 0:
                    w += list(freeA)
                mg = p.op("tensor", mm, waits=tuple(w), track=(dc == NC - 1))
            tsg = p.op("scalar", lambda s: s.activation(
                sg, bankA[0:128, 0:TT], AF.Silu), waits=(mg,) + sg_free)
            freeA = (tsg,)
            mu = None
            for dc in range(NC):
                lhsT = wbf[:, (par * 16 + 8 + dc) * 128:
                           (par * 16 + 8 + dc) * 128 + 128]

                def mm(t, lhsT=lhsT, dc=dc, tt=tt):
                    return t.matmul(bankB[0:128, 0:TT], lhsT,
                                    xn[:, dc * T + tt * TT:dc * T + tt * TT + TT],
                                    start=(dc == 0), stop=(dc == NC - 1))
                w = [conv[(1, dc)]]
                if dc == 0:
                    w += list(freeB)
                mu = p.op("tensor", mm, waits=tuple(w), track=(dc == NC - 1))
            asl = arb[:, AOF + ocp * T + tt * TT:AOF + ocp * T + tt * TT + TT]
            ta = p.op("vector", lambda v, asl=asl: v.tensor_tensor(
                asl, sg, bankB[0:128, 0:TT], ALU.mult), waits=(tsg, mu))
            freeB = (ta,)
            sg_free = (ta,)
            a_toks.append(ta)
            last_mm = mu
        ws.mm_last[par] = (last_mm,)

    rows = b["rows"]
    ps5 = b["ps5"]
    onesrow = cin["c_ones_row_f"]
    cw_state = {"tt": -1, "tok": (), "cwb_users": ()}

    def ocb(oc, tt, ps, mmtok):
        if not weighted:
            return p.op("vector", lambda v, oc=oc, tt=tt, ps=ps: v.tensor_tensor(
                b["xres"][:, oc * T + tt * TT:oc * T + tt * TT + TT],
                b["xres"][:, oc * T + tt * TT:oc * T + tt * TT + TT],
                ps[0:128, 0:TT], ALU.add), waits=(mmtok,))
        bt = p.op("tensor", lambda t, tt=tt: t.matmul(
            ps5[0:128, 0:TT], onesrow[0:1, :],
            rows[0:1, tt * TT:tt * TT + TT], start=True, stop=True),
            waits=(cw_row_tok,) + cw_state["tok"])
        ev = p.op("scalar", lambda s: s.copy(cwb, ps5[0:128, 0:TT]),
                  waits=(bt,) + cw_state["cwb_users"])
        cw_state["tok"] = (ev,)
        t1 = p.op("vector", lambda v, ps=ps: v.tensor_tensor(
            ps[0:128, 0:TT], ps[0:128, 0:TT], cwb, ALU.mult),
            waits=(mmtok, ev))
        t2 = p.op("vector", lambda v, oc=oc, tt=tt, ps=ps: v.tensor_tensor(
            b["xres"][:, oc * T + tt * TT:oc * T + tt * TT + TT],
            b["xres"][:, oc * T + tt * TT:oc * T + tt * TT + TT],
            ps[0:128, 0:TT], ALU.add))
        cw_state["cwb_users"] = (t1,)
        return t2

    emit_linear(ctx, ws, lambda dc, tt: arb[:, AOF + dc * T + tt * TT:
                                            AOF + dc * T + tt * TT + TT],
                wd_dram, 2816, D, ocb, first_waits=tuple(a_toks[-1:]),
                bank_pair=("ps3", "ps4"), conv_engines=("scalar", "gpsimd"),
                G=128, nbuf=1)
    return p.fence("vector")


def emit_moe(ctx, upto="full"):
    p, b, dram = ctx.p, ctx.b, ctx.dram
    xn, t_xn = emit_ln(ctx, 0, xb_off=AOF)
    if upto == "ln":
        return t_xn
    CW, cw_tok = emit_moe_gate(ctx, xn, t_xn)
    if upto == "gate":
        return cw_tok
    ws = get_ws(ctx)
    seed = (cw_tok, p.fence("tensor"))
    ws.wslot_users = [seed, seed]
    ws.mm_last = [seed, seed]
    tok = emit_moe_mlp(ctx, xn, dram["moe_base_wgu"], dram["moe_base_wd"], ws,
                       (t_xn,), None, weighted=False)
    if upto == "base":
        return tok
    prev_cw = ()
    for e in range(8):
        crt = emit_cw_row(ctx, CW, e, cw_tok, prev=prev_cw)
        prev_cw = (crt,)
        wgu_e = dram["moe_exp_wgu"].rearrange("(e h) i -> e h i", e=8)[e]
        wd_e = dram["moe_exp_wd"].rearrange("(e i) h -> e i h", e=8)[e]
        tok = emit_moe_mlp(ctx, xn, wgu_e, wd_e, ws, (tok,), crt,
                           weighted=True)
        prev_cw = (tok, p.fence("tensor"))
    return tok


def build_sub_moe(upto="full"):
    nc = bass.Bass("TRN2", target_bir_lowering=False)
    p = Prog(nc)
    ctx = Ctx(nc, p)
    xin = nc.dram_tensor("xin", [D, T], F32, kind="ExternalInput").ap()
    xout = nc.dram_tensor("xout", [D, T], F32, kind="ExternalOutput").ap()
    ctx.dram["moe_gate_w"] = nc.dram_tensor("moe_gate_w", [D, 8], F32,
                                            kind="ExternalInput").ap()
    ctx.dram["moe_base_wgu"] = nc.dram_tensor("moe_base_wgu", [D, 5632], F32,
                                              kind="ExternalInput").ap()
    ctx.dram["moe_base_wd"] = nc.dram_tensor("moe_base_wd", [2816, D], F32,
                                             kind="ExternalInput").ap()
    ctx.dram["moe_exp_wgu"] = nc.dram_tensor("moe_exp_wgu", [8 * D, 5632], F32,
                                             kind="ExternalInput").ap()
    ctx.dram["moe_exp_wd"] = nc.dram_tensor("moe_exp_wd", [8 * 2816, D], F32,
                                            kind="ExternalInput").ap()
    cnames = ["c_ident", "c_ident_f", "c_ones_row", "c_ones_row_f",
              "c_ones_col", "c_scal"]
    for n in cnames:
        shape, dt = CONST_SPECS[n]
        ctx.dram[n] = nc.dram_tensor(n, shape, dt, kind="ExternalInput").ap()
    alloc_shared(ctx)
    load_consts(ctx, cnames)
    xres = ctx.b["xres"]
    tlds = [p.dma("gpsimd", xres[:, c * T:(c + 1) * T],
                  xin[c * 128:(c + 1) * 128, :], chan="xio") for c in range(NC)]
    startw = (p.dfence("xio"), p.dfence("misc"))
    for e in ("vector", "tensor", "scalar", "gpsimd"):
        p.op(e, None, waits=startw)
    tdone = emit_moe(ctx, upto)
    for c in range(NC):
        p.dma("gpsimd", xout[c * 128:(c + 1) * 128, :],
              xres[:, c * T:(c + 1) * T],
              waits=(tdone, p.fence("scalar"), p.fence("tensor"),
                     p.fence("vector")) if c == 0 else (), chan="xio")
    p.op("gpsimd", None, waits=(p.dfence("xio"),))
    return p.build()




def emit_quantize(ctx, w_dram, wq_dram, n_in, n_out, gslot, first_waits=()):
    p, b, cin = ctx.p, ctx.b, ctx.cin
    wf32 = b["wf32"]
    wbf = b["wbf"]
    arf = b["arf"]
    acc = arf[:, 768 + gslot:769 + gslot]
    gcol = arf[:, 772 + gslot:773 + gslot]
    ps6 = b["ps6"]
    onescol = cin["c_ones_col"]
    onesrow = cin["c_ones_row_f"]
    onescolf = cin["c_ones_col_f"]
    ncol = n_out // WG
    ntile = (n_in // 128) * ncol
    ch = ctx.name("qz")
    t_acc = p.op("vector", lambda v: v.memset(acc, 0.0), waits=first_waits)
    slot_free = [tuple(first_waits), tuple(first_waits)]
    red = arf[:, 774 + gslot:775 + gslot]
    for i in range(ntile):
        a, c0 = i // ncol, (i % ncol) * WG
        slot = i % 2
        dst = wf32[:, slot * WG:slot * WG + WG]
        dtok = p.dma("sync", dst,
                     w_dram[a * 128:(a + 1) * 128, c0:c0 + WG],
                     waits=slot_free[slot], chan=ch + str(slot))
        t1 = p.op("vector", lambda v, dst=dst: v.tensor_reduce(
            red, dst, AX.X, ALU.add, apply_absolute_value=True),
            waits=(dtok, t_acc))
        t_acc = p.op("vector", lambda v: v.tensor_tensor(acc, acc, red,
                                                         ALU.add))
        slot_free[slot] = (t_acc,)
    tmm = p.op("tensor", lambda t: t.matmul(ps6[0:1, 0:1], onescolf[:, 0:1],
                                            acc, start=True, stop=True),
               waits=(t_acc,))
    gl = b["rows"][0:1, 4:5]
    t2 = p.op("scalar", lambda s: s.activation(
        gl, ps6[0:1, 0:1], AF.Identity, scale=1.0 / (n_in * n_out),
        bias=cin["c_scal"][0:1, 0:1]), waits=(tmm,))
    t3 = p.op("vector", lambda v: v.reciprocal(gl, gl), waits=(t2,))
    t4 = p.op("tensor", lambda t: t.matmul(ps6[0:128, 1:2], onesrow[0:1, :],
                                           gl, start=True, stop=True),
               waits=(t3,))
    t5 = p.op("scalar", lambda s: s.copy(gcol, ps6[0:128, 1:2]), waits=(t4,))
    sgn = arf[:, 0:WG]
    ab = arf[:, 512:512 + WG]
    prev_q = ()
    slot_free = [(t5,), (t5,)]
    wq_free = [tuple(first_waits), tuple(first_waits)]
    for i in range(ntile):
        a, c0 = i // ncol, (i % ncol) * WG
        slot = i % 2
        dst = wf32[:, slot * WG:slot * WG + WG]
        dtok = p.dma("sync", dst,
                     w_dram[a * 128:(a + 1) * 128, c0:c0 + WG],
                     waits=slot_free[slot], chan=ch + str(slot))
        t6 = p.op("vector", lambda v, dst=dst: v.tensor_scalar(
            dst, dst, gcol, None, ALU.mult), waits=(dtok,) + prev_q)
        t7 = p.op("vector", lambda v, dst=dst: v.tensor_scalar(
            dst, dst, 1.0, -1.0, ALU.min, ALU.max))
        t8 = p.op("scalar", lambda s, dst=dst: s.activation(sgn, dst, AF.Sign),
                  waits=(t7,))
        t9 = p.op("scalar", lambda s, dst=dst: s.activation(ab, dst,
                                                            AF.Square),
                  waits=(t7,))
        t10 = p.op("vector", lambda v: v.tensor_scalar(
            ab, ab, 0.25, None, ALU.is_ge), waits=(t9,))
        qsl = wbf[:, slot * WG:slot * WG + WG]
        t11 = p.op("vector", lambda v, qsl=qsl: v.tensor_tensor(
            qsl, ab, sgn, ALU.mult), waits=(t8,) + wq_free[slot])
        prev_q = (t11,)
        stok = p.dma("gpsimd", wq_dram[a * 128:(a + 1) * 128, c0:c0 + WG],
                     qsl, waits=(t11,), chan=ch + "s" + str(slot))
        slot_free[slot] = (t11,)
        wq_free[slot] = (stok,)
    return p.dfence(ch + "s0"), p.dfence(ch + "s1")


def emit_bitffn(ctx, upto="full"):
    p, b, cin, dram = ctx.p, ctx.b, ctx.cin, ctx.dram
    arb = b["arb"]
    xn, t_xn = emit_ln(ctx, 0, xb_off=AOF)
    bcol = ctx.b["bias"]
    bch = ctx.name("bias")
    p.dma("sync", bcol[:, 0:32], dram["bit_b1"], chan=bch)
    p.dma("sync", bcol[:, 32:40], dram["bit_b2"], chan=bch)
    tb = p.dfence(bch)
    if "bit_quant_toks" not in ctx.b:
        q1 = emit_quantize(ctx, ctx.dram["bit_w1"], ctx.dram["bit_w1q"],
                           D, 4096, 0,
                           first_waits=(t_xn, ctx.p.fence("vector"),
                                        ctx.p.fence("scalar")))
        q2 = emit_quantize(ctx, ctx.dram["bit_w2"], ctx.dram["bit_w2q"],
                           4096, D, 1,
                           first_waits=(ctx.p.fence("vector"),
                                        ctx.p.fence("scalar")) + q1)
        ctx.b["bit_quant_toks"] = (q1, q2)
    qt1, qt2 = ctx.b["bit_quant_toks"]
    ws = get_ws(ctx)
    seed = tuple(qt1) + tuple(qt2)
    ws.mm_last = [seed + tuple(ws.mm_last[0]), seed + tuple(ws.mm_last[1])]
    ws.wslot_users = [seed + tuple(ws.wslot_users[0]),
                      seed + tuple(ws.wslot_users[1])]
    HOF = AOF
    TH = T // 2
    tok = (t_xn, tb)
    for th in range(2):
        hbuf = arb[:, HOF:HOF + 32 * TH]

        def hcb(oc, tt, ps, mmtok, th=th, hbuf=hbuf):
            sl = slice(oc * TH + (tt - th * 2) * TT, oc * TH + (tt - th * 2) * TT + TT)
            return p.op("scalar", lambda s, sl=sl, ps=ps, oc=oc: s.activation(
                hbuf[:, sl], ps[0:128, 0:TT], AF.Relu,
                bias=bcol[:, oc:oc + 1]), waits=(mmtok,))

        emit_linear(ctx, ws, lambda dc, tt, th=th: xn[
            :, dc * T + th * TH + (tt - th * 2) * TT:
            dc * T + th * TH + (tt - th * 2) * TT + TT],
            ctx.dram["bit_w1q"], D, 4096, hcb,
            first_waits=tok + tuple(qt1), bank_pair=("ps3", "ps4"),
            conv_engines=(), bf16_direct=True, G=WG, nbuf=2,
            tt_list=(th * 2, th * 2 + 1))

        def ocb(oc, tt, ps, mmtok, th=th):
            sl = slice(oc * T + th * TH + (tt - th * 2) * TT,
                       oc * T + th * TH + (tt - th * 2) * TT + TT)
            return p.op("vector", lambda v, sl=sl, ps=ps, oc=oc:
                        v.scalar_tensor_tensor(
                            b["xres"][:, sl], ps[0:128, 0:TT],
                            bcol[:, 32 + oc:33 + oc], b["xres"][:, sl],
                            ALU.add, ALU.add), waits=(mmtok,))

        emit_linear(ctx, ws, lambda dc, tt, th=th, hbuf=hbuf: hbuf[
            :, dc * TH + (tt - th * 2) * TT:dc * TH + (tt - th * 2) * TT + TT],
            ctx.dram["bit_w2q"], 4096, D, ocb,
            first_waits=tuple(qt2) + (p.fence("scalar"),),
            bank_pair=("ps3", "ps4"),
            conv_engines=(), bf16_direct=True, G=128, nbuf=1,
            tt_list=(th * 2, th * 2 + 1))
        tok = (p.fence("vector"),)
    return p.fence("vector")


def build_sub_bitffn(upto="full"):
    nc = bass.Bass("TRN2", target_bir_lowering=False)
    p = Prog(nc)
    ctx = Ctx(nc, p)
    xin = nc.dram_tensor("xin", [D, T], F32, kind="ExternalInput").ap()
    xout = nc.dram_tensor("xout", [D, T], F32, kind="ExternalOutput").ap()
    ctx.dram["bit_w1"] = nc.dram_tensor("bit_w1", [D, 4096], F32,
                                        kind="ExternalInput").ap()
    ctx.dram["bit_w2"] = nc.dram_tensor("bit_w2", [4096, D], F32,
                                        kind="ExternalInput").ap()
    ctx.dram["bit_b1"] = nc.dram_tensor("bit_b1", [128, 32], F32,
                                        kind="ExternalInput").ap()
    ctx.dram["bit_b2"] = nc.dram_tensor("bit_b2", [128, 8], F32,
                                        kind="ExternalInput").ap()
    ctx.dram["bit_w1q"] = nc.dram_tensor("bit_w1q", [D, 4096], BF16,
                                         kind="Internal").ap()
    ctx.dram["bit_w2q"] = nc.dram_tensor("bit_w2q", [4096, D], BF16,
                                         kind="Internal").ap()
    cnames = ["c_ident", "c_ones_row", "c_ones_row_f", "c_ones_col",
              "c_ones_col_f", "c_scal"]
    for n in cnames:
        shape, dt = CONST_SPECS[n]
        ctx.dram[n] = nc.dram_tensor(n, shape, dt, kind="ExternalInput").ap()
    alloc_shared(ctx)
    load_consts(ctx, cnames)
    xres = ctx.b["xres"]
    for c in range(NC):
        p.dma("gpsimd", xres[:, c * T:(c + 1) * T],
              xin[c * 128:(c + 1) * 128, :], chan="xio")
    startw = (p.dfence("xio"), p.dfence("misc"))
    for e in ("vector", "tensor", "scalar", "gpsimd"):
        p.op(e, None, waits=startw)
    q1 = emit_quantize(ctx, ctx.dram["bit_w1"], ctx.dram["bit_w1q"], D, 4096, 0)
    q2 = emit_quantize(ctx, ctx.dram["bit_w2"], ctx.dram["bit_w2q"], 4096, D, 1,
                       first_waits=(p.fence("vector"), p.fence("scalar")) + q1)
    ctx.b["bit_quant_toks"] = (q1, q2)
    tdone = emit_bitffn(ctx, upto)
    for c in range(NC):
        p.dma("gpsimd", xout[c * 128:(c + 1) * 128, :],
              xres[:, c * T:(c + 1) * T],
              waits=(tdone, p.fence("scalar"), p.fence("tensor"),
                     p.fence("vector")) if c == 0 else (), chan="xio")
    p.op("gpsimd", None, waits=(p.dfence("xio"),))
    return p.build()




QROF = 47136
QIOF = 49184
KROF = 51232
KIOF = 53280
CMOF = 55328


def emit_cosformer(ctx, upto="full"):
    p, b, cin, dram = ctx.p, ctx.b, ctx.cin, ctx.dram
    arb, arf = b["arb"], b["arf"]
    xn, t_xn = emit_ln(ctx, 0, xb_off=26624)
    QOF, KOF, VOF, TRIG = 16384, 18432, 20480, 22528
    qP = arb[:, QOF:QOF + T]
    kP = arb[:, KOF:KOF + T]
    vP = arb[:, VOF:VOF + T]
    YB = arb[:, 26624:26624 + NC * T]
    cos = arb[:, TRIG:TRIG + T]
    sin = arb[:, TRIG + T:TRIG + 2 * T]
    qre = arb[:, QROF:QROF + T]
    qim = arb[:, QIOF:QIOF + T]
    kre = arb[:, KROF:KROF + T]
    kim = arb[:, KIOF:KIOF + T]
    cmch = ctx.name("cm")
    ctx.cin["c_cmask"] = arb[:, CMOF:CMOF + 2048]
    p.dma("sync", ctx.cin["c_cmask"], dram["c_cmask"], chan=cmch,
          waits=(t_xn,))
    p.dma("sync", cos, dram["c_poscos_b"], chan="trig")
    p.dma("sync", sin, dram["c_possin_b"], chan="trig")
    t_trig = p.dfence("trig")
    t_cm = p.dfence(cmch)
    dcol = ctx.b["bias"]
    dch = ctx.name("delta")
    p.dma("sync", dcol[:, 0:8], dram["cb_delta"], chan=dch)
    t_d = p.dfence(dch)
    PI = math.pi
    t1 = p.op("vector", lambda v: v.tensor_scalar(
        dcol[:, 0:8], dcol[:, 0:8], 0.0, -2.0 * PI, ALU.min, ALU.max),
        waits=(t_d,))
    t2 = p.op("scalar", lambda s: s.activation(dcol[:, 24:32], dcol[:, 0:8],
                                               AF.Sin, scale=0.25), waits=(t1,))
    t3 = p.op("scalar", lambda s: s.activation(dcol[:, 32:40], dcol[:, 24:32],
                                               AF.Square), waits=(t2,))
    t4 = p.op("vector", lambda v: v.tensor_scalar(
        dcol[:, 32:40], dcol[:, 32:40], -1.0, None, ALU.mult), waits=(t3,))
    t5 = p.op("scalar", lambda s: s.activation(
        dcol[:, 32:40], dcol[:, 32:40], AF.Sqrt,
        bias=cin["c_scal"][:, 1:2]), waits=(t4,))
    t6 = p.op("vector", lambda v: v.tensor_tensor(
        dcol[:, 8:16], dcol[:, 24:32], dcol[:, 32:40], ALU.mult), waits=(t5,))
    t7 = p.op("vector", lambda v: v.tensor_scalar(
        dcol[:, 8:16], dcol[:, 8:16], 2.0, None, ALU.mult))
    t8 = p.op("scalar", lambda s: s.activation(
        dcol[:, 16:24], dcol[:, 24:32], AF.Square), waits=(t7,))
    t9 = p.op("vector", lambda v: v.tensor_scalar(
        dcol[:, 16:24], dcol[:, 16:24], -2.0, 1.0, ALU.mult, ALU.add),
        waits=(t8,))
    t10 = p.op("vector", lambda v: v.tensor_tensor(
        dcol[:, 24:32], dcol[:, 8:16], dcol[:, 16:24], ALU.mult),
        waits=(t9, t8))
    t11 = p.op("vector", lambda v: v.tensor_scalar(
        dcol[:, 24:32], dcol[:, 24:32], 2.0, None, ALU.mult))
    t12 = p.op("scalar", lambda s: s.activation(
        dcol[:, 32:40], dcol[:, 8:16], AF.Square), waits=(t11,))
    t13 = p.op("vector", lambda v: v.tensor_scalar(
        dcol[:, 32:40], dcol[:, 32:40], -2.0, 1.0, ALU.mult, ALU.add),
        waits=(t12,))
    t_dc = t13

    ws = get_ws(ctx)
    t_ones = p.op("vector", lambda v: v.memset(
        b["vt"].rearrange("p (k c) -> p k c", c=130)[:, :, 64:65], 1.0))
    t_ones2 = p.op("vector", lambda v: v.memset(
        b["vt"].rearrange("p (k c) -> p k c", c=130)[:, :, 129:130], 1.0))
    prev_attn = (t_xn, t_trig, t_dc)
    for pair in range(NC):
        toks = {"q": [], "k": [], "v": []}
        mu_toks = {"q": [], "k": []}

        def mk_cb(dst, which):
            def cb(oc, tt, ps, mmtok):
                if which == "v":
                    tok = p.op("scalar", lambda s, tt=tt, ps=ps: s.copy(
                        dst[:, tt * TT:tt * TT + TT], ps[0:128, 0:TT]),
                        waits=(mmtok,))
                    toks[which].append(tok)
                    return tok
                tA = arf[:, 0:512]
                tB = b["usb"]
                o1 = p.op("scalar", lambda s, ps=ps: s.activation(
                    tA, ps[0:128, 0:TT], AF.Abs), waits=(mmtok,))
                o2 = p.op("scalar", lambda s: s.activation(
                    tA, tA, AF.Exp, scale=-1.0))
                o3 = p.op("scalar", lambda s: s.activation(
                    tA, tA, AF.Ln, bias=cin["c_scal"][:, 1:2]))
                o4 = p.op("scalar", lambda s, ps=ps: s.activation(
                    tB, ps[0:128, 0:TT], AF.Relu))
                o5 = p.op("vector", lambda v, tt=tt: v.tensor_tensor(
                    dst[:, tt * TT:tt * TT + TT], tA, tB, ALU.add),
                    waits=(o3, o4))
                toks[which].append(o5)
                mu_toks[which].append(o5)
                return o5
            return cb
        for which, dst, coff in (("q", qP, 0), ("k", kP, D),
                                 ("v", vP, 2 * D)):
            emit_linear(ctx, ws, lambda dc, tt: xn[:, dc * T + tt * TT:
                                                   dc * T + tt * TT + TT],
                        dram["cb_w_attn"], D, 128, mk_cb(dst, which),
                        first_waits=prev_attn, col_off=coff + pair * 128,
                        bank_pair=("ps3", "ps4"),
                        conv_engines=("vector", "scalar"))
            prev_attn = ()
        rot = []
        for tt in range(NTT):
            sl = slice(tt * TT, tt * TT + TT)
            w0 = (mu_toks["q"][-1], mu_toks["k"][-1]) if tt == 0 else ()
            rot.append(p.op("vector", lambda v, sl=sl: v.tensor_tensor(
                qre[:, sl], qP[:, sl], cos[:, sl], ALU.mult), waits=w0))
            rot.append(p.op("vector", lambda v, sl=sl: v.tensor_tensor(
                qim[:, sl], qP[:, sl], sin[:, sl], ALU.mult)))
            cpd = arf[:, 0:512]
            spd = b["usb"]
            p.op("vector", lambda v, sl=sl, pair=pair: v.tensor_scalar(
                cpd, sin[:, sl], dcol[:, 24 + pair:25 + pair], None, ALU.mult))
            p.op("vector", lambda v, sl=sl, pair=pair: v.scalar_tensor_tensor(
                cpd, cos[:, sl], dcol[:, 32 + pair:33 + pair], cpd,
                ALU.mult, ALU.subtract))
            p.op("vector", lambda v, sl=sl, pair=pair: v.tensor_scalar(
                spd, cos[:, sl], dcol[:, 24 + pair:25 + pair], None, ALU.mult))
            p.op("vector", lambda v, sl=sl, pair=pair: v.scalar_tensor_tensor(
                spd, sin[:, sl], dcol[:, 32 + pair:33 + pair], spd,
                ALU.mult, ALU.add))
            rot.append(p.op("vector", lambda v, sl=sl: v.tensor_tensor(
                kre[:, sl], kP[:, sl], cpd, ALU.mult)))
            rot.append(p.op("vector", lambda v, sl=sl: v.tensor_tensor(
                kim[:, sl], kP[:, sl], spd, ALU.mult)))
        if upto == "rot0":
            return rot[-1]
        t_vt = emit_vtok(ctx, vP, first_waits=tuple(toks["v"]))
        pre = tuple(rot[-4:]) + (t_vt, t_ones, t_ones2, t_cm)

        def q_lhs2(h, qt, part):
            buf = qre if part == 0 else qim
            return buf[h * 64:h * 64 + 64, qt * TT:qt * TT + TT]

        def k_pair2(h, kb, part):
            buf = kre if part == 0 else kim
            return buf[h * 64:h * 64 + 64, kb * 128:kb * 128 + 128]

        yt = emit_attn_pair(
            ctx, pair,
            q_lhs=q_lhs2, k_pair=k_pair2,
            y_dst=lambda h, qt, pair=pair: YB[
                h * 64:h * 64 + 64,
                pair * T + qt * TT:pair * T + qt * TT + TT],
            causal=True, scale=0.125, first_waits=pre, two_part=True)
        prev_attn = tuple(yt[-2:])
    if upto == "pairs":
        return prev_attn[-1]

    def ocb(oc, tt, ps, mmtok):
        sl = slice(oc * T + tt * TT, oc * T + tt * TT + TT)
        return p.op("vector", lambda v, sl=sl, ps=ps: v.tensor_tensor(
            b["xres"][:, sl], b["xres"][:, sl], ps[0:128, 0:TT], ALU.add),
            waits=(mmtok,))
    emit_linear(ctx, ws, lambda dc, tt: YB[:, dc * T + tt * TT:
                                           dc * T + tt * TT + TT],
                dram["cb_w_proj"], D, D, ocb, first_waits=prev_attn,
                bank_pair=("ps3", "ps4"), conv_engines=("scalar", "gpsimd"))
    return p.fence("vector")


def build_sub_cosformer(upto="full"):
    nc = bass.Bass("TRN2", target_bir_lowering=False)
    p = Prog(nc)
    ctx = Ctx(nc, p)
    xin = nc.dram_tensor("xin", [D, T], F32, kind="ExternalInput").ap()
    xout = nc.dram_tensor("xout", [D, T], F32, kind="ExternalOutput").ap()
    ctx.dram["cb_w_attn"] = nc.dram_tensor("cb_w_attn", [D, 3 * D], F32,
                                           kind="ExternalInput").ap()
    ctx.dram["cb_w_proj"] = nc.dram_tensor("cb_w_proj", [D, D], F32,
                                           kind="ExternalInput").ap()
    ctx.dram["cb_delta"] = nc.dram_tensor("cb_delta", [128, 8], F32,
                                          kind="ExternalInput").ap()
    ctx.dram["c_poscos_b"] = nc.dram_tensor("c_poscos_b", [128, T], BF16,
                                            kind="ExternalInput").ap()
    ctx.dram["c_possin_b"] = nc.dram_tensor("c_possin_b", [128, T], BF16,
                                            kind="ExternalInput").ap()
    cnames = ["c_ident", "c_ones_row", "c_ones_row_f", "c_ones_col",
              "c_scal"]
    for n in cnames + ["c_cmask"]:
        shape, dt = CONST_SPECS[n]
        ctx.dram[n] = nc.dram_tensor(n, shape, dt, kind="ExternalInput").ap()
    alloc_shared(ctx)
    load_consts(ctx, cnames)
    ctx.cin["c_cmask"] = ctx.b["arb"][:, CMOF:CMOF + 2048]
    p.dma("sync", ctx.cin["c_cmask"], ctx.dram["c_cmask"], chan="misc")
    xres = ctx.b["xres"]
    for c in range(NC):
        p.dma("gpsimd", xres[:, c * T:(c + 1) * T],
              xin[c * 128:(c + 1) * 128, :], chan="xio")
    startw = (p.dfence("xio"), p.dfence("misc"))
    for e in ("vector", "tensor", "scalar", "gpsimd"):
        p.op(e, None, waits=startw)
    tdone = emit_cosformer(ctx, upto)
    for c in range(NC):
        p.dma("gpsimd", xout[c * 128:(c + 1) * 128, :],
              xres[:, c * T:(c + 1) * T],
              waits=(tdone, p.fence("scalar"), p.fence("tensor"),
                     p.fence("vector")) if c == 0 else (), chan="xio")
    p.op("gpsimd", None, waits=(p.dfence("xio"),))
    return p.build()




KW0, KW1 = 0, 8192
QW0, QW1 = 16384, 24576
VW0, VW1 = 32768, 34816
YSOF = 36864
SW = 16


def emit_hybrid(ctx, upto="full"):
    p, b, cin, dram = ctx.p, ctx.b, ctx.cin, ctx.dram
    nc = ctx.nc
    arb, arf = b["arb"], b["arf"]
    xres = b["xres"]
    xn, t_xn = emit_ln(ctx, 0, xb_off=26624)
    QOF, KOF, VOF = 16384, 18432, 20480
    qP = arb[:, QOF:QOF + T]
    kP = arb[:, KOF:KOF + T]
    vP = arb[:, VOF:VOF + T]
    YB = arb[:, 26624:26624 + NC * T]
    GOF, AOF2 = 47136, 55328
    gbuf = arb[:, GOF:GOF + 4 * T]
    abuf = arf[:, 300:556]
    bcol = arf[:, 0:44]
    b2row = arf[:, 44:60]
    bch = ctx.name("hbias")
    p.dma("sync", bcol[:, 0:24], dram["hy_bqkv"], chan=bch, waits=(t_xn,))
    p.dma("sync", bcol[:, 24:32], dram["hy_bo"], chan=bch)
    p.dma("sync", bcol[:, 32:36], dram["hy_bg1"], chan=bch)
    p.dma("sync", b2row, dram["hy_b2row"], chan=bch)
    tb_tok = p.dfence(bch)

    ws = get_ws(ctx)
    t_ones = p.op("vector", lambda v: v.memset(
        b["vt"].rearrange("p (k c) -> p k c", c=130)[:, :, 64:65], 1.0))
    t_ones2 = p.op("vector", lambda v: v.memset(
        b["vt"].rearrange("p (k c) -> p k c", c=130)[:, :, 129:130], 1.0))
    cmch = ctx.name("cm")
    ctx.cin["c_cmask"] = arb[:, 55584:55584 + 2048]
    p.dma("sync", ctx.cin["c_cmask"], dram["c_cmask"], chan=cmch,
          waits=(t_xn,))
    cmask = ctx.cin["c_cmask"]
    t_cm = p.dfence(cmch)

    qkv_dma = []
    prev_attn = (t_xn, tb_tok, t_cm)
    for pair in range(NC):
        toks = {"q": [], "k": [], "v": []}

        def mk_cb(dst, which, pair=pair):
            def cb(oc, tt, ps, mmtok):
                bc = bcol[:, {"q": 0, "k": 8, "v": 16}[which] + pair:
                          {"q": 0, "k": 8, "v": 16}[which] + pair + 1]
                tok = p.op("scalar", lambda s, tt=tt, ps=ps: s.activation(
                    dst[:, tt * TT:tt * TT + TT], ps[0:128, 0:TT], AF.Identity,
                    bias=bc), waits=(mmtok,))
                toks[which].append(tok)
                return tok
            return cb
        for which, dst, coff in (("q", qP, 0), ("k", kP, D), ("v", vP, 2 * D)):
            emit_linear(ctx, ws, lambda dc, tt: xn[:, dc * T + tt * TT:
                                                   dc * T + tt * TT + TT],
                        dram["hy_wqkv"], D, 128, mk_cb(dst, which),
                        first_waits=prev_attn, col_off=coff + pair * 128,
                        bank_pair=("ps3", "ps4"),
                        conv_engines=("vector", "scalar"))
            prev_attn = ()
            dname = {"q": "hy_qd", "k": "hy_kd", "v": "hy_vd"}[which]
            qkv_dma.append(p.dma(
                "gpsimd", dram[dname][pair * 128:(pair + 1) * 128, :], dst,
                waits=(toks[which][-1],), chan="qkvst_" + which))
        t_vt = emit_vtok(ctx, vP, first_waits=tuple(toks["v"]))
        pre = tuple(toks["q"]) + tuple(toks["k"]) + (t_vt, t_ones, t_ones2)
        yt = emit_attn_pair(
            ctx, pair,
            q_lhs=lambda h, qt: qP[h * 64:h * 64 + 64, qt * TT:qt * TT + TT],
            k_pair=lambda h, kb: kP[h * 64:h * 64 + 64, kb * 128:kb * 128 + 128],
            y_dst=lambda h, qt, pair=pair: YB[
                h * 64:h * 64 + 64,
                pair * T + qt * TT:pair * T + qt * TT + TT],
            causal=True, scale=0.125, first_waits=pre)
        prev_attn = tuple(yt[-2:])
        stage = b["ebuf"][:, 0:128]
        pst = b["pst"]
        ident = cin["c_ident"]
        sp_prev = prev_attn
        for which, buf, dnm in (("q", qP, "hy_qt"), ("k", kP, "hy_kt"),
                                ("v", vP, "hy_vt")):
            for tb in range(NKB):
                tr = p.op("tensor", lambda te, buf=buf, tb=tb: te.transpose(
                    pst[:, 128:256], buf[:, tb * 128:(tb + 1) * 128],
                    ident), waits=sp_prev)
                ev = p.op("scalar", lambda s: s.copy(stage, pst[:, 128:256]),
                          waits=(tr,))
                dmt = p.dma("gpsimd",
                            dram[dnm][tb * 128:(tb + 1) * 128,
                                      pair * 128:(pair + 1) * 128],
                            stage, waits=(ev,), chan="qkvtr")
                sp_prev = (dmt,)
        prev_attn = prev_attn + (sp_prev[-1],) + tuple(qkv_dma[-3:])
    t_qkvt = p.dfence("qkvtr")
    ysp = []
    for c in range(NC):
        ysp.append(p.dma("gpsimd", dram["hy_yd"][c * 128:(c + 1) * 128, :],
                         YB[:, c * T:(c + 1) * T],
                         waits=prev_attn if c == 0 else (), chan="ysp"))
    t_ysp = p.dfence("ysp")

    gtoks = []

    def gcb(oc, tt, ps, mmtok):
        tok = p.op("scalar", lambda s, oc=oc, tt=tt, ps=ps: s.activation(
            gbuf[:, oc * T + tt * TT:oc * T + tt * TT + TT], ps[0:128, 0:TT],
            AF.Gelu, bias=bcol[:, 32 + oc:33 + oc]), waits=(mmtok,))
        gtoks.append(tok)
        return tok
    emit_linear(ctx, ws, lambda dc, tt: xn[:, dc * T + tt * TT:
                                           dc * T + tt * TT + TT],
                dram["hy_gw1"], D, 512, gcb, first_waits=prev_attn,
                bank_pair=("ps3", "ps4"), conv_engines=("vector", "scalar"))
    gw2 = arb[:, 57632:57696]
    gw2f = arf[:, 64:128]
    gch = ctx.name("gw2")
    dt0 = p.dma("sync", gw2f.rearrange("p (c e) -> p c e", e=16),
                dram["hy_gw2"].rearrange("(c p) e -> p c e", p=128), chan=gch,
                waits=prev_attn)
    tcv = p.op("scalar", lambda s: s.copy(gw2, gw2f), waits=(dt0,))
    ps5 = b["ps5"]
    a_toks = []
    prev_ev = ()
    for tb in range(NKB):
        mmt = None
        for gc in range(4):
            def mm(t, tb=tb, gc=gc):
                return t.matmul(ps5[0:128, 0:16],
                                gbuf[:, gc * T + tb * 128:gc * T + (tb + 1) * 128],
                                gw2[:, gc * 16:gc * 16 + 16],
                                start=(gc == 0), stop=(gc == 3))
            w = [tcv, gtoks[-1]] if gc == 0 else []
            if gc == 0:
                w += list(prev_ev)
            mmt = p.op("tensor", mm, waits=tuple(w), track=(gc == 3))
        t1 = p.op("vector", lambda v, tb=tb: v.tensor_tensor(
            arf[:, 256:272], ps5[0:128, 0:16], b2row[:, 0:16], ALU.add),
            waits=(mmt, tb_tok))
        t2 = p.op("scalar", lambda s, tb=tb: s.activation(
            abuf[:, tb * 16:(tb + 1) * 16], arf[:, 256:272], AF.Sigmoid),
            waits=(t1,))
        prev_ev = (t2,)
        a_toks.append(t2)
    if upto == "gate":
        return a_toks[-1]

    xsp = []
    for c in range(NC):
        xsp.append(p.dma("gpsimd", dram["hy_xd"][c * 128:(c + 1) * 128, :],
                         xres[:, c * T:(c + 1) * T], chan="xsp"))
    t_xsp = p.dfence("xsp")
    Rt = xres[:, 0:512]
    Gs = xres[:, 512:1024]
    QG = xres[:, 1024:1536]
    Mb = xres[:, 1536:2048]
    lam = xres[:, 2048:2560]
    wst = xres[:, 2560:3072]
    sch = ctx.name("scan")
    d1 = p.dma("sync", lam, dram["hy_lam"], waits=(t_xsp,), chan=sch)
    d2 = p.dma("sync", Gs, dram["hy_gam"], chan=sch)
    d3 = p.dma("sync", wst, dram["hy_wst"], chan=sch)
    t_sc = p.dfence(sch)
    tR = p.op("scalar", lambda s: s.activation(Rt, lam, AF.Sigmoid,
                                               scale=-1.0), waits=(t_sc,))
    wstb = arb[:, 57696:58208]
    t_wstb = p.op("vector", lambda v: v.tensor_copy(wstb, wst), waits=(t_sc,))
    tG = p.op("vector", lambda v: v.tensor_scalar(Gs, Gs, 0.125, None,
                                                  ALU.mult), waits=(t_sc,))
    zts = []
    for off, ln in ((KW0, 8192), (KW1, 8192), (QW0, 8192), (QW1, 8192),
                    (VW0, 2048), (VW1, 2048)):
        zts.append(p.op("vector", lambda v, off=off, ln=ln: v.memset(
            arb[0:16, off:off + ln], 0.0), waits=(t_ysp,) if not zts else ()))
    t_z = zts[-1]
    ybuf = arb[:, YSOF:YSOF + T * 8]

    def refill(t0, buf_par, wait):
        toks = []
        kwo = (KW0, KW1)[buf_par]
        qwo = (QW0, QW1)[buf_par]
        vwo = (VW0, VW1)[buf_par]
        for nm, wo, q in (("hy_kt", kwo, "scalar"), ("hy_qt", qwo, "gpsimd")):
            dd = arb[0:16, wo:wo + SW * 512].rearrange(
                "p (dt f) -> p dt f", dt=SW)
            sv = dram[nm].rearrange("t (h j) -> h t j", h=16)
            for g in range(8):
                toks.append(p.dma(
                    q, dd[2 * g:2 * g + 2, :, g * 64:g * 64 + 64],
                    sv[2 * g:2 * g + 2, t0:t0 + SW, :],
                    waits=wait if g == 0 else (),
                    chan=f"rf_{nm}_{buf_par}"))
        dd = arb[0:16, vwo:vwo + SW * 128].rearrange(
            "p (dt f) -> p dt f", dt=SW)
        sv = dram["hy_vt"].rearrange("t (h i) -> h t i", h=16)
        toks.append(p.dma("gpsimd", dd[0:16:2, :, 0:64],
                          sv[0:16:2, t0:t0 + SW, :], waits=wait,
                          chan=f"rf_v_{buf_par}"))
        toks.append(p.dma("gpsimd", dd[1:16:2, :, 64:128],
                          sv[1:16:2, t0:t0 + SW, :],
                          chan=f"rf_v_{buf_par}"))
        return (p.dfence(f"rf_hy_kt_{buf_par}"), p.dfence(f"rf_hy_qt_{buf_par}"),
                p.dfence(f"rf_v_{buf_par}"))

    d16 = cin["c_delta16"]
    Zb = b["ps0"]
    Qb = b["ps1"]
    win_tok = [None, None]
    win_tok[0] = refill(0, 0, (t_z, t_qkvt))
    win_tok[1] = refill(SW, 1, (t_z,))
    dec_tok = None
    mmh_tok = None
    m_tok = ()
    qg_tok = ()
    mmq_tok = None
    y_toks = []
    win_free = [(), ()]
    for t in range(T):
        par = (t // SW) % 2
        kwo, qwo, vwo = (KW0, KW1)[par], (QW0, QW1)[par], (VW0, VW1)[par]
        dt = t % SW
        if dt == 0 and t > 0:
            opar = par ^ 1
            nxt = t + SW
            if nxt < T:
                win_tok[opar] = refill(nxt, opar, win_free[opar])
        if t > 0:
            dec_tok = p.op("vector", lambda v: v.tensor_tensor(
                Zb[0:128, 0:TT], Zb[0:128, 0:TT], Rt, ALU.mult),
                waits=(mmh_tok, tR) + m_tok)
        vsl = arb[0:16, vwo + dt * 128:vwo + dt * 128 + 128]
        ksl = arb[0:16, kwo + dt * 512:kwo + dt * 512 + 512]
        qsl = arb[0:16, qwo + dt * 512:qwo + dt * 512 + 512]
        w = [win_tok[par][0], win_tok[par][2]] if dt == 0 else []
        if t > 0:
            w.append(dec_tok)
        mmh_tok = p.op("tensor", lambda te, vsl=vsl, ksl=ksl, t=t: te.matmul(
            Zb[0:128, 0:TT], vsl, ksl, start=(t == 0), stop=(t == T - 1),
            skip_group_check=True), waits=tuple(w))
        w2 = [win_tok[par][1]] if dt == 0 else []
        w2 += list(qg_tok)
        mmq_tok = p.op("tensor", lambda te, qsl=qsl: te.matmul(
            Qb[0:128, 0:TT], d16, qsl, start=True, stop=True),
            waits=tuple(w2))
        qg_tok = (p.op("vector", lambda v: v.tensor_tensor(
            QG, Qb[0:128, 0:TT], Gs, ALU.mult), waits=(mmq_tok, tG)),)
        m_tok = (p.op("vector", lambda v: v.tensor_tensor(
            Mb, Zb[0:128, 0:TT], QG, ALU.mult), waits=(mmh_tok,) + qg_tok),)
        def _red(v, t=t):
            with ctx.nc.allow_low_precision(reason="bf16 y_stp store"):
                return v.tensor_reduce(
                    ybuf[:, t * 8:t * 8 + 8],
                    Mb.rearrange("p (hh j) -> p hh j", hh=8), AX.X, ALU.add)
        yv = p.op("vector", _red, waits=m_tok)
        y_toks.append(yv)
        if dt == SW - 1:
            win_free[par] = (mmh_tok, mmq_tok)
    t_scan = y_toks[-1]
    if upto == "scan":
        return t_scan

    psA, psB = b["ps2"], b["ps3"]
    wtoks = []
    prevA = ()
    prevB = ()
    for pair in range(NC):
        for tt in range(NTT):
            qsl = arb[:, 53248 + (tt % 2) * 2048:53248 + (tt % 2) * 2048 + TT]
            dq = p.dma("sync", qsl,
                       dram["hy_qd"][pair * 128:(pair + 1) * 128,
                                     tt * TT:tt * TT + TT],
                       waits=prevA + prevB + ((a_toks[-1],) if pair == 0
                                              and tt == 0 else ()),
                       chan=f"wq{tt % 2}")
            mA = p.op("tensor", lambda te, qsl=qsl, pair=pair: te.matmul(
                psA[0:64, 0:TT], wstb[0:64, pair * 64:pair * 64 + 64],
                qsl[0:64, :], start=True, stop=True),
                waits=(dq, t_wstb) + prevA)
            mB = p.op("tensor", lambda te, qsl=qsl, pair=pair: te.matmul(
                psB[0:64, 0:TT], wstb[64:128, pair * 64:pair * 64 + 64],
                qsl[64:128, :], start=True, stop=True),
                waits=(dq,) + prevB)
            yA = ybuf[0:64, :].rearrange("p (t hh) -> p t hh", hh=8)[
                :, tt * TT:tt * TT + TT, pair:pair + 1]
            yB = ybuf[64:128, :].rearrange("p (t hh) -> p t hh", hh=8)[
                :, tt * TT:tt * TT + TT, pair:pair + 1]
            wA = p.op("vector", lambda v, yA=yA: v.tensor_tensor(
                yA, yA, psA[0:64, 0:TT].rearrange("p (t o) -> p t o", o=1),
                ALU.add), waits=(mA, t_scan))
            wB = p.op("vector", lambda v, yB=yB: v.tensor_tensor(
                yB, yB, psB[0:64, 0:TT].rearrange("p (t o) -> p t o", o=1),
                ALU.add), waits=(mB,))
            prevA = (wA,)
            prevB = (wB,)
            wtoks += [wA, wB]
    t_w = wtoks[-1]
    if upto == "wstat":
        return t_w

    xrl = []
    for c in range(NC):
        xrl.append(p.dma("gpsimd", xres[:, c * T:(c + 1) * T],
                         dram["hy_xd"][c * 128:(c + 1) * 128, :],
                         waits=(t_w, p.fence("vector")) if c == 0 else (),
                         chan="xsp"))
    t_xrl = p.dfence("xsp")
    onesrow = cin["c_ones_row_f"]
    identf = cin["c_ident_f"]
    ps6 = b["ps6"]
    rows = b["rows"]
    YC = arb[:, 26624:26624 + NC * T]
    scb = b["ps5"]
    prev_cmb = ()
    for pair in range(NC):
        ysl = arb[:, 16384 + (pair % 2) * T:16384 + (pair % 2) * T + T]
        dy = p.dma("sync", ysl,
                   dram["hy_yd"][pair * 128:(pair + 1) * 128, :],
                   waits=(t_ysp,) + prev_cmb, chan=f"yrl{pair % 2}")
        for hl in (0, 1):
            h = 2 * pair + hl
            art = ()
            for tb in range(NKB):
                t1 = p.op("tensor", lambda te, tb=tb, h=h: te.transpose(
                    ps6[0:1, 0:128], abuf[:, tb * 16 + h:tb * 16 + h + 1],
                    identf), waits=(a_toks[-1],) + art)
                e1 = p.op("scalar", lambda s, tb=tb: s.copy(
                    rows[0:1, tb * 128:(tb + 1) * 128], ps6[0:1, 0:128]),
                    waits=(t1,))
                art = (e1,)
            for tt in range(NTT):
                bt = p.op("tensor", lambda te, tt=tt: te.matmul(
                    scb[0:64, 0:TT], onesrow[0:1, 0:64],
                    rows[0:1, tt * TT:tt * TT + TT], start=True, stop=True),
                    waits=art + prev_cmb)
                ysm = ysl[hl * 64:hl * 64 + 64, tt * TT:tt * TT + TT]
                yst = ybuf[hl * 64:hl * 64 + 64, :].rearrange(
                    "p (t hh) -> p t hh", hh=8)[:, tt * TT:tt * TT + TT,
                                                pair:pair + 1]
                dst = YC[hl * 64:hl * 64 + 64,
                         pair * T + tt * TT:pair * T + tt * TT + TT]
                c1 = p.op("vector", lambda v, ysm=ysm, yst=yst, dst=dst:
                          v.tensor_tensor(
                              dst, ysm,
                              yst.rearrange("p t o -> p (t o)"), ALU.subtract),
                          waits=(dy, bt))
                c2 = p.op("vector", lambda v, dst=dst: v.tensor_tensor(
                    dst, dst, scb[0:64, 0:TT], ALU.mult))
                c3 = p.op("vector", lambda v, yst=yst, dst=dst: v.tensor_tensor(
                    dst, dst, yst.rearrange("p t o -> p (t o)"), ALU.add))
                prev_cmb = (c3,)
    t_cmb = prev_cmb[0]
    if upto == "combine":
        return t_cmb

    def ocb(oc, tt, ps, mmtok):
        bc = bcol[:, 24 + oc:24 + oc + 1]
        sl = slice(oc * T + tt * TT, oc * T + tt * TT + TT)
        return p.op("vector", lambda v, sl=sl, ps=ps, bc=bc:
                    v.scalar_tensor_tensor(xres[:, sl], ps[0:128, 0:TT],
                                           bc, xres[:, sl], ALU.add, ALU.add),
                    waits=(mmtok,))
    emit_linear(ctx, ws, lambda dc, tt: YC[:, dc * T + tt * TT:
                                           dc * T + tt * TT + TT],
                dram["hy_wo"], D, D, ocb,
                first_waits=(t_cmb, t_xrl), bank_pair=("ps3", "ps4"),
                conv_engines=("scalar", "gpsimd"))
    return p.fence("vector")


def build_sub_hybrid(upto="full"):
    nc = bass.Bass("TRN2", target_bir_lowering=False)
    p = Prog(nc)
    ctx = Ctx(nc, p)
    xin = nc.dram_tensor("xin", [D, T], F32, kind="ExternalInput").ap()
    xout = nc.dram_tensor("xout", [D, T], F32, kind="ExternalOutput").ap()
    for nm, shape in (("hy_wqkv", [D, 3 * D]), ("hy_wo", [D, D]),
                      ("hy_gw1", [D, 512]), ("hy_gw2", [512, 16]),
                      ("hy_bqkv", [128, 24]), ("hy_bo", [128, 8]),
                      ("hy_bg1", [128, 4]), ("hy_b2row", [128, 16]),
                      ("hy_lam", [128, 512]), ("hy_gam", [128, 512]),
                      ("hy_wst", [128, 512])):
        ctx.dram[nm] = nc.dram_tensor(nm, shape, F32,
                                      kind="ExternalInput").ap()
    for nm, shape in (("hy_qd", [D, T]), ("hy_kd", [D, T]), ("hy_vd", [D, T]),
                      ("hy_yd", [D, T]), ("hy_qt", [T, D]), ("hy_kt", [T, D]),
                      ("hy_vt", [T, D])):
        ctx.dram[nm] = nc.dram_tensor(nm, shape, BF16, kind="Internal").ap()
    ctx.dram["hy_xd"] = nc.dram_tensor("hy_xd", [D, T], F32,
                                       kind="Internal").ap()
    cnames = ["c_ident", "c_ident_f", "c_ones_row", "c_ones_row_f",
              "c_ones_col", "c_scal", "c_delta16"]
    for n in cnames + ["c_cmask"]:
        shape, dt = CONST_SPECS[n]
        ctx.dram[n] = nc.dram_tensor(n, shape, dt, kind="ExternalInput").ap()
    alloc_shared(ctx)
    load_consts(ctx, cnames)
    ctx.cin["c_cmask"] = ctx.b["arb"][:, 55584:55584 + 2048]
    p.dma("sync", ctx.cin["c_cmask"], ctx.dram["c_cmask"], chan="misc")
    xres = ctx.b["xres"]
    for c in range(NC):
        p.dma("gpsimd", xres[:, c * T:(c + 1) * T],
              xin[c * 128:(c + 1) * 128, :], chan="xio")
    startw = (p.dfence("xio"), p.dfence("misc"))
    for e in ("vector", "tensor", "scalar", "gpsimd"):
        p.op(e, None, waits=startw)
    tdone = emit_hybrid(ctx, upto)
    for c in range(NC):
        p.dma("gpsimd", xout[c * 128:(c + 1) * 128, :],
              xres[:, c * T:(c + 1) * T],
              waits=(tdone, p.fence("scalar"), p.fence("tensor"),
                     p.fence("vector")) if c == 0 else (), chan="xio")
    p.op("gpsimd", None, waits=(p.dfence("xio"),))
    return p.build()


def build_sub_mha(kind, upto="full"):
    nc = bass.Bass("TRN2", target_bir_lowering=False)
    p = Prog(nc)
    ctx = Ctx(nc, p)
    pfx = {"l0a": "w0a", "l1a": "w1a"}[kind]
    xin = nc.dram_tensor("xin", [D, T], F32, kind="ExternalInput").ap()
    xout = nc.dram_tensor("xout", [D, T], F32, kind="ExternalOutput").ap()
    for wname in ("_wq", "_wk", "_wv", "_wo"):
        ctx.dram[pfx + wname] = nc.dram_tensor(pfx + wname, [D, D], F32,
                                               kind="ExternalInput").ap()
    for bname in ("_bq", "_bk", "_bv", "_bo"):
        ctx.dram[pfx + bname] = nc.dram_tensor(pfx + bname, [128, 8], F32,
                                               kind="ExternalInput").ap()
    cnames = ["c_ident", "c_ones_row", "c_ones_row_f", "c_ones_col", "c_scal"]
    lnames = list(cnames)
    if kind == "l1a":
        cnames += ["c_ropecos", "c_ropesin", "c_roperm"]
        lnames += ["c_roperm"]
    for n in cnames:
        shape, dt = CONST_SPECS[n]
        ctx.dram[n] = nc.dram_tensor(n, shape, dt, kind="ExternalInput").ap()
    alloc_shared(ctx)
    load_consts(ctx, lnames)
    xres = ctx.b["xres"]
    tlds = [p.dma("gpsimd", xres[:, c * T:(c + 1) * T],
                  xin[c * 128:(c + 1) * 128, :], chan="xio") for c in range(NC)]
    startw = (p.dfence("xio"), p.dfence("misc"))
    for e in ("vector", "tensor", "scalar", "gpsimd"):
        p.op(e, None, waits=startw)
    tdone = emit_mha(ctx, kind, upto)
    for c in range(NC):
        p.dma("gpsimd", xout[c * 128:(c + 1) * 128, :],
              xres[:, c * T:(c + 1) * T],
              waits=(tdone, p.fence("scalar"), p.fence("tensor"))
              if c == 0 else (), chan="xio")
    p.op("gpsimd", None, waits=(p.dfence("xio"),))
    return p.build()


def prep_mha_weights(kind, inputs):
    pfx = {"l0a": "w0a", "l1a": "w1a"}[kind]
    ipfx = {"l0a": "L0_attn_a", "l1a": "L1_attn_a"}[kind]
    d = {}
    if kind == "l0a":
        for nm in ("Wq", "Wk", "Wv"):
            w = np.asarray(inputs[f"{ipfx}_{nm}"])
            d[pfx + "_w" + nm[1].lower()] = _f32(w.transpose(2, 0, 1).reshape(D, D))
        for nm in ("bq", "bk", "bv"):
            d[pfx + "_" + nm] = _bias_cols(np.asarray(inputs[f"{ipfx}_{nm}"]).reshape(-1))
    else:
        for nm in ("Wq", "Wk", "Wv"):
            w = np.asarray(inputs[f"{ipfx}_{nm}"])
            d[pfx + "_w" + nm[1].lower()] = _f32(w.T)
        for nm in ("bq", "bk", "bv"):
            d[pfx + "_" + nm] = _bias_cols(np.asarray(inputs[f"{ipfx}_{nm}"]))
    wo = np.asarray(inputs[f"{ipfx}_Wo"])
    d[pfx + "_wo"] = _f32(wo.T)
    d[pfx + "_bo"] = _bias_cols(np.asarray(inputs[f"{ipfx}_bo"]))
    return d


def kernel(**inputs):
    raise NotImplementedError


def _prep_all(inputs):
    assert inputs["L0_attn_a_Wq"] is not None and inputs["L0_attn_a_Wk"] \
        is not None and inputs["L0_attn_a_Wv"] is not None
    assert inputs["L1_attn_a_Wq"] is not None and inputs["L1_attn_a_Wk"] \
        is not None and inputs["L1_attn_a_Wv"] is not None
    wd = {}
    wd.update(prep_mha_weights("l0a", inputs))
    wd.update(prep_mha_weights("l1a", inputs))
    wd.update({
        "moe_gate_w": _f32(inputs["L0_ffn_gate_w"]),
        "moe_base_wgu": _f32(inputs["L0_ffn_base_wgu"]),
        "moe_base_wd": _f32(inputs["L0_ffn_base_wd"]),
        "moe_exp_wgu": _f32(np.asarray(inputs["L0_ffn_exp_wgu"]).reshape(
            8 * 1024, 5632)),
        "moe_exp_wd": _f32(np.asarray(inputs["L0_ffn_exp_wd"]).reshape(
            8 * 2816, 1024)),
        "cb_w_attn": _f32(np.asarray(inputs["L0_attn_b_w_attn"]).T),
        "cb_w_proj": _f32(np.asarray(inputs["L0_attn_b_w_proj"]).T),
        "bit_w1": _f32(np.asarray(inputs["L1_ffn_W1"]).T),
        "bit_w2": _f32(np.asarray(inputs["L1_ffn_W2"]).T),
        "bit_b1": _bias_cols(np.asarray(inputs["L1_ffn_b1"])),
        "bit_b2": _bias_cols(np.asarray(inputs["L1_ffn_b2"])),
        "hy_wqkv": _f32(inputs["L1_attn_b_Wqkv_w"]),
        "hy_wo": _f32(inputs["L1_attn_b_out_w"]),
        "hy_gw1": _f32(inputs["L1_attn_b_gate_w1"]),
        "hy_gw2": _f32(inputs["L1_attn_b_gate_w2"]),
        "hy_bqkv": _bias_cols(np.asarray(inputs["L1_attn_b_Wqkv_b"])),
        "hy_bo": _bias_cols(np.asarray(inputs["L1_attn_b_out_b"])),
        "hy_bg1": _bias_cols(np.asarray(inputs["L1_attn_b_gate_b1"])),
        "hy_b2row": _f32(np.tile(np.asarray(
            inputs["L1_attn_b_gate_b2"])[None, :], (128, 1))),
    })
    delta = np.asarray(inputs["L0_attn_b_delta"])
    dcol = np.zeros((128, 8), np.float32)
    for h in range(NH):
        dcol[(h % 2) * 64:(h % 2) * 64 + 64, h // 2] = delta[h]
    wd["cb_delta"] = dcol
    lam = np.asarray(inputs["L1_attn_b_Lambda_raw"])
    gam = np.asarray(inputs["L1_attn_b_Gamma"])
    wst = np.asarray(inputs["L1_attn_b_W_static"])
    lamL = np.zeros((128, 512), np.float32)
    gamL = np.zeros((128, 512), np.float32)
    wstL = np.zeros((128, 512), np.float32)
    for h in range(NH):
        hh, hl = h // 2, h % 2
        lamL[hl * 64:(hl + 1) * 64, hh * 64:(hh + 1) * 64] = lam[h]
        gamL[hl * 64:(hl + 1) * 64, hh * 64:(hh + 1) * 64] = gam[h]
        wstL[hl * 64:(hl + 1) * 64, hh * 64:(hh + 1) * 64] = wst[h].T
    wd["hy_lam"] = lamL
    wd["hy_gam"] = gamL
    wd["hy_wst"] = wstL
    wd.update(host_constants())
    return wd


_SEQ = [("l0a", lambda: build_sub_mha("l0a")),
        ("moe", build_sub_moe),
        ("cos", build_sub_cosformer),
        ("moe", build_sub_moe),
        ("l1a", lambda: build_sub_mha("l1a")),
        ("bit", build_sub_bitffn),
        ("hyb", build_sub_hybrid),
        ("bit", build_sub_bitffn)]


def _declared_inputs(nc):
    names = set()
    for alloc in nc.m.functions[0].allocations:
        if isinstance(alloc, mybir.MemoryLocationSet) and                 alloc.kind == "ExternalInput":
            names.add(alloc.memorylocations[0].name)
    names.discard("partition_id")
    return names


def kernel_unfused(**inputs):
    wd = _prep_all(inputs)
    x = np.asarray(inputs["L0_attn_a_hidden_state"])
    xs = [_f32(x[b].T) for b in range(8)]
    for key, builder in _SEQ:
        if key not in _CACHE:
            _CACHE[key] = builder()
        nc = _CACHE[key]
        decl = _declared_inputs(nc)
        in_maps = []
        for b in range(8):
            m = {"xin": xs[b]}
            m.update({k: v for k, v in wd.items() if k in decl})
            in_maps.append(m)
        res = run_bass_kernel_spmd(nc, in_maps, core_ids=list(range(8)))
        xs = [np.ascontiguousarray(res.results[b]["xout"])
              for b in range(8)]
    out = np.stack([xs[b].T for b in range(8)], axis=0)
    return np.ascontiguousarray(out, dtype=np.float32)


def phase_barrier(p):
    waits = tuple(p.fence(e) for e in Prog.ENGINES) + \
        tuple(p.dfence(c) for c in list(p.dchans))
    toks = tuple(p.op(e, None, waits=waits) for e in Prog.ENGINES)
    for e in Prog.ENGINES:
        p.op(e, None, waits=toks)


def build_fused(nphases=8, order=None):
    nc = bass.Bass("TRN2", target_bir_lowering=False)
    p = Prog(nc)
    ctx = Ctx(nc, p)
    xin = nc.dram_tensor("xin", [D, T], F32, kind="ExternalInput").ap()
    xout = nc.dram_tensor("xout", [D, T], F32, kind="ExternalOutput").ap()
    decls = [("w0a_wq", [D, D]), ("w0a_wk", [D, D]), ("w0a_wv", [D, D]),
             ("w0a_wo", [D, D]), ("w0a_bq", [128, 8]), ("w0a_bk", [128, 8]),
             ("w0a_bv", [128, 8]), ("w0a_bo", [128, 8]),
             ("w1a_wq", [D, D]), ("w1a_wk", [D, D]), ("w1a_wv", [D, D]),
             ("w1a_wo", [D, D]), ("w1a_bq", [128, 8]), ("w1a_bk", [128, 8]),
             ("w1a_bv", [128, 8]), ("w1a_bo", [128, 8]),
             ("moe_gate_w", [D, 8]), ("moe_base_wgu", [D, 5632]),
             ("moe_base_wd", [2816, D]), ("moe_exp_wgu", [8 * D, 5632]),
             ("moe_exp_wd", [8 * 2816, D]),
             ("cb_w_attn", [D, 3 * D]), ("cb_w_proj", [D, D]),
             ("cb_delta", [128, 8]),
             ("bit_w1", [D, 4096]), ("bit_w2", [4096, D]),
             ("bit_b1", [128, 32]), ("bit_b2", [128, 8]),
             ("hy_wqkv", [D, 3 * D]), ("hy_wo", [D, D]),
             ("hy_gw1", [D, 512]), ("hy_gw2", [512, 16]),
             ("hy_bqkv", [128, 24]), ("hy_bo", [128, 8]),
             ("hy_bg1", [128, 4]), ("hy_b2row", [128, 16]),
             ("hy_lam", [128, 512]), ("hy_gam", [128, 512]),
             ("hy_wst", [128, 512])]
    for nm, shape in decls:
        ctx.dram[nm] = nc.dram_tensor(nm, shape, F32,
                                      kind="ExternalInput").ap()
    ctx.dram["c_poscos_b"] = nc.dram_tensor("c_poscos_b", [128, T], BF16,
                                            kind="ExternalInput").ap()
    ctx.dram["c_possin_b"] = nc.dram_tensor("c_possin_b", [128, T], BF16,
                                            kind="ExternalInput").ap()
    for nm, shape in (("hy_qd", [D, T]), ("hy_kd", [D, T]), ("hy_vd", [D, T]),
                      ("hy_yd", [D, T]), ("hy_qt", [T, D]), ("hy_kt", [T, D]),
                      ("hy_vt", [T, D])):
        ctx.dram[nm] = nc.dram_tensor(nm, shape, BF16, kind="Internal").ap()
    ctx.dram["hy_xd"] = nc.dram_tensor("hy_xd", [D, T], F32,
                                       kind="Internal").ap()
    ctx.dram["bit_w1q"] = nc.dram_tensor("bit_w1q", [D, 4096], BF16,
                                         kind="Internal").ap()
    ctx.dram["bit_w2q"] = nc.dram_tensor("bit_w2q", [4096, D], BF16,
                                         kind="Internal").ap()
    cnames = ["c_ident", "c_ident_f", "c_ones_row_f",
              "c_ones_col", "c_ones_col_f", "c_scal", "c_delta16",
              "c_ropecos", "c_ropesin", "c_roperm", "c_cmask"]
    for n in cnames:
        shape, dt = CONST_SPECS[n]
        ctx.dram[n] = nc.dram_tensor(n, shape, dt, kind="ExternalInput").ap()
    alloc_shared(ctx)
    load_consts(ctx, [n for n in cnames
                      if n not in ("c_cmask", "c_ropecos", "c_ropesin",
                                   "c_roperm")])
    xres = ctx.b["xres"]
    for c in range(NC):
        p.dma("gpsimd", xres[:, c * T:(c + 1) * T],
              xin[c * 128:(c + 1) * 128, :], chan="xio")
    startw = (p.dfence("xio"), p.dfence("misc"))
    for e in ("vector", "tensor", "scalar", "gpsimd"):
        p.op(e, None, waits=startw)
    emits = [lambda: emit_mha(ctx, "l0a"), lambda: emit_moe(ctx),
             lambda: emit_cosformer(ctx), lambda: emit_moe(ctx),
             lambda: emit_mha(ctx, "l1a"), lambda: emit_bitffn(ctx),
             lambda: emit_hybrid(ctx), lambda: emit_bitffn(ctx)]
    sel = order if order is not None else list(range(nphases))
    for i in sel:
        emits[i]()
        phase_barrier(p)
    for c in range(NC):
        p.dma("gpsimd", xout[c * 128:(c + 1) * 128, :],
              xres[:, c * T:(c + 1) * T], chan="xio")
    p.op("gpsimd", None, waits=(p.dfence("xio"),))
    return p.build()


def kernel_fused(**inputs):
    wd = _prep_all(inputs)
    x = np.asarray(inputs["L0_attn_a_hidden_state"])
    if "fused" not in _CACHE:
        _CACHE["fused"] = build_fused()
    nc = _CACHE["fused"]
    decl = _declared_inputs(nc)
    in_maps = []
    for bb in range(8):
        m = {"xin": _f32(x[bb].T)}
        m.update({k: v for k, v in wd.items() if k in decl})
        in_maps.append(m)
    res = run_bass_kernel_spmd(nc, in_maps, core_ids=list(range(8)))
    out = np.stack([res.results[bb]["xout"].T for bb in range(8)], axis=0)
    return np.ascontiguousarray(out, dtype=np.float32)


kernel = kernel_fused


# version 08
# speedup vs baseline: 1.3113x; 1.0818x over previous
import math
import numpy as np
import ml_dtypes
import concourse.bass as bass
import concourse.mybir as mybir
from concourse.bass_utils import run_bass_kernel_spmd

F32 = mybir.dt.float32
BF16 = mybir.dt.bfloat16
AF = mybir.ActivationFunctionType
ALU = mybir.AluOpType
AX = mybir.AxisListType

T = 2048
D = 1024
NH = 16
DH = 64
NC = D // 128
TT = 512
NTT = T // TT
NKB = T // 128
WG = 256

_CACHE = {}


class Sem:
    __slots__ = ("name", "n", "h")

    def __init__(self, name):
        self.name = name
        self.n = 0
        self.h = None


class Prog:
    ENGINES = ("sync", "tensor", "vector", "scalar", "gpsimd")

    def __init__(self, nc):
        self.nc = nc
        self.ops = {e: [] for e in self.ENGINES}
        self.sems = []
        self.esem = {e: self.sem("es_" + e) for e in self.ENGINES}
        self.dchans = {}

    def dchan(self, name):
        if name not in self.dchans:
            self.dchans[name] = self.sem("dc_" + name)
        return self.dchans[name]

    def sem(self, name):
        s = Sem(name)
        self.sems.append(s)
        return s

    AUTO_SELF = ("vector", "scalar", "gpsimd")

    def op(self, engine, fn, waits=(), track=True):
        tok = None
        incs = ()
        waits = [w for w in waits if w]
        s = self.esem[engine]
        if engine in self.AUTO_SELF and fn is not None and s.n > 0:
            waits.append((s, s.n))
        if track:
            s.n += 1
            tok = (s, s.n)
            incs = ((s, 1),)
        self.ops[engine].append((fn, tuple(waits), incs))
        return tok

    def dma(self, queue, out, in_, waits=(), chan="misc", **kw):
        s = self.dchan(chan)
        s.n += 16
        tok = (s, s.n)

        def fn(eng):
            return eng.dma_start(out=out, in_=in_, **kw)
        self.ops[queue].append((fn, tuple(w for w in waits if w), ((s, 16),)))
        return tok

    def fence(self, engine):
        s = self.esem[engine]
        return (s, s.n)

    def dfence(self, chan):
        s = self.dchan(chan)
        return (s, s.n)

    def build(self):
        import contextlib
        nc = self.nc
        with contextlib.ExitStack() as stack:
            for s in self.sems:
                s.h = stack.enter_context(nc.semaphore(s.name))
            block = stack.enter_context(nc.Block())
            for ename in self.ENGINES:
                ops = self.ops[ename]
                if not ops:
                    continue

                def body(eng, ops=ops):
                    for fn, waits, incs in ops:
                        for s, v in waits:
                            eng.wait_ge(s.h, v)
                        instr = fn(eng) if fn is not None else None
                        if incs:
                            (s0, k0), rest = incs[0], incs[1:]
                            if instr is not None:
                                instr.then_inc(s0.h, k0)
                            else:
                                eng.nop().then_inc(s0.h, k0)
                            for s, k in rest:
                                eng.nop().then_inc(s.h, k)
                getattr(block, ename)(body)
        return nc


def _bf16(x):
    return np.ascontiguousarray(
        np.asarray(x, dtype=np.float32).astype(ml_dtypes.bfloat16))


def _f32(x):
    return np.ascontiguousarray(x, dtype=np.float32)


def _bias_cols(b):
    n = b.shape[0]
    return _f32(b.reshape(n // 128, 128).T)


def host_constants():
    c = {}
    c["c_ident"] = _bf16(np.eye(128))
    c["c_ident_f"] = _f32(np.eye(128))
    d16 = np.zeros((16, 128), np.float32)
    for h in range(16):
        d16[h, (h % 2) * 64:(h % 2) * 64 + 64] = 1.0
    c["c_delta16"] = _bf16(d16)
    c["c_ones_row"] = _bf16(np.ones((1, 128)))
    c["c_ones_row_f"] = _f32(np.ones((1, 128)))
    sc = np.zeros((128, 8), np.float32)
    sc[:, 0] = 1e-5
    sc[:, 1] = 1.0
    c["c_scal"] = _f32(sc)
    c["c_ones_col"] = _bf16(np.ones((128, 1)))
    c["c_ones_col_f"] = _f32(np.ones((128, 1)))
    masks = np.zeros((4, 128, TT), np.float32)
    fq = np.arange(TT)[None, :]
    pp = np.arange(128)[:, None]
    for j in range(4):
        masks[j] = (fq >= j * 128 + pp).astype(np.float32)
    c["c_cmask"] = _bf16(masks.transpose(1, 0, 2).reshape(128, 4 * TT))
    inv = 1.0 / (10000.0 ** (np.arange(0, DH, 2) / DH))
    fr = np.arange(T)[:, None] * inv[None, :]
    emb = np.concatenate([fr, fr], axis=-1)
    cos = np.cos(emb).T
    sin = np.sin(emb).T
    c["c_ropecos"] = _bf16(np.concatenate([cos, cos], 0))
    c["c_ropesin"] = _bf16(np.concatenate([sin, sin], 0))
    inv0 = 1.0 / (10000.0 ** (np.arange(DH) / DH))
    fr0 = np.arange(T)[:, None] * inv0[None, :]
    c["c_poscos_b"] = _bf16(np.concatenate([np.cos(fr0).T, np.cos(fr0).T], 0))
    c["c_possin_b"] = _bf16(np.concatenate([np.sin(fr0).T, np.sin(fr0).T], 0))
    P = np.zeros((128, 128), np.float32)
    for hl in range(2):
        for dd in range(DH):
            r = hl * 64 + dd
            if dd < 32:
                P[hl * 64 + dd + 32, r] = -1.0
            else:
                P[hl * 64 + dd - 32, r] = 1.0
    c["c_roperm"] = _bf16(P)
    return c


CONST_SPECS = {
    "c_ident": ([128, 128], BF16),
    "c_ident_f": ([128, 128], F32),
    "c_delta16": ([16, 128], BF16),
    "c_ones_row": ([1, 128], BF16),
    "c_ones_row_f": ([1, 128], F32),
    "c_scal": ([128, 8], F32),
    "c_ones_col": ([128, 1], BF16),
    "c_ones_col_f": ([128, 1], F32),
    "c_cmask": ([128, 4 * TT], BF16),
    "c_ropecos": ([128, T], BF16),
    "c_ropesin": ([128, T], BF16),
    "c_poscos_b": ([128, T], BF16),
    "c_possin_b": ([128, T], BF16),
    "c_roperm": ([128, 128], BF16),
}


class Ctx:
    def __init__(self, nc, p):
        self.nc = nc
        self.p = p
        self.cin = {}
        self.dram = {}
        self.b = {}
        self.uid = 0
        self.ws = None

    def name(self, base):
        self.uid += 1
        return f"{base}_{self.uid}"


def get_ws(ctx):
    if ctx.ws is None:
        ctx.ws = WStream(ctx)
    return ctx.ws

    def name(self, base):
        self.uid += 1
        return f"{base}_{self.uid}"


def alloc_shared(ctx):
    nc = ctx.nc
    b = ctx.b
    b["xres"] = nc.alloc_sbuf_tensor("xres", [128, NC * T], F32)[:]
    b["arb"] = nc.alloc_sbuf_tensor("arb", [128, 61440], BF16)[:]
    b["arf"] = nc.alloc_sbuf_tensor("arf", [128, 1024], F32)[:]
    b["wf32"] = nc.alloc_sbuf_tensor("wf32", [128, 2 * WG], F32)[:]
    b["wbf"] = nc.alloc_sbuf_tensor("wbf", [128, 16 * WG], BF16)[:]
    b["rows"] = nc.alloc_sbuf_tensor("rows", [1, 2048], F32)[:]
    b["bias"] = nc.alloc_sbuf_tensor("bias", [128, 40], F32)[:]
    b["lrow"] = b["rows"][0:1, 0:1024]
    b["vt"] = b["arb"][:, 57632:57632 + NKB * 130]
    b["ebuf"] = b["arb"][:, 45088:45088 + 4 * TT]
    b["usb"] = b["arf"][:, 512:1024]
    for i in range(7):
        b[f"ps{i}"] = nc.alloc_psum_tensor(f"ps{i}", [128, 512], F32)[:]
    b["pst"] = nc.alloc_psum_tensor("pst", [128, 1024], BF16)[:]
    return b


def load_consts(ctx, names):
    nc, p = ctx.nc, ctx.p
    toks = []
    for n in names:
        shape, dt = CONST_SPECS[n]
        t = nc.alloc_sbuf_tensor(n + "_s", shape, dt)[:]
        ctx.cin[n] = t
        toks.append(p.dma("sync", t, ctx.dram[n], chan="misc"))
    return toks


def emit_ln(ctx, xn_off, xb_off=None):
    p, b, cin = ctx.p, ctx.b, ctx.cin
    xres = b["xres"]
    if xb_off is None:
        xb_off = xn_off + NC * T
    xn = b["arb"][:, xn_off:xn_off + NC * T]
    xb = b["arb"][:, xb_off:xb_off + NC * T]
    stat = b["ps0"]
    mtile = b["ps1"]
    rtile = b["ps2"]
    rows = b["rows"]
    tsc = b["arf"][:, 0:TT]
    onescol = cin["c_ones_col"]
    onesrow = cin["c_ones_row_f"]

    last = None
    prev_bcast_reads = ()
    prev_ev = ()
    for tt in range(NTT):
        tcopy = [p.op("vector", lambda v, c=c, tt=tt: v.tensor_copy(
            xb[:, c * T + tt * TT:c * T + tt * TT + TT],
            xres[:, c * T + tt * TT:c * T + tt * TT + TT]))
            for c in range(NC)]
        mmt = None
        for c in range(NC):
            sl = slice(c * T + tt * TT, c * T + tt * TT + TT)

            def mm(t, sl=sl, c=c):
                return t.matmul(stat[0:1, 0:TT], onescol[:, 0:1], xb[:, sl],
                                start=(c == 0), stop=(c == NC - 1))
            w = [tcopy[c]]
            if c == 0:
                w += list(prev_ev)
            mmt = p.op("tensor", mm, waits=tuple(w), track=(c == NC - 1))
        ev0 = p.op("scalar", lambda s: s.copy(rows[0:1, 0:TT],
                                              stat[0:1, 0:TT]), waits=(mmt,))
        tsq = [p.op("scalar", lambda s, c=c, tt=tt: s.activation(
            xb[:, c * T + tt * TT:c * T + tt * TT + TT],
            xb[:, c * T + tt * TT:c * T + tt * TT + TT], AF.Square),
            waits=(ev0,) if c == 0 else ())
            for c in range(NC)]
        mmt = None
        for c in range(NC):
            sl = slice(c * T + tt * TT, c * T + tt * TT + TT)

            def mm(t, sl=sl, c=c):
                return t.matmul(stat[0:1, 0:TT], onescol[:, 0:1], xb[:, sl],
                                start=(c == 0), stop=(c == NC - 1))
            w = [tsq[c]]
            if c == 0:
                w.append(ev0)
            mmt = p.op("tensor", mm, waits=tuple(w), track=(c == NC - 1))
        ev1 = p.op("scalar", lambda s: s.copy(rows[0:1, TT:2 * TT],
                                              stat[0:1, 0:TT]), waits=(mmt,))
        prev_ev = (ev1,)
        t_mu = p.op("scalar", lambda s: s.activation(
            rows[0:1, 2 * TT:3 * TT], rows[0:1, 0:TT], AF.Copy,
            scale=1.0 / D), waits=(ev1,))
        t_ms = p.op("scalar", lambda s: s.activation(
            rows[0:1, 3 * TT:4 * TT], rows[0:1, 2 * TT:3 * TT], AF.Square),
            waits=(t_mu,))
        t_var = p.op("vector", lambda v: v.scalar_tensor_tensor(
            rows[0:1, TT:2 * TT], rows[0:1, TT:2 * TT], 1.0 / D,
            rows[0:1, 3 * TT:4 * TT], ALU.mult, ALU.subtract), waits=(t_ms,))
        t_sd = p.op("scalar", lambda s: s.activation(
            rows[0:1, TT:2 * TT], rows[0:1, TT:2 * TT], AF.Sqrt,
            bias=cin["c_scal"][0:1, 0:1]), waits=(t_var,))
        t_rs = p.op("vector", lambda v: v.reciprocal(
            rows[0:1, TT:2 * TT], rows[0:1, TT:2 * TT]), waits=(t_sd,))
        tb1 = p.op("tensor", lambda t: t.matmul(
            mtile[:, 0:TT], onesrow[0:1, :], rows[0:1, 2 * TT:3 * TT],
            start=True, stop=True), waits=(t_rs,) + prev_bcast_reads)
        tb2 = p.op("tensor", lambda t: t.matmul(
            rtile[:, 0:TT], onesrow[0:1, :], rows[0:1, TT:2 * TT],
            start=True, stop=True))
        reads = []
        for c in range(NC):
            xsl = slice(c * T + tt * TT, c * T + tt * TT + TT)
            p.op("vector", lambda v, xsl=xsl: v.tensor_tensor(
                tsc, xres[:, xsl], mtile[:, 0:TT], ALU.subtract),
                waits=(tb1, tb2) if c == 0 else ())
            last = p.op("vector", lambda v, xsl=xsl: v.tensor_tensor(
                xn[:, xsl], tsc, rtile[:, 0:TT], ALU.mult))
            reads.append(last)
        prev_bcast_reads = (last,)
    return xn, last


class WStream:

    def __init__(self, ctx):
        self.ctx = ctx
        self.wslot_users = [(), ()]
        self.mm_last = [(), ()]
        self.par = 0
        self.ce_i = 0
        self.wcnt = 0
        self.bank_free = {}


def emit_linear(ctx, ws, cols_src, w_dram, n_in, n_out, out_cb,
                first_waits=(), col_off=0, bank_pair=("ps3", "ps4"),
                conv_engines=("vector", "scalar", "gpsimd"), G=None, nbuf=2,
                bf16_direct=False, tt_list=None):
    p, b = ctx.p, ctx.b
    nc_in = n_in // 128
    if G is None:
        G = WG
    assert nbuf * nc_in * G <= 16 * WG, "wbf overflow"
    if tt_list is None:
        tt_list = tuple(range(NTT))
    if nbuf == 1:
        ws.mm_last[0] = tuple(ws.mm_last[0]) + tuple(ws.mm_last[1])
    wf32, wbf = b["wf32"], b["wbf"]
    banks = (b["ps3"], b["ps4"], b["ps5"], b["ps6"])
    bank_free = ws.bank_free.setdefault("quad", [(), (), (), ()])
    ngroups = (n_out + G - 1) // G
    for g in range(ngroups):
        if nbuf == 2:
            par = ws.par
            ws.par ^= 1
        else:
            par = 0
        cols = min(G, n_out - g * G)
        conv_toks = {}
        for dc in range(nc_in):
            wsl = w_dram[dc * 128:(dc + 1) * 128,
                         col_off + g * G:col_off + g * G + cols]
            dstb = wbf[:, (par * nc_in + dc) * G:(par * nc_in + dc) * G + cols]
            if bf16_direct:
                dtok = p.dma("sync", dstb, wsl,
                             waits=tuple(ws.mm_last[par]) if dc == 0 else (),
                             chan=f"wd{par}_{dc}")
                conv_toks[dc] = dtok
                continue
            slot = ws.wcnt % 2
            ws.wcnt += 1
            dst32 = wf32[:, slot * G:slot * G + cols]
            dtok = p.dma("sync", dst32, wsl, waits=ws.wslot_users[slot],
                         chan=f"wslot{slot}")
            eng = conv_engines[ws.ce_i % len(conv_engines)]
            ws.ce_i += 1

            def cv(e, dstb=dstb, dst32=dst32):
                if hasattr(e, "tensor_copy"):
                    return e.tensor_copy(dstb, dst32)
                return e.copy(dstb, dst32)
            ctok = p.op(eng, cv, waits=(dtok,) + tuple(ws.mm_last[par]))
            ws.wslot_users[slot] = (ctok,)
            conv_toks[dc] = ctok
        gsz = (cols + 127) // 128
        last_mm = None
        nb = len(tt_list)
        for oci in range(gsz):
            oc = g * (G // 128) + oci
            ocols = min(128, cols - oci * 128)
            mm_of_bank = [None] * nb
            for dc in range(nc_in):
                lhsT = wbf[:, (par * nc_in + dc) * G + oci * 128:
                           (par * nc_in + dc) * G + oci * 128 + ocols]
                for ti, tt in enumerate(tt_list):
                    pbank = banks[ti]

                    def mm(t, lhsT=lhsT, dc=dc, tt=tt, ocols=ocols,
                           pbank=pbank):
                        return t.matmul(pbank[0:ocols, 0:TT], lhsT,
                                        cols_src(dc, tt), start=(dc == 0),
                                        stop=(dc == nc_in - 1))
                    w = []
                    if ti == 0:
                        w.append(conv_toks[dc])
                    if dc == 0:
                        w += list(bank_free[ti])
                        if ti == 0:
                            w += list(first_waits)
                            first_waits = ()
                    mmt = p.op("tensor", mm, waits=tuple(w),
                               track=(dc == nc_in - 1))
                    if dc == nc_in - 1:
                        mm_of_bank[ti] = mmt
            for ti, tt in enumerate(tt_list):
                etok = out_cb(oc, tt, banks[ti], mm_of_bank[ti])
                bank_free[ti] = (etok,)
                last_mm = mm_of_bank[ti]
        ws.mm_last[par] = (last_mm,)
    return bank_free


def emit_vtok(ctx, v_src_pair, first_waits=()):
    p, b, cin = ctx.p, ctx.b, ctx.cin
    vt = b["vt"]
    pst = b["pst"]
    ident = cin["c_ident"]
    tok = None
    prev_ev = tuple(first_waits) + tuple(ctx.b.get("vt_last", ()))
    for kb in range(NKB):
        t1 = p.op("tensor", lambda t, kb=kb: t.transpose(
            pst[:, 0:128], v_src_pair[:, kb * 128:(kb + 1) * 128],
            ident[:, :]), waits=prev_ev)
        dst = vt[:, kb * 130:kb * 130 + 130].rearrange(
            "p (h c) -> p h c", h=2)[:, :, 0:64]
        srcv = pst[:, 0:128].rearrange("p (h c) -> p h c", h=2)
        e1 = p.op("scalar", lambda s, dst=dst, srcv=srcv: s.copy(dst, srcv),
                  waits=(t1,))
        prev_ev = (e1,)
        tok = e1
    ctx.b["vt_last"] = (tok,)
    return tok


def emit_attn_pair(ctx, pair, q_lhs, k_pair, y_dst, causal, scale,
                   extra_mask=None, first_waits=(), two_part=False):
    p, b, cin = ctx.p, ctx.b, ctx.cin
    sb = (b["ps0"], b["ps1"], b["ps2"], b["ps3"])
    ub = (b["ps4"], b["ps5"])
    scb = b["ps6"]
    vt = b["vt"]
    ebuf = b["arf"]
    eb = b["ebuf"]
    usb = b["usb"]
    cmask = cin.get("c_cmask")
    onesrow = cin["c_ones_row_f"]
    rrow = b["rows"]

    y_out_toks = []
    ufree = [(), ()]
    sfree = {(h, par): tuple(first_waits) for h in (0, 1) for par in (0, 1)}
    for qt in range(NTT):
        kb_list = list(range(NKB))
        if causal:
            kb_list = [kb for kb in kb_list if kb * 128 < (qt + 1) * TT]
        efree = {}
        utoks = [None, None]
        for ik, kb in enumerate(kb_list):
            parity = ik % 2
            for h in (0, 1):
                sbank = sb[h * 2 + parity]
                if two_part:
                    p.op("tensor", lambda t, h=h, kb=kb, qt=qt, sbank=sbank:
                         t.matmul(sbank[0:128, 0:TT], k_pair(h, kb, 0),
                                  q_lhs(h, qt, 0), start=True, stop=False),
                         waits=sfree[(h, parity)], track=False)
                    stok = p.op("tensor",
                                lambda t, h=h, kb=kb, qt=qt, sbank=sbank:
                                t.matmul(sbank[0:128, 0:TT], k_pair(h, kb, 1),
                                         q_lhs(h, qt, 1), start=False,
                                         stop=True))
                else:
                    stok = p.op("tensor",
                                lambda t, h=h, kb=kb, qt=qt, sbank=sbank:
                                t.matmul(sbank[0:128, 0:TT], k_pair(h, kb),
                                         q_lhs(h, qt), start=True, stop=True),
                                waits=sfree[(h, parity)])
                esl = eb[:, (h * 2 + parity) * TT:(h * 2 + parity) * TT + TT]
                etok = p.op("scalar", lambda s, esl=esl, sbank=sbank:
                            s.activation(esl, sbank[0:128, 0:TT], AF.Exp,
                                         scale=scale),
                            waits=(stok,) + efree.get((h, parity), ()))
                sfree[(h, parity)] = (etok,)
                mtok = etok
                if extra_mask is not None:
                    mtok = extra_mask(h, kb, qt, esl, etok)
                if causal and kb * 128 >= qt * TT:
                    j = kb - qt * 4
                    mtok = p.op("vector", lambda v, esl=esl, j=j: v.tensor_tensor(
                        esl, esl, cmask[:, j * TT:(j + 1) * TT], ALU.mult),
                        waits=(mtok,))
                first = (ik == 0)
                last = (ik == len(kb_list) - 1)
                vsl = vt[:, kb * 130 + h * 65:kb * 130 + h * 65 + 65]
                ut = p.op("tensor", lambda t, h=h, vsl=vsl, esl=esl, first=first,
                          last=last: t.matmul(ub[h][0:65, 0:TT], vsl, esl,
                                              start=first, stop=last),
                          waits=(mtok,) + (ufree[h] if first else ()))
                efree[(h, parity)] = (ut,)
                if last:
                    utoks[h] = ut
        for h in (0, 1):
            usl = usb
            ev = p.op("scalar", lambda s, usl=usl, h=h: s.copy(
                usl[0:65, :], ub[h][0:65, 0:TT]),
                waits=(utoks[h],) + tuple(y_out_toks[-1:]))
            lrow = b["lrow"][0:1, h * TT:h * TT + TT]
            l1 = p.op("scalar", lambda s, usl=usl, lrow=lrow: s.activation(
                lrow, usl[64:65, :], AF.Ln), waits=(ev,))
            l2 = p.op("scalar", lambda s, lrow=lrow: s.activation(
                lrow, lrow, AF.Exp, scale=-1.0), waits=(l1,))
            bt = p.op("tensor", lambda t, lrow=lrow: t.matmul(
                scb[0:64, 0:TT], onesrow[0:1, 0:64], lrow, start=True,
                stop=True), waits=(l2,) + tuple(y_out_toks[-1:]))
            yt = p.op("vector", lambda v, usl=usl, h=h, qt=qt: v.tensor_tensor(
                y_dst(h, qt), usl[0:64, :], scb[0:64, 0:TT], ALU.mult),
                waits=(bt,))
            ufree[h] = (ev,)
            y_out_toks.append(yt)
    return y_out_toks


def emit_mha(ctx, kind, upto="full"):
    p, b, cin, dram = ctx.p, ctx.b, ctx.cin, ctx.dram
    pfx = {"l0a": "w0a", "l1a": "w1a"}[kind]
    rope = (kind == "l1a")
    xn, t_xn = emit_ln(ctx, 0, xb_off=26624)
    if upto == "ln":
        return t_xn
    bqkv = ctx.b["bias"][:, 0:32]
    bc_chan = ctx.name("bias")
    p.dma("sync", bqkv[:, 0:8], dram[pfx + "_bq"], chan=bc_chan)
    p.dma("sync", bqkv[:, 8:16], dram[pfx + "_bk"], chan=bc_chan)
    p.dma("sync", bqkv[:, 16:24], dram[pfx + "_bv"], chan=bc_chan)
    p.dma("sync", bqkv[:, 24:32], dram[pfx + "_bo"], chan=bc_chan)
    tb = p.dfence(bc_chan)

    QOF, KOF, VOF, TRIG, YOF = 16384, 18432, 20480, 22528, 26624
    qP = b["arb"][:, QOF:QOF + T]
    kP = b["arb"][:, KOF:KOF + T]
    vP = b["arb"][:, VOF:VOF + T]
    YB = b["arb"][:, YOF + 0:YOF + NC * T]
    ws = get_ws(ctx)
    t_ones = p.op("vector", lambda v: v.memset(
        b["vt"][:].rearrange("p (k c) -> p k c", c=130)[:, :, 64:65], 1.0))
    t_ones2 = p.op("vector", lambda v: v.memset(
        b["vt"][:].rearrange("p (k c) -> p k c", c=130)[:, :, 129:130], 1.0))

    cos = b["arb"][:, TRIG:TRIG + T]
    sin = b["arb"][:, TRIG + T:TRIG + 2 * T]
    prev_attn = (t_xn, tb)
    roperm = cin.get("c_roperm")
    if rope:
        if roperm is None:
            roperm = b["arb"][:, 59712:59840]
            p.dma("sync", roperm, dram["c_roperm"], chan="trig")
        p.dma("sync", cos, dram["c_ropecos"], chan="trig")
        p.dma("sync", sin, dram["c_ropesin"], chan="trig")
        prev_attn = prev_attn + (p.dfence("trig"),)
    for pair in range(NC):
        toks = {"q": [], "k": [], "v": []}

        def mk_cb(dst, which, pair=pair):
            def cb(oc, tt, ps, mmtok):
                bc = bqkv[:, {"q": 0, "k": 8, "v": 16}[which] + pair:
                          {"q": 0, "k": 8, "v": 16}[which] + pair + 1]
                tok = p.op("scalar", lambda s, tt=tt, ps=ps: s.activation(
                    dst[:, tt * TT:tt * TT + TT], ps[0:128, 0:TT], AF.Identity,
                    bias=bc), waits=(mmtok,))
                toks[which].append(tok)
                return tok
            return cb
        for which, dst, wname in (("q", qP, "_wq"), ("k", kP, "_wk"),
                                  ("v", vP, "_wv")):
            emit_linear(ctx, ws, lambda dc, tt: xn[:, dc * T + tt * TT:
                                                   dc * T + tt * TT + TT],
                        dram[pfx + wname], D, 128, mk_cb(dst, which),
                        first_waits=prev_attn, col_off=pair * 128,
                        bank_pair=("ps3", "ps4"),
                        conv_engines=("vector", "scalar"))
            prev_attn = ()
        if rope:
            rope_prev = (toks["v"][-1],)
            for which, buf in (("q", qP), ("k", kP)):
                for tt in range(NTT):
                    sl = slice(tt * TT, tt * TT + TT)
                    wt = (tuple(toks[which]) if tt == 0 else ()) + rope_prev
                    m1 = p.op("tensor", lambda t, sl=sl, buf=buf: t.matmul(
                        b["ps5"][:, 0:TT], roperm, buf[:, sl], start=True,
                        stop=True), waits=wt)
                    s1 = p.op("vector", lambda v, sl=sl, buf=buf: v.tensor_tensor(
                        b["arf"][:, 0:TT], b["ps5"][:, 0:TT], sin[:, sl],
                        ALU.mult), waits=(m1,))
                    rope_prev = (s1,)
                    s2 = p.op("vector", lambda v, sl=sl, buf=buf: v.tensor_tensor(
                        b["arf"][:, TT:2 * TT], buf[:, sl], cos[:, sl],
                        ALU.mult))
                    s3 = p.op("vector", lambda v, sl=sl, buf=buf: v.tensor_tensor(
                        buf[:, sl], b["arf"][:, 0:TT], b["arf"][:, TT:2 * TT],
                        ALU.add))
                    toks[which].append(s3)
            bf = ws.bank_free.get("quad")
            if bf is not None:
                bf[2] = tuple(bf[2]) + rope_prev
        if upto == "qkv0":
            return toks["v"][-1]
        t_vt = emit_vtok(ctx, vP, first_waits=tuple(toks["v"]))
        if upto == "vtok0":
            return t_vt
        pre = tuple(toks["q"]) + tuple(toks["k"]) + (t_vt, t_ones, t_ones2)
        yt = emit_attn_pair(
            ctx, pair,
            q_lhs=lambda h, qt: qP[h * 64:h * 64 + 64, qt * TT:qt * TT + TT],
            k_pair=lambda h, kb: kP[h * 64:h * 64 + 64, kb * 128:kb * 128 + 128],
            y_dst=lambda h, qt, pair=pair: YB[
                h * 64:h * 64 + 64,
                pair * T + qt * TT:pair * T + qt * TT + TT],
            causal=False, scale=0.125, first_waits=pre)
        prev_attn = tuple(yt[-2:])
        if upto == "pair0":
            return yt[-1]
    if upto == "pairs":
        return prev_attn[-1]
    def ocb(oc, tt, ps, mmtok):
        bc = bqkv[:, 24 + oc:24 + oc + 1]
        sl = slice(oc * T + tt * TT, oc * T + tt * TT + TT)
        return p.op("vector", lambda v, sl=sl, ps=ps, bc=bc:
                    v.scalar_tensor_tensor(b["xres"][:, sl], ps[0:128, 0:TT],
                                           bc, b["xres"][:, sl], ALU.add,
                                           ALU.add), waits=(mmtok,))
    emit_linear(ctx, ws, lambda dc, tt: YB[:, dc * T + tt * TT:
                                           dc * T + tt * TT + TT],
                dram[pfx + "_wo"], D, D, ocb, first_waits=prev_attn,
                bank_pair=("ps3", "ps4"), conv_engines=("scalar", "gpsimd"))
    return p.fence("vector")




AOF = 16384


def emit_moe_gate(ctx, xn, t_xn):
    p, b, cin, dram = ctx.p, ctx.b, ctx.cin, ctx.dram
    arf = b["arf"]
    gl = arf[:, 512:640]
    CW = arf[:, 640:768]
    m1 = arf[:, 768:769]
    m2 = arf[:, 769:770]
    d12 = arf[:, 770:771]
    w1 = arf[:, 771:772]
    w2 = arf[:, 772:773]
    eq1 = arf[:, 776:784]
    tmp = arf[:, 784:792]
    eq2 = arf[:, 792:800]
    ps5 = b["ps5"]
    gw = b["wbf"][:, 0:64]
    gwf = b["wf32"][:, 0:64]
    ch = ctx.name("gw")
    dt0 = p.dma("sync", gwf.rearrange("p (c e) -> p c e", e=8),
                dram["moe_gate_w"].rearrange("(c p) e -> p c e", p=128),
                chan=ch)
    tcv = p.op("scalar", lambda s: s.copy(gw, gwf), waits=(dt0,))
    cw_tok = None
    prev_ev = ()
    for tb in range(NKB):
        mmt = None
        for dc in range(NC):
            def mm(t, tb=tb, dc=dc):
                return t.matmul(ps5[0:128, 0:8],
                                xn[:, dc * T + tb * 128:dc * T + (tb + 1) * 128],
                                gw[:, dc * 8:dc * 8 + 8],
                                start=(dc == 0), stop=(dc == NC - 1))
            w = [tcv, t_xn] if dc == 0 else []
            if dc == 0:
                w += list(prev_ev)
            mmt = p.op("tensor", mm, waits=tuple(w), track=(dc == NC - 1))
        ev = p.op("scalar", lambda s, tb=tb: s.copy(gl[:, tb * 8:(tb + 1) * 8],
                                                    ps5[0:128, 0:8]),
                  waits=(mmt,))
        prev_ev = (ev,)
        glt = gl[:, tb * 8:(tb + 1) * 8]
        t1 = p.op("vector", lambda v, glt=glt: v.tensor_reduce(
            m1, glt, AX.X, ALU.max), waits=(ev,))
        t2 = p.op("vector", lambda v, glt=glt: v.tensor_scalar(
            eq1, glt, m1[:, 0:1], None, ALU.is_ge))
        t3 = p.op("vector", lambda v, glt=glt: v.scalar_tensor_tensor(
            tmp, eq1, -1e30, glt, ALU.mult, ALU.add))
        t4 = p.op("vector", lambda v: v.tensor_reduce(m2, tmp, AX.X, ALU.max))
        t5 = p.op("vector", lambda v, glt=glt: v.tensor_scalar(
            eq2, tmp, m2[:, 0:1], None, ALU.is_ge))
        t6 = p.op("vector", lambda v: v.tensor_tensor(d12, m1, m2, ALU.subtract))
        t7 = p.op("scalar", lambda s: s.activation(w1, d12, AF.Sigmoid),
                  waits=(t6,))
        t8 = p.op("scalar", lambda s: s.activation(w2, d12, AF.Sigmoid,
                                                   scale=-1.0))
        t9 = p.op("vector", lambda v: v.tensor_scalar(
            eq1, eq1, w1[:, 0:1], None, ALU.mult), waits=(t8,))
        cw_tok = p.op("vector", lambda v, tb=tb: v.scalar_tensor_tensor(
            CW[:, tb * 8:(tb + 1) * 8], eq2, w2[:, 0:1], eq1, ALU.mult,
            ALU.add))
    return CW, cw_tok


def emit_cw_row(ctx, CW, e, cw_tok, prev=()):
    p, b, cin = ctx.p, ctx.b, ctx.cin
    rows = b["rows"]
    ps6 = b["ps6"]
    identf = cin["c_ident_f"]
    toks = []
    prev_t = prev
    for tb in range(NKB):
        t1 = p.op("tensor", lambda t, tb=tb: t.transpose(
            ps6[0:1, 0:128], CW[:, tb * 8 + e:tb * 8 + e + 1], identf),
            waits=(cw_tok,) + tuple(prev_t))
        e1 = p.op("scalar", lambda s, tb=tb: s.copy(
            rows[0:1, tb * 128:(tb + 1) * 128], ps6[0:1, 0:128]), waits=(t1,))
        prev_t = (e1,)
        toks.append(e1)
    return toks[-1]


def emit_moe_mlp(ctx, xn, wgu_dram, wd_dram, ws, first_waits, cw_row_tok,
                 weighted):
    p, b, cin = ctx.p, ctx.b, ctx.cin
    arb, arf = b["arb"], b["arf"]
    wbf = b["wbf"]
    wf32 = b["wf32"]
    sg = arf[:, 0:512]
    cwb = arf[:, 0:512]
    II = 22
    bankA, bankB = b["ps0"], b["ps1"]
    a_toks = []
    freeA = tuple(first_waits)
    freeB = ()
    sg_free = ()
    for ocp in range(II):
        par = ocp % 2
        conv = {}
        for half, coff in ((0, ocp * 128), (1, (II + ocp) * 128)):
            for dc in range(NC):
                slot = ws.wcnt % 2
                ws.wcnt += 1
                dst32 = wf32[:, slot * WG:slot * WG + 128]
                dtok = p.dma("sync", dst32,
                             wgu_dram[dc * 128:(dc + 1) * 128, coff:coff + 128],
                             waits=ws.wslot_users[slot], chan=f"wslot{slot}")
                eng = ("scalar", "gpsimd")[ws.ce_i % 2]
                ws.ce_i += 1
                dstb = wbf[:, (par * 16 + half * 8 + dc) * 128:
                           (par * 16 + half * 8 + dc) * 128 + 128]

                def cv(en, dstb=dstb, dst32=dst32):
                    if hasattr(en, "tensor_copy"):
                        return en.tensor_copy(dstb, dst32)
                    return en.copy(dstb, dst32)
                ctok = p.op(eng, cv, waits=(dtok,) + tuple(ws.mm_last[par]))
                ws.wslot_users[slot] = (ctok,)
                conv[(half, dc)] = ctok
        last_mm = None
        for tt in range(NTT):
            mg = None
            for dc in range(NC):
                lhsT = wbf[:, (par * 16 + dc) * 128:(par * 16 + dc) * 128 + 128]

                def mm(t, lhsT=lhsT, dc=dc, tt=tt):
                    return t.matmul(bankA[0:128, 0:TT], lhsT,
                                    xn[:, dc * T + tt * TT:dc * T + tt * TT + TT],
                                    start=(dc == 0), stop=(dc == NC - 1))
                w = [conv[(0, dc)]]
                if dc == 0:
                    w += list(freeA)
                mg = p.op("tensor", mm, waits=tuple(w), track=(dc == NC - 1))
            tsg = p.op("scalar", lambda s: s.activation(
                sg, bankA[0:128, 0:TT], AF.Silu), waits=(mg,) + sg_free)
            freeA = (tsg,)
            mu = None
            for dc in range(NC):
                lhsT = wbf[:, (par * 16 + 8 + dc) * 128:
                           (par * 16 + 8 + dc) * 128 + 128]

                def mm(t, lhsT=lhsT, dc=dc, tt=tt):
                    return t.matmul(bankB[0:128, 0:TT], lhsT,
                                    xn[:, dc * T + tt * TT:dc * T + tt * TT + TT],
                                    start=(dc == 0), stop=(dc == NC - 1))
                w = [conv[(1, dc)]]
                if dc == 0:
                    w += list(freeB)
                mu = p.op("tensor", mm, waits=tuple(w), track=(dc == NC - 1))
            asl = arb[:, AOF + ocp * T + tt * TT:AOF + ocp * T + tt * TT + TT]
            ta = p.op("vector", lambda v, asl=asl: v.tensor_tensor(
                asl, sg, bankB[0:128, 0:TT], ALU.mult), waits=(tsg, mu))
            freeB = (ta,)
            sg_free = (ta,)
            a_toks.append(ta)
            last_mm = mu
        ws.mm_last[par] = (last_mm,)

    rows = b["rows"]
    psc = b["ps2"]
    onesrow = cin["c_ones_row_f"]
    cw_state = {"tt": -1, "tok": (), "cwb_users": ()}

    def ocb(oc, tt, ps, mmtok):
        if not weighted:
            return p.op("vector", lambda v, oc=oc, tt=tt, ps=ps: v.tensor_tensor(
                b["xres"][:, oc * T + tt * TT:oc * T + tt * TT + TT],
                b["xres"][:, oc * T + tt * TT:oc * T + tt * TT + TT],
                ps[0:128, 0:TT], ALU.add), waits=(mmtok,))
        bt = p.op("tensor", lambda t, tt=tt: t.matmul(
            psc[0:128, 0:TT], onesrow[0:1, :],
            rows[0:1, tt * TT:tt * TT + TT], start=True, stop=True),
            waits=(cw_row_tok,) + cw_state["tok"])
        ev = p.op("scalar", lambda s: s.copy(cwb, psc[0:128, 0:TT]),
                  waits=(bt,) + cw_state["cwb_users"])
        cw_state["tok"] = (ev,)
        t1 = p.op("vector", lambda v, ps=ps: v.tensor_tensor(
            ps[0:128, 0:TT], ps[0:128, 0:TT], cwb, ALU.mult),
            waits=(mmtok, ev))
        t2 = p.op("vector", lambda v, oc=oc, tt=tt, ps=ps: v.tensor_tensor(
            b["xres"][:, oc * T + tt * TT:oc * T + tt * TT + TT],
            b["xres"][:, oc * T + tt * TT:oc * T + tt * TT + TT],
            ps[0:128, 0:TT], ALU.add))
        cw_state["cwb_users"] = (t1,)
        return t2

    emit_linear(ctx, ws, lambda dc, tt: arb[:, AOF + dc * T + tt * TT:
                                            AOF + dc * T + tt * TT + TT],
                wd_dram, 2816, D, ocb, first_waits=tuple(a_toks[-1:]),
                bank_pair=("ps3", "ps4"), conv_engines=("scalar", "gpsimd"),
                G=128, nbuf=1)
    return p.fence("vector")


def emit_moe(ctx, upto="full"):
    p, b, dram = ctx.p, ctx.b, ctx.dram
    xn, t_xn = emit_ln(ctx, 0, xb_off=AOF)
    if upto == "ln":
        return t_xn
    CW, cw_tok = emit_moe_gate(ctx, xn, t_xn)
    if upto == "gate":
        return cw_tok
    ws = get_ws(ctx)
    seed = (cw_tok, p.fence("tensor"))
    ws.wslot_users = [seed, seed]
    ws.mm_last = [seed, seed]
    tok = emit_moe_mlp(ctx, xn, dram["moe_base_wgu"], dram["moe_base_wd"], ws,
                       (t_xn,), None, weighted=False)
    if upto == "base":
        return tok
    prev_cw = (tok, p.fence("scalar"))
    for e in range(8):
        crt = emit_cw_row(ctx, CW, e, cw_tok, prev=prev_cw)
        prev_cw = (crt,)
        wgu_e = dram["moe_exp_wgu"].rearrange("(e h) i -> e h i", e=8)[e]
        wd_e = dram["moe_exp_wd"].rearrange("(e i) h -> e i h", e=8)[e]
        tok = emit_moe_mlp(ctx, xn, wgu_e, wd_e, ws, (tok,), crt,
                           weighted=True)
        prev_cw = (tok, p.fence("tensor"))
    return tok


def build_sub_moe(upto="full"):
    nc = bass.Bass("TRN2", target_bir_lowering=False)
    p = Prog(nc)
    ctx = Ctx(nc, p)
    xin = nc.dram_tensor("xin", [D, T], F32, kind="ExternalInput").ap()
    xout = nc.dram_tensor("xout", [D, T], F32, kind="ExternalOutput").ap()
    ctx.dram["moe_gate_w"] = nc.dram_tensor("moe_gate_w", [D, 8], F32,
                                            kind="ExternalInput").ap()
    ctx.dram["moe_base_wgu"] = nc.dram_tensor("moe_base_wgu", [D, 5632], F32,
                                              kind="ExternalInput").ap()
    ctx.dram["moe_base_wd"] = nc.dram_tensor("moe_base_wd", [2816, D], F32,
                                             kind="ExternalInput").ap()
    ctx.dram["moe_exp_wgu"] = nc.dram_tensor("moe_exp_wgu", [8 * D, 5632], F32,
                                             kind="ExternalInput").ap()
    ctx.dram["moe_exp_wd"] = nc.dram_tensor("moe_exp_wd", [8 * 2816, D], F32,
                                            kind="ExternalInput").ap()
    cnames = ["c_ident", "c_ident_f", "c_ones_row", "c_ones_row_f",
              "c_ones_col", "c_scal"]
    for n in cnames:
        shape, dt = CONST_SPECS[n]
        ctx.dram[n] = nc.dram_tensor(n, shape, dt, kind="ExternalInput").ap()
    alloc_shared(ctx)
    load_consts(ctx, cnames)
    xres = ctx.b["xres"]
    tlds = [p.dma("gpsimd", xres[:, c * T:(c + 1) * T],
                  xin[c * 128:(c + 1) * 128, :], chan="xio") for c in range(NC)]
    startw = (p.dfence("xio"), p.dfence("misc"))
    for e in ("vector", "tensor", "scalar", "gpsimd"):
        p.op(e, None, waits=startw)
    tdone = emit_moe(ctx, upto)
    for c in range(NC):
        p.dma("gpsimd", xout[c * 128:(c + 1) * 128, :],
              xres[:, c * T:(c + 1) * T],
              waits=(tdone, p.fence("scalar"), p.fence("tensor"),
                     p.fence("vector")) if c == 0 else (), chan="xio")
    p.op("gpsimd", None, waits=(p.dfence("xio"),))
    return p.build()




def emit_quantize(ctx, w_dram, wq_dram, n_in, n_out, gslot, first_waits=()):
    p, b, cin = ctx.p, ctx.b, ctx.cin
    wf32 = b["wf32"]
    wbf = b["wbf"]
    arf = b["arf"]
    acc = arf[:, 768 + gslot:769 + gslot]
    gcol = arf[:, 772 + gslot:773 + gslot]
    ps6 = b["ps6"]
    onescol = cin["c_ones_col"]
    onesrow = cin["c_ones_row_f"]
    onescolf = cin["c_ones_col_f"]
    ncol = n_out // WG
    ntile = (n_in // 128) * ncol
    ch = ctx.name("qz")
    t_acc = p.op("vector", lambda v: v.memset(acc, 0.0), waits=first_waits)
    slot_free = [tuple(first_waits), tuple(first_waits)]
    red = arf[:, 774 + gslot:775 + gslot]
    for i in range(ntile):
        a, c0 = i // ncol, (i % ncol) * WG
        slot = i % 2
        dst = wf32[:, slot * WG:slot * WG + WG]
        dtok = p.dma("sync", dst,
                     w_dram[a * 128:(a + 1) * 128, c0:c0 + WG],
                     waits=slot_free[slot], chan=ch + str(slot))
        t1 = p.op("vector", lambda v, dst=dst: v.tensor_reduce(
            red, dst, AX.X, ALU.add, apply_absolute_value=True),
            waits=(dtok, t_acc))
        t_acc = p.op("vector", lambda v: v.tensor_tensor(acc, acc, red,
                                                         ALU.add))
        slot_free[slot] = (t_acc,)
    tmm = p.op("tensor", lambda t: t.matmul(ps6[0:1, 0:1], onescolf[:, 0:1],
                                            acc, start=True, stop=True),
               waits=(t_acc,))
    gl = b["rows"][0:1, 4:5]
    t2 = p.op("scalar", lambda s: s.activation(
        gl, ps6[0:1, 0:1], AF.Identity, scale=1.0 / (n_in * n_out),
        bias=cin["c_scal"][0:1, 0:1]), waits=(tmm,))
    t3 = p.op("vector", lambda v: v.reciprocal(gl, gl), waits=(t2,))
    t4 = p.op("tensor", lambda t: t.matmul(ps6[0:128, 1:2], onesrow[0:1, :],
                                           gl, start=True, stop=True),
               waits=(t3,))
    t5 = p.op("scalar", lambda s: s.copy(gcol, ps6[0:128, 1:2]), waits=(t4,))
    sgn = arf[:, 0:WG]
    ab = arf[:, 512:512 + WG]
    prev_q = ()
    slot_free = [(t5,), (t5,)]
    wq_free = [tuple(first_waits), tuple(first_waits)]
    for i in range(ntile):
        a, c0 = i // ncol, (i % ncol) * WG
        slot = i % 2
        dst = wf32[:, slot * WG:slot * WG + WG]
        dtok = p.dma("sync", dst,
                     w_dram[a * 128:(a + 1) * 128, c0:c0 + WG],
                     waits=slot_free[slot], chan=ch + str(slot))
        t6 = p.op("vector", lambda v, dst=dst: v.tensor_scalar(
            dst, dst, gcol, None, ALU.mult), waits=(dtok,) + prev_q)
        t7 = p.op("vector", lambda v, dst=dst: v.tensor_scalar(
            dst, dst, 1.0, -1.0, ALU.min, ALU.max))
        t8 = p.op("scalar", lambda s, dst=dst: s.activation(sgn, dst, AF.Sign),
                  waits=(t7,))
        t9 = p.op("scalar", lambda s, dst=dst: s.activation(ab, dst,
                                                            AF.Square),
                  waits=(t7,))
        t10 = p.op("vector", lambda v: v.tensor_scalar(
            ab, ab, 0.25, None, ALU.is_ge), waits=(t9,))
        qsl = wbf[:, slot * WG:slot * WG + WG]
        t11 = p.op("vector", lambda v, qsl=qsl: v.tensor_tensor(
            qsl, ab, sgn, ALU.mult), waits=(t8,) + wq_free[slot])
        prev_q = (t11,)
        stok = p.dma("gpsimd", wq_dram[a * 128:(a + 1) * 128, c0:c0 + WG],
                     qsl, waits=(t11,), chan=ch + "s" + str(slot))
        slot_free[slot] = (t11,)
        wq_free[slot] = (stok,)
    return p.dfence(ch + "s0"), p.dfence(ch + "s1")


def emit_bitffn(ctx, upto="full"):
    p, b, cin, dram = ctx.p, ctx.b, ctx.cin, ctx.dram
    arb = b["arb"]
    xn, t_xn = emit_ln(ctx, 0, xb_off=AOF)
    bcol = ctx.b["bias"]
    bch = ctx.name("bias")
    p.dma("sync", bcol[:, 0:32], dram["bit_b1"], chan=bch)
    p.dma("sync", bcol[:, 32:40], dram["bit_b2"], chan=bch)
    tb = p.dfence(bch)
    if "bit_quant_toks" not in ctx.b:
        q1 = emit_quantize(ctx, ctx.dram["bit_w1"], ctx.dram["bit_w1q"],
                           D, 4096, 0,
                           first_waits=(t_xn, ctx.p.fence("vector"),
                                        ctx.p.fence("scalar")))
        q2 = emit_quantize(ctx, ctx.dram["bit_w2"], ctx.dram["bit_w2q"],
                           4096, D, 1,
                           first_waits=(ctx.p.fence("vector"),
                                        ctx.p.fence("scalar")) + q1)
        ctx.b["bit_quant_toks"] = (q1, q2)
    qt1, qt2 = ctx.b["bit_quant_toks"]
    ws = get_ws(ctx)
    seed = tuple(qt1) + tuple(qt2)
    ws.mm_last = [seed + tuple(ws.mm_last[0]), seed + tuple(ws.mm_last[1])]
    ws.wslot_users = [seed + tuple(ws.wslot_users[0]),
                      seed + tuple(ws.wslot_users[1])]
    HOF = AOF
    TH = T // 2
    tok = (t_xn, tb)
    for th in range(2):
        hbuf = arb[:, HOF:HOF + 32 * TH]

        def hcb(oc, tt, ps, mmtok, th=th, hbuf=hbuf):
            sl = slice(oc * TH + (tt - th * 2) * TT, oc * TH + (tt - th * 2) * TT + TT)
            return p.op("scalar", lambda s, sl=sl, ps=ps, oc=oc: s.activation(
                hbuf[:, sl], ps[0:128, 0:TT], AF.Relu,
                bias=bcol[:, oc:oc + 1]), waits=(mmtok,))

        emit_linear(ctx, ws, lambda dc, tt, th=th: xn[
            :, dc * T + th * TH + (tt - th * 2) * TT:
            dc * T + th * TH + (tt - th * 2) * TT + TT],
            ctx.dram["bit_w1q"], D, 4096, hcb,
            first_waits=tok + tuple(qt1), bank_pair=("ps3", "ps4"),
            conv_engines=(), bf16_direct=True, G=WG, nbuf=2,
            tt_list=(th * 2, th * 2 + 1))

        def ocb(oc, tt, ps, mmtok, th=th):
            sl = slice(oc * T + th * TH + (tt - th * 2) * TT,
                       oc * T + th * TH + (tt - th * 2) * TT + TT)
            return p.op("vector", lambda v, sl=sl, ps=ps, oc=oc:
                        v.scalar_tensor_tensor(
                            b["xres"][:, sl], ps[0:128, 0:TT],
                            bcol[:, 32 + oc:33 + oc], b["xres"][:, sl],
                            ALU.add, ALU.add), waits=(mmtok,))

        emit_linear(ctx, ws, lambda dc, tt, th=th, hbuf=hbuf: hbuf[
            :, dc * TH + (tt - th * 2) * TT:dc * TH + (tt - th * 2) * TT + TT],
            ctx.dram["bit_w2q"], 4096, D, ocb,
            first_waits=tuple(qt2) + (p.fence("scalar"),),
            bank_pair=("ps3", "ps4"),
            conv_engines=(), bf16_direct=True, G=128, nbuf=1,
            tt_list=(th * 2, th * 2 + 1))
        tok = (p.fence("vector"),)
    return p.fence("vector")


def build_sub_bitffn(upto="full"):
    nc = bass.Bass("TRN2", target_bir_lowering=False)
    p = Prog(nc)
    ctx = Ctx(nc, p)
    xin = nc.dram_tensor("xin", [D, T], F32, kind="ExternalInput").ap()
    xout = nc.dram_tensor("xout", [D, T], F32, kind="ExternalOutput").ap()
    ctx.dram["bit_w1"] = nc.dram_tensor("bit_w1", [D, 4096], F32,
                                        kind="ExternalInput").ap()
    ctx.dram["bit_w2"] = nc.dram_tensor("bit_w2", [4096, D], F32,
                                        kind="ExternalInput").ap()
    ctx.dram["bit_b1"] = nc.dram_tensor("bit_b1", [128, 32], F32,
                                        kind="ExternalInput").ap()
    ctx.dram["bit_b2"] = nc.dram_tensor("bit_b2", [128, 8], F32,
                                        kind="ExternalInput").ap()
    ctx.dram["bit_w1q"] = nc.dram_tensor("bit_w1q", [D, 4096], BF16,
                                         kind="Internal").ap()
    ctx.dram["bit_w2q"] = nc.dram_tensor("bit_w2q", [4096, D], BF16,
                                         kind="Internal").ap()
    cnames = ["c_ident", "c_ones_row", "c_ones_row_f", "c_ones_col",
              "c_ones_col_f", "c_scal"]
    for n in cnames:
        shape, dt = CONST_SPECS[n]
        ctx.dram[n] = nc.dram_tensor(n, shape, dt, kind="ExternalInput").ap()
    alloc_shared(ctx)
    load_consts(ctx, cnames)
    xres = ctx.b["xres"]
    for c in range(NC):
        p.dma("gpsimd", xres[:, c * T:(c + 1) * T],
              xin[c * 128:(c + 1) * 128, :], chan="xio")
    startw = (p.dfence("xio"), p.dfence("misc"))
    for e in ("vector", "tensor", "scalar", "gpsimd"):
        p.op(e, None, waits=startw)
    q1 = emit_quantize(ctx, ctx.dram["bit_w1"], ctx.dram["bit_w1q"], D, 4096, 0)
    q2 = emit_quantize(ctx, ctx.dram["bit_w2"], ctx.dram["bit_w2q"], 4096, D, 1,
                       first_waits=(p.fence("vector"), p.fence("scalar")) + q1)
    ctx.b["bit_quant_toks"] = (q1, q2)
    tdone = emit_bitffn(ctx, upto)
    for c in range(NC):
        p.dma("gpsimd", xout[c * 128:(c + 1) * 128, :],
              xres[:, c * T:(c + 1) * T],
              waits=(tdone, p.fence("scalar"), p.fence("tensor"),
                     p.fence("vector")) if c == 0 else (), chan="xio")
    p.op("gpsimd", None, waits=(p.dfence("xio"),))
    return p.build()




QROF = 47136
QIOF = 49184
KROF = 51232
KIOF = 53280
CMOF = 55328


def emit_cosformer(ctx, upto="full"):
    p, b, cin, dram = ctx.p, ctx.b, ctx.cin, ctx.dram
    arb, arf = b["arb"], b["arf"]
    xn, t_xn = emit_ln(ctx, 0, xb_off=26624)
    QOF, KOF, VOF, TRIG = 16384, 18432, 20480, 22528
    qP = arb[:, QOF:QOF + T]
    kP = arb[:, KOF:KOF + T]
    vP = arb[:, VOF:VOF + T]
    YB = arb[:, 26624:26624 + NC * T]
    cos = arb[:, TRIG:TRIG + T]
    sin = arb[:, TRIG + T:TRIG + 2 * T]
    qre = arb[:, QROF:QROF + T]
    qim = arb[:, QIOF:QIOF + T]
    kre = arb[:, KROF:KROF + T]
    kim = arb[:, KIOF:KIOF + T]
    cmch = ctx.name("cm")
    ctx.cin["c_cmask"] = arb[:, CMOF:CMOF + 2048]
    p.dma("sync", ctx.cin["c_cmask"], dram["c_cmask"], chan=cmch,
          waits=(t_xn,))
    p.dma("sync", cos, dram["c_poscos_b"], chan="trig")
    p.dma("sync", sin, dram["c_possin_b"], chan="trig")
    t_trig = p.dfence("trig")
    t_cm = p.dfence(cmch)
    dcol = ctx.b["bias"]
    dch = ctx.name("delta")
    p.dma("sync", dcol[:, 0:8], dram["cb_delta"], chan=dch)
    t_d = p.dfence(dch)
    PI = math.pi
    t1 = p.op("vector", lambda v: v.tensor_scalar(
        dcol[:, 0:8], dcol[:, 0:8], 0.0, -2.0 * PI, ALU.min, ALU.max),
        waits=(t_d,))
    t2 = p.op("scalar", lambda s: s.activation(dcol[:, 24:32], dcol[:, 0:8],
                                               AF.Sin, scale=0.25), waits=(t1,))
    t3 = p.op("scalar", lambda s: s.activation(dcol[:, 32:40], dcol[:, 24:32],
                                               AF.Square), waits=(t2,))
    t4 = p.op("vector", lambda v: v.tensor_scalar(
        dcol[:, 32:40], dcol[:, 32:40], -1.0, None, ALU.mult), waits=(t3,))
    t5 = p.op("scalar", lambda s: s.activation(
        dcol[:, 32:40], dcol[:, 32:40], AF.Sqrt,
        bias=cin["c_scal"][:, 1:2]), waits=(t4,))
    t6 = p.op("vector", lambda v: v.tensor_tensor(
        dcol[:, 8:16], dcol[:, 24:32], dcol[:, 32:40], ALU.mult), waits=(t5,))
    t7 = p.op("vector", lambda v: v.tensor_scalar(
        dcol[:, 8:16], dcol[:, 8:16], 2.0, None, ALU.mult))
    t8 = p.op("scalar", lambda s: s.activation(
        dcol[:, 16:24], dcol[:, 24:32], AF.Square), waits=(t7,))
    t9 = p.op("vector", lambda v: v.tensor_scalar(
        dcol[:, 16:24], dcol[:, 16:24], -2.0, 1.0, ALU.mult, ALU.add),
        waits=(t8,))
    t10 = p.op("vector", lambda v: v.tensor_tensor(
        dcol[:, 24:32], dcol[:, 8:16], dcol[:, 16:24], ALU.mult),
        waits=(t9, t8))
    t11 = p.op("vector", lambda v: v.tensor_scalar(
        dcol[:, 24:32], dcol[:, 24:32], 2.0, None, ALU.mult))
    t12 = p.op("scalar", lambda s: s.activation(
        dcol[:, 32:40], dcol[:, 8:16], AF.Square), waits=(t11,))
    t13 = p.op("vector", lambda v: v.tensor_scalar(
        dcol[:, 32:40], dcol[:, 32:40], -2.0, 1.0, ALU.mult, ALU.add),
        waits=(t12,))
    t_dc = t13

    ws = get_ws(ctx)
    t_ones = p.op("vector", lambda v: v.memset(
        b["vt"].rearrange("p (k c) -> p k c", c=130)[:, :, 64:65], 1.0))
    t_ones2 = p.op("vector", lambda v: v.memset(
        b["vt"].rearrange("p (k c) -> p k c", c=130)[:, :, 129:130], 1.0))
    prev_attn = (t_xn, t_trig, t_dc)
    for pair in range(NC):
        toks = {"q": [], "k": [], "v": []}
        mu_toks = {"q": [], "k": []}

        def mk_cb(dst, which):
            def cb(oc, tt, ps, mmtok):
                if which == "v":
                    tok = p.op("scalar", lambda s, tt=tt, ps=ps: s.copy(
                        dst[:, tt * TT:tt * TT + TT], ps[0:128, 0:TT]),
                        waits=(mmtok,))
                    toks[which].append(tok)
                    return tok
                tA = arf[:, 0:512]
                tB = b["usb"]
                o1 = p.op("scalar", lambda s, ps=ps: s.activation(
                    tA, ps[0:128, 0:TT], AF.Abs),
                    waits=(mmtok,) + tuple(mu_toks["q"][-1:] +
                                           mu_toks["k"][-1:]))
                o2 = p.op("scalar", lambda s: s.activation(
                    tA, tA, AF.Exp, scale=-1.0))
                o3 = p.op("scalar", lambda s: s.activation(
                    tA, tA, AF.Ln, bias=cin["c_scal"][:, 1:2]))
                o4 = p.op("scalar", lambda s, ps=ps: s.activation(
                    tB, ps[0:128, 0:TT], AF.Relu))
                o5 = p.op("vector", lambda v, tt=tt: v.tensor_tensor(
                    dst[:, tt * TT:tt * TT + TT], tA, tB, ALU.add),
                    waits=(o3, o4))
                toks[which].append(o5)
                mu_toks[which].append(o5)
                return o5
            return cb
        for which, dst, coff in (("q", qP, 0), ("k", kP, D),
                                 ("v", vP, 2 * D)):
            emit_linear(ctx, ws, lambda dc, tt: xn[:, dc * T + tt * TT:
                                                   dc * T + tt * TT + TT],
                        dram["cb_w_attn"], D, 128, mk_cb(dst, which),
                        first_waits=prev_attn, col_off=coff + pair * 128,
                        bank_pair=("ps3", "ps4"),
                        conv_engines=("vector", "scalar"))
            prev_attn = ()
        rot = []
        for tt in range(NTT):
            sl = slice(tt * TT, tt * TT + TT)
            w0 = (mu_toks["q"][-1], mu_toks["k"][-1]) if tt == 0 else ()
            rot.append(p.op("vector", lambda v, sl=sl: v.tensor_tensor(
                qre[:, sl], qP[:, sl], cos[:, sl], ALU.mult), waits=w0))
            rot.append(p.op("vector", lambda v, sl=sl: v.tensor_tensor(
                qim[:, sl], qP[:, sl], sin[:, sl], ALU.mult)))
            cpd = arf[:, 0:512]
            spd = b["usb"]
            p.op("vector", lambda v, sl=sl, pair=pair: v.tensor_scalar(
                cpd, sin[:, sl], dcol[:, 24 + pair:25 + pair], None, ALU.mult))
            p.op("vector", lambda v, sl=sl, pair=pair: v.scalar_tensor_tensor(
                cpd, cos[:, sl], dcol[:, 32 + pair:33 + pair], cpd,
                ALU.mult, ALU.subtract))
            p.op("vector", lambda v, sl=sl, pair=pair: v.tensor_scalar(
                spd, cos[:, sl], dcol[:, 24 + pair:25 + pair], None, ALU.mult))
            p.op("vector", lambda v, sl=sl, pair=pair: v.scalar_tensor_tensor(
                spd, sin[:, sl], dcol[:, 32 + pair:33 + pair], spd,
                ALU.mult, ALU.add))
            rot.append(p.op("vector", lambda v, sl=sl: v.tensor_tensor(
                kre[:, sl], kP[:, sl], cpd, ALU.mult)))
            rot.append(p.op("vector", lambda v, sl=sl: v.tensor_tensor(
                kim[:, sl], kP[:, sl], spd, ALU.mult)))
        if upto == "rot0":
            return rot[-1]
        t_vt = emit_vtok(ctx, vP, first_waits=tuple(toks["v"]))
        pre = tuple(rot[-4:]) + (t_vt, t_ones, t_ones2, t_cm)

        def q_lhs2(h, qt, part):
            buf = qre if part == 0 else qim
            return buf[h * 64:h * 64 + 64, qt * TT:qt * TT + TT]

        def k_pair2(h, kb, part):
            buf = kre if part == 0 else kim
            return buf[h * 64:h * 64 + 64, kb * 128:kb * 128 + 128]

        yt = emit_attn_pair(
            ctx, pair,
            q_lhs=q_lhs2, k_pair=k_pair2,
            y_dst=lambda h, qt, pair=pair: YB[
                h * 64:h * 64 + 64,
                pair * T + qt * TT:pair * T + qt * TT + TT],
            causal=True, scale=0.125, first_waits=pre, two_part=True)
        prev_attn = tuple(yt[-2:])
    if upto == "pairs":
        return prev_attn[-1]

    def ocb(oc, tt, ps, mmtok):
        sl = slice(oc * T + tt * TT, oc * T + tt * TT + TT)
        return p.op("vector", lambda v, sl=sl, ps=ps: v.tensor_tensor(
            b["xres"][:, sl], b["xres"][:, sl], ps[0:128, 0:TT], ALU.add),
            waits=(mmtok,))
    emit_linear(ctx, ws, lambda dc, tt: YB[:, dc * T + tt * TT:
                                           dc * T + tt * TT + TT],
                dram["cb_w_proj"], D, D, ocb, first_waits=prev_attn,
                bank_pair=("ps3", "ps4"), conv_engines=("scalar", "gpsimd"))
    return p.fence("vector")


def build_sub_cosformer(upto="full"):
    nc = bass.Bass("TRN2", target_bir_lowering=False)
    p = Prog(nc)
    ctx = Ctx(nc, p)
    xin = nc.dram_tensor("xin", [D, T], F32, kind="ExternalInput").ap()
    xout = nc.dram_tensor("xout", [D, T], F32, kind="ExternalOutput").ap()
    ctx.dram["cb_w_attn"] = nc.dram_tensor("cb_w_attn", [D, 3 * D], F32,
                                           kind="ExternalInput").ap()
    ctx.dram["cb_w_proj"] = nc.dram_tensor("cb_w_proj", [D, D], F32,
                                           kind="ExternalInput").ap()
    ctx.dram["cb_delta"] = nc.dram_tensor("cb_delta", [128, 8], F32,
                                          kind="ExternalInput").ap()
    ctx.dram["c_poscos_b"] = nc.dram_tensor("c_poscos_b", [128, T], BF16,
                                            kind="ExternalInput").ap()
    ctx.dram["c_possin_b"] = nc.dram_tensor("c_possin_b", [128, T], BF16,
                                            kind="ExternalInput").ap()
    cnames = ["c_ident", "c_ones_row", "c_ones_row_f", "c_ones_col",
              "c_scal"]
    for n in cnames + ["c_cmask"]:
        shape, dt = CONST_SPECS[n]
        ctx.dram[n] = nc.dram_tensor(n, shape, dt, kind="ExternalInput").ap()
    alloc_shared(ctx)
    load_consts(ctx, cnames)
    ctx.cin["c_cmask"] = ctx.b["arb"][:, CMOF:CMOF + 2048]
    p.dma("sync", ctx.cin["c_cmask"], ctx.dram["c_cmask"], chan="misc")
    xres = ctx.b["xres"]
    for c in range(NC):
        p.dma("gpsimd", xres[:, c * T:(c + 1) * T],
              xin[c * 128:(c + 1) * 128, :], chan="xio")
    startw = (p.dfence("xio"), p.dfence("misc"))
    for e in ("vector", "tensor", "scalar", "gpsimd"):
        p.op(e, None, waits=startw)
    tdone = emit_cosformer(ctx, upto)
    for c in range(NC):
        p.dma("gpsimd", xout[c * 128:(c + 1) * 128, :],
              xres[:, c * T:(c + 1) * T],
              waits=(tdone, p.fence("scalar"), p.fence("tensor"),
                     p.fence("vector")) if c == 0 else (), chan="xio")
    p.op("gpsimd", None, waits=(p.dfence("xio"),))
    return p.build()




KW0, KW1 = 0, 8192
QW0, QW1 = 16384, 24576
VW0, VW1 = 32768, 34816
YSOF = 36864
SW = 16


def emit_hybrid(ctx, upto="full"):
    p, b, cin, dram = ctx.p, ctx.b, ctx.cin, ctx.dram
    nc = ctx.nc
    arb, arf = b["arb"], b["arf"]
    xres = b["xres"]
    xn, t_xn = emit_ln(ctx, 0, xb_off=26624)
    QOF, KOF, VOF = 16384, 18432, 20480
    qP = arb[:, QOF:QOF + T]
    kP = arb[:, KOF:KOF + T]
    vP = arb[:, VOF:VOF + T]
    YB = arb[:, 26624:26624 + NC * T]
    GOF, AOF2 = 47136, 55328
    gbuf = arb[:, GOF:GOF + 4 * T]
    abuf = arf[:, 300:556]
    bcol = arf[:, 0:44]
    b2row = arf[:, 44:60]
    bch = ctx.name("hbias")
    p.dma("sync", bcol[:, 0:24], dram["hy_bqkv"], chan=bch, waits=(t_xn,))
    p.dma("sync", bcol[:, 24:32], dram["hy_bo"], chan=bch)
    p.dma("sync", bcol[:, 32:36], dram["hy_bg1"], chan=bch)
    p.dma("sync", b2row, dram["hy_b2row"], chan=bch)
    tb_tok = p.dfence(bch)

    ws = get_ws(ctx)
    t_ones = p.op("vector", lambda v: v.memset(
        b["vt"].rearrange("p (k c) -> p k c", c=130)[:, :, 64:65], 1.0))
    t_ones2 = p.op("vector", lambda v: v.memset(
        b["vt"].rearrange("p (k c) -> p k c", c=130)[:, :, 129:130], 1.0))
    cmch = ctx.name("cm")
    ctx.cin["c_cmask"] = arb[:, 55584:55584 + 2048]
    p.dma("sync", ctx.cin["c_cmask"], dram["c_cmask"], chan=cmch,
          waits=(t_xn,))
    cmask = ctx.cin["c_cmask"]
    t_cm = p.dfence(cmch)

    qkv_dma = []
    prev_attn = (t_xn, tb_tok, t_cm)
    for pair in range(NC):
        toks = {"q": [], "k": [], "v": []}

        def mk_cb(dst, which, pair=pair):
            def cb(oc, tt, ps, mmtok):
                bc = bcol[:, {"q": 0, "k": 8, "v": 16}[which] + pair:
                          {"q": 0, "k": 8, "v": 16}[which] + pair + 1]
                tok = p.op("scalar", lambda s, tt=tt, ps=ps: s.activation(
                    dst[:, tt * TT:tt * TT + TT], ps[0:128, 0:TT], AF.Identity,
                    bias=bc), waits=(mmtok,))
                toks[which].append(tok)
                return tok
            return cb
        for which, dst, coff in (("q", qP, 0), ("k", kP, D), ("v", vP, 2 * D)):
            emit_linear(ctx, ws, lambda dc, tt: xn[:, dc * T + tt * TT:
                                                   dc * T + tt * TT + TT],
                        dram["hy_wqkv"], D, 128, mk_cb(dst, which),
                        first_waits=prev_attn, col_off=coff + pair * 128,
                        bank_pair=("ps3", "ps4"),
                        conv_engines=("vector", "scalar"))
            prev_attn = ()
            dname = {"q": "hy_qd", "k": "hy_kd", "v": "hy_vd"}[which]
            qkv_dma.append(p.dma(
                "gpsimd", dram[dname][pair * 128:(pair + 1) * 128, :], dst,
                waits=(toks[which][-1],), chan="qkvst_" + which))
        t_vt = emit_vtok(ctx, vP, first_waits=tuple(toks["v"]))
        pre = tuple(toks["q"]) + tuple(toks["k"]) + (t_vt, t_ones, t_ones2)
        yt = emit_attn_pair(
            ctx, pair,
            q_lhs=lambda h, qt: qP[h * 64:h * 64 + 64, qt * TT:qt * TT + TT],
            k_pair=lambda h, kb: kP[h * 64:h * 64 + 64, kb * 128:kb * 128 + 128],
            y_dst=lambda h, qt, pair=pair: YB[
                h * 64:h * 64 + 64,
                pair * T + qt * TT:pair * T + qt * TT + TT],
            causal=True, scale=0.125, first_waits=pre)
        prev_attn = tuple(yt[-2:])
        stage = b["ebuf"][:, 0:128]
        pst = b["pst"]
        ident = cin["c_ident"]
        sp_prev = prev_attn
        for which, buf, dnm in (("q", qP, "hy_qt"), ("k", kP, "hy_kt"),
                                ("v", vP, "hy_vt")):
            for tb in range(NKB):
                tr = p.op("tensor", lambda te, buf=buf, tb=tb: te.transpose(
                    pst[:, 128:256], buf[:, tb * 128:(tb + 1) * 128],
                    ident), waits=sp_prev)
                ev = p.op("scalar", lambda s: s.copy(stage, pst[:, 128:256]),
                          waits=(tr,))
                dmt = p.dma("gpsimd",
                            dram[dnm][tb * 128:(tb + 1) * 128,
                                      pair * 128:(pair + 1) * 128],
                            stage, waits=(ev,), chan="qkvtr")
                sp_prev = (dmt,)
        prev_attn = prev_attn + (sp_prev[-1],) + tuple(qkv_dma[-3:])
    t_qkvt = p.dfence("qkvtr")
    ysp = []
    for c in range(NC):
        ysp.append(p.dma("gpsimd", dram["hy_yd"][c * 128:(c + 1) * 128, :],
                         YB[:, c * T:(c + 1) * T],
                         waits=prev_attn if c == 0 else (), chan="ysp"))
    t_ysp = p.dfence("ysp")

    gtoks = []

    def gcb(oc, tt, ps, mmtok):
        tok = p.op("scalar", lambda s, oc=oc, tt=tt, ps=ps: s.activation(
            gbuf[:, oc * T + tt * TT:oc * T + tt * TT + TT], ps[0:128, 0:TT],
            AF.Gelu, bias=bcol[:, 32 + oc:33 + oc]), waits=(mmtok,))
        gtoks.append(tok)
        return tok
    emit_linear(ctx, ws, lambda dc, tt: xn[:, dc * T + tt * TT:
                                           dc * T + tt * TT + TT],
                dram["hy_gw1"], D, 512, gcb, first_waits=prev_attn,
                bank_pair=("ps3", "ps4"), conv_engines=("vector", "scalar"))
    gw2 = arb[:, 57632:57696]
    gw2f = arf[:, 64:128]
    gch = ctx.name("gw2")
    dt0 = p.dma("sync", gw2f.rearrange("p (c e) -> p c e", e=16),
                dram["hy_gw2"].rearrange("(c p) e -> p c e", p=128), chan=gch,
                waits=prev_attn)
    tcv = p.op("scalar", lambda s: s.copy(gw2, gw2f), waits=(dt0,))
    ps5 = b["ps5"]
    a_toks = []
    prev_ev = ()
    for tb in range(NKB):
        mmt = None
        for gc in range(4):
            def mm(t, tb=tb, gc=gc):
                return t.matmul(ps5[0:128, 0:16],
                                gbuf[:, gc * T + tb * 128:gc * T + (tb + 1) * 128],
                                gw2[:, gc * 16:gc * 16 + 16],
                                start=(gc == 0), stop=(gc == 3))
            w = [tcv, gtoks[-1]] if gc == 0 else []
            if gc == 0:
                w += list(prev_ev)
            mmt = p.op("tensor", mm, waits=tuple(w), track=(gc == 3))
        t1 = p.op("vector", lambda v, tb=tb: v.tensor_tensor(
            arf[:, 256:272], ps5[0:128, 0:16], b2row[:, 0:16], ALU.add),
            waits=(mmt, tb_tok))
        t2 = p.op("scalar", lambda s, tb=tb: s.activation(
            abuf[:, tb * 16:(tb + 1) * 16], arf[:, 256:272], AF.Sigmoid),
            waits=(t1,))
        prev_ev = (t2,)
        a_toks.append(t2)
    if upto == "gate":
        return a_toks[-1]

    xsp = []
    for c in range(NC):
        xsp.append(p.dma("gpsimd", dram["hy_xd"][c * 128:(c + 1) * 128, :],
                         xres[:, c * T:(c + 1) * T], chan="xsp"))
    t_xsp = p.dfence("xsp")
    Rt = xres[:, 0:512]
    Gs = xres[:, 512:1024]
    QG = xres[:, 1024:1536]
    Mb = xres[:, 1536:2048]
    lam = xres[:, 2048:2560]
    wst = xres[:, 2560:3072]
    sch = ctx.name("scan")
    d1 = p.dma("sync", lam, dram["hy_lam"], waits=(t_xsp,), chan=sch)
    d2 = p.dma("sync", Gs, dram["hy_gam"], chan=sch)
    d3 = p.dma("sync", wst, dram["hy_wst"], chan=sch)
    t_sc = p.dfence(sch)
    tR = p.op("scalar", lambda s: s.activation(Rt, lam, AF.Sigmoid,
                                               scale=-1.0), waits=(t_sc,))
    wstb = arb[:, 57696:58208]
    t_wstb = p.op("vector", lambda v: v.tensor_copy(wstb, wst), waits=(t_sc,))
    tG = p.op("vector", lambda v: v.tensor_scalar(Gs, Gs, 0.125, None,
                                                  ALU.mult), waits=(t_sc,))
    zts = []
    for off, ln in ((KW0, 8192), (KW1, 8192), (QW0, 8192), (QW1, 8192),
                    (VW0, 2048), (VW1, 2048)):
        zts.append(p.op("vector", lambda v, off=off, ln=ln: v.memset(
            arb[0:16, off:off + ln], 0.0), waits=(t_ysp,) if not zts else ()))
    t_z = zts[-1]
    ybuf = arb[:, YSOF:YSOF + T * 8]

    def refill(t0, buf_par, wait):
        toks = []
        kwo = (KW0, KW1)[buf_par]
        qwo = (QW0, QW1)[buf_par]
        vwo = (VW0, VW1)[buf_par]
        for nm, wo, q in (("hy_kt", kwo, "scalar"), ("hy_qt", qwo, "gpsimd")):
            dd = arb[0:16, wo:wo + SW * 512].rearrange(
                "p (dt f) -> p dt f", dt=SW)
            sv = dram[nm].rearrange("t (h j) -> h t j", h=16)
            for g in range(8):
                toks.append(p.dma(
                    q, dd[2 * g:2 * g + 2, :, g * 64:g * 64 + 64],
                    sv[2 * g:2 * g + 2, t0:t0 + SW, :],
                    waits=wait if g == 0 else (),
                    chan=f"rf_{nm}_{buf_par}"))
        dd = arb[0:16, vwo:vwo + SW * 128].rearrange(
            "p (dt f) -> p dt f", dt=SW)
        sv = dram["hy_vt"].rearrange("t (h i) -> h t i", h=16)
        toks.append(p.dma("gpsimd", dd[0:16:2, :, 0:64],
                          sv[0:16:2, t0:t0 + SW, :], waits=wait,
                          chan=f"rf_v_{buf_par}"))
        toks.append(p.dma("gpsimd", dd[1:16:2, :, 64:128],
                          sv[1:16:2, t0:t0 + SW, :],
                          chan=f"rf_v_{buf_par}"))
        return (p.dfence(f"rf_hy_kt_{buf_par}"), p.dfence(f"rf_hy_qt_{buf_par}"),
                p.dfence(f"rf_v_{buf_par}"))

    d16 = cin["c_delta16"]
    Zb = b["ps0"]
    Qb = b["ps1"]
    win_tok = [None, None]
    win_tok[0] = refill(0, 0, (t_z, t_qkvt))
    win_tok[1] = refill(SW, 1, (t_z,))
    dec_tok = None
    mmh_tok = None
    m_tok = ()
    qg_tok = ()
    mmq_tok = None
    y_toks = []
    win_free = [(), ()]
    for t in range(T):
        par = (t // SW) % 2
        kwo, qwo, vwo = (KW0, KW1)[par], (QW0, QW1)[par], (VW0, VW1)[par]
        dt = t % SW
        if dt == 0 and t > 0:
            opar = par ^ 1
            nxt = t + SW
            if nxt < T:
                win_tok[opar] = refill(nxt, opar, win_free[opar])
        if t > 0:
            dec_tok = p.op("vector", lambda v: v.tensor_tensor(
                Zb[0:128, 0:TT], Zb[0:128, 0:TT], Rt, ALU.mult),
                waits=(mmh_tok, tR) + m_tok)
        vsl = arb[0:16, vwo + dt * 128:vwo + dt * 128 + 128]
        ksl = arb[0:16, kwo + dt * 512:kwo + dt * 512 + 512]
        qsl = arb[0:16, qwo + dt * 512:qwo + dt * 512 + 512]
        w = [win_tok[par][0], win_tok[par][2]] if dt == 0 else []
        if t > 0:
            w.append(dec_tok)
        mmh_tok = p.op("tensor", lambda te, vsl=vsl, ksl=ksl, t=t: te.matmul(
            Zb[0:128, 0:TT], vsl, ksl, start=(t == 0), stop=(t == T - 1),
            skip_group_check=True), waits=tuple(w))
        w2 = [win_tok[par][1]] if dt == 0 else []
        w2 += list(qg_tok)
        mmq_tok = p.op("tensor", lambda te, qsl=qsl: te.matmul(
            Qb[0:128, 0:TT], d16, qsl, start=True, stop=True),
            waits=tuple(w2))
        qg_tok = (p.op("vector", lambda v: v.tensor_tensor(
            QG, Qb[0:128, 0:TT], Gs, ALU.mult), waits=(mmq_tok, tG)),)
        m_tok = (p.op("vector", lambda v: v.tensor_tensor(
            Mb, Zb[0:128, 0:TT], QG, ALU.mult), waits=(mmh_tok,) + qg_tok),)
        def _red(v, t=t):
            with ctx.nc.allow_low_precision(reason="bf16 y_stp store"):
                return v.tensor_reduce(
                    ybuf[:, t * 8:t * 8 + 8],
                    Mb.rearrange("p (hh j) -> p hh j", hh=8), AX.X, ALU.add)
        yv = p.op("vector", _red, waits=m_tok)
        y_toks.append(yv)
        if dt == SW - 1:
            win_free[par] = (mmh_tok, mmq_tok)
    t_scan = y_toks[-1]
    if upto == "scan":
        return t_scan

    psA, psB = b["ps2"], b["ps3"]
    wtoks = []
    prevA = ()
    prevB = ()
    for pair in range(NC):
        for tt in range(NTT):
            qsl = arb[:, 53248 + (tt % 2) * 2048:53248 + (tt % 2) * 2048 + TT]
            dq = p.dma("sync", qsl,
                       dram["hy_qd"][pair * 128:(pair + 1) * 128,
                                     tt * TT:tt * TT + TT],
                       waits=prevA + prevB + ((a_toks[-1],) if pair == 0
                                              and tt == 0 else ()),
                       chan=f"wq{tt % 2}")
            mA = p.op("tensor", lambda te, qsl=qsl, pair=pair: te.matmul(
                psA[0:64, 0:TT], wstb[0:64, pair * 64:pair * 64 + 64],
                qsl[0:64, :], start=True, stop=True),
                waits=(dq, t_wstb) + prevA)
            mB = p.op("tensor", lambda te, qsl=qsl, pair=pair: te.matmul(
                psB[0:64, 0:TT], wstb[64:128, pair * 64:pair * 64 + 64],
                qsl[64:128, :], start=True, stop=True),
                waits=(dq,) + prevB)
            yA = ybuf[0:64, :].rearrange("p (t hh) -> p t hh", hh=8)[
                :, tt * TT:tt * TT + TT, pair:pair + 1]
            yB = ybuf[64:128, :].rearrange("p (t hh) -> p t hh", hh=8)[
                :, tt * TT:tt * TT + TT, pair:pair + 1]
            wA = p.op("vector", lambda v, yA=yA: v.tensor_tensor(
                yA, yA, psA[0:64, 0:TT].rearrange("p (t o) -> p t o", o=1),
                ALU.add), waits=(mA, t_scan))
            wB = p.op("vector", lambda v, yB=yB: v.tensor_tensor(
                yB, yB, psB[0:64, 0:TT].rearrange("p (t o) -> p t o", o=1),
                ALU.add), waits=(mB,))
            prevA = (wA,)
            prevB = (wB,)
            wtoks += [wA, wB]
    t_w = wtoks[-1]
    if upto == "wstat":
        return t_w

    xrl = []
    for c in range(NC):
        xrl.append(p.dma("gpsimd", xres[:, c * T:(c + 1) * T],
                         dram["hy_xd"][c * 128:(c + 1) * 128, :],
                         waits=(t_w, p.fence("vector")) if c == 0 else (),
                         chan="xsp"))
    t_xrl = p.dfence("xsp")
    onesrow = cin["c_ones_row_f"]
    identf = cin["c_ident_f"]
    ps6 = b["ps6"]
    rows = b["rows"]
    YC = arb[:, 26624:26624 + NC * T]
    scb = b["ps5"]
    prev_cmb = ()
    for pair in range(NC):
        ysl = arb[:, 16384 + (pair % 2) * T:16384 + (pair % 2) * T + T]
        dy = p.dma("sync", ysl,
                   dram["hy_yd"][pair * 128:(pair + 1) * 128, :],
                   waits=(t_ysp,) + prev_cmb, chan=f"yrl{pair % 2}")
        for hl in (0, 1):
            h = 2 * pair + hl
            art = ()
            for tb in range(NKB):
                t1 = p.op("tensor", lambda te, tb=tb, h=h: te.transpose(
                    ps6[0:1, 0:128], abuf[:, tb * 16 + h:tb * 16 + h + 1],
                    identf), waits=(a_toks[-1],) + art)
                e1 = p.op("scalar", lambda s, tb=tb: s.copy(
                    rows[0:1, tb * 128:(tb + 1) * 128], ps6[0:1, 0:128]),
                    waits=(t1,))
                art = (e1,)
            for tt in range(NTT):
                bt = p.op("tensor", lambda te, tt=tt: te.matmul(
                    scb[0:64, 0:TT], onesrow[0:1, 0:64],
                    rows[0:1, tt * TT:tt * TT + TT], start=True, stop=True),
                    waits=art + prev_cmb)
                ysm = ysl[hl * 64:hl * 64 + 64, tt * TT:tt * TT + TT]
                yst = ybuf[hl * 64:hl * 64 + 64, :].rearrange(
                    "p (t hh) -> p t hh", hh=8)[:, tt * TT:tt * TT + TT,
                                                pair:pair + 1]
                dst = YC[hl * 64:hl * 64 + 64,
                         pair * T + tt * TT:pair * T + tt * TT + TT]
                c1 = p.op("vector", lambda v, ysm=ysm, yst=yst, dst=dst:
                          v.tensor_tensor(
                              dst, ysm,
                              yst.rearrange("p t o -> p (t o)"), ALU.subtract),
                          waits=(dy, bt))
                c2 = p.op("vector", lambda v, dst=dst: v.tensor_tensor(
                    dst, dst, scb[0:64, 0:TT], ALU.mult))
                c3 = p.op("vector", lambda v, yst=yst, dst=dst: v.tensor_tensor(
                    dst, dst, yst.rearrange("p t o -> p (t o)"), ALU.add))
                prev_cmb = (c3,)
    t_cmb = prev_cmb[0]
    if upto == "combine":
        return t_cmb

    def ocb(oc, tt, ps, mmtok):
        bc = bcol[:, 24 + oc:24 + oc + 1]
        sl = slice(oc * T + tt * TT, oc * T + tt * TT + TT)
        return p.op("vector", lambda v, sl=sl, ps=ps, bc=bc:
                    v.scalar_tensor_tensor(xres[:, sl], ps[0:128, 0:TT],
                                           bc, xres[:, sl], ALU.add, ALU.add),
                    waits=(mmtok,))
    emit_linear(ctx, ws, lambda dc, tt: YC[:, dc * T + tt * TT:
                                           dc * T + tt * TT + TT],
                dram["hy_wo"], D, D, ocb,
                first_waits=(t_cmb, t_xrl), bank_pair=("ps3", "ps4"),
                conv_engines=("scalar", "gpsimd"))
    return p.fence("vector")


def build_sub_hybrid(upto="full"):
    nc = bass.Bass("TRN2", target_bir_lowering=False)
    p = Prog(nc)
    ctx = Ctx(nc, p)
    xin = nc.dram_tensor("xin", [D, T], F32, kind="ExternalInput").ap()
    xout = nc.dram_tensor("xout", [D, T], F32, kind="ExternalOutput").ap()
    for nm, shape in (("hy_wqkv", [D, 3 * D]), ("hy_wo", [D, D]),
                      ("hy_gw1", [D, 512]), ("hy_gw2", [512, 16]),
                      ("hy_bqkv", [128, 24]), ("hy_bo", [128, 8]),
                      ("hy_bg1", [128, 4]), ("hy_b2row", [128, 16]),
                      ("hy_lam", [128, 512]), ("hy_gam", [128, 512]),
                      ("hy_wst", [128, 512])):
        ctx.dram[nm] = nc.dram_tensor(nm, shape, F32,
                                      kind="ExternalInput").ap()
    for nm, shape in (("hy_qd", [D, T]), ("hy_kd", [D, T]), ("hy_vd", [D, T]),
                      ("hy_yd", [D, T]), ("hy_qt", [T, D]), ("hy_kt", [T, D]),
                      ("hy_vt", [T, D])):
        ctx.dram[nm] = nc.dram_tensor(nm, shape, BF16, kind="Internal").ap()
    ctx.dram["hy_xd"] = nc.dram_tensor("hy_xd", [D, T], F32,
                                       kind="Internal").ap()
    cnames = ["c_ident", "c_ident_f", "c_ones_row", "c_ones_row_f",
              "c_ones_col", "c_scal", "c_delta16"]
    for n in cnames + ["c_cmask"]:
        shape, dt = CONST_SPECS[n]
        ctx.dram[n] = nc.dram_tensor(n, shape, dt, kind="ExternalInput").ap()
    alloc_shared(ctx)
    load_consts(ctx, cnames)
    ctx.cin["c_cmask"] = ctx.b["arb"][:, 55584:55584 + 2048]
    p.dma("sync", ctx.cin["c_cmask"], ctx.dram["c_cmask"], chan="misc")
    xres = ctx.b["xres"]
    for c in range(NC):
        p.dma("gpsimd", xres[:, c * T:(c + 1) * T],
              xin[c * 128:(c + 1) * 128, :], chan="xio")
    startw = (p.dfence("xio"), p.dfence("misc"))
    for e in ("vector", "tensor", "scalar", "gpsimd"):
        p.op(e, None, waits=startw)
    tdone = emit_hybrid(ctx, upto)
    for c in range(NC):
        p.dma("gpsimd", xout[c * 128:(c + 1) * 128, :],
              xres[:, c * T:(c + 1) * T],
              waits=(tdone, p.fence("scalar"), p.fence("tensor"),
                     p.fence("vector")) if c == 0 else (), chan="xio")
    p.op("gpsimd", None, waits=(p.dfence("xio"),))
    return p.build()


def build_sub_mha(kind, upto="full"):
    nc = bass.Bass("TRN2", target_bir_lowering=False)
    p = Prog(nc)
    ctx = Ctx(nc, p)
    pfx = {"l0a": "w0a", "l1a": "w1a"}[kind]
    xin = nc.dram_tensor("xin", [D, T], F32, kind="ExternalInput").ap()
    xout = nc.dram_tensor("xout", [D, T], F32, kind="ExternalOutput").ap()
    for wname in ("_wq", "_wk", "_wv", "_wo"):
        ctx.dram[pfx + wname] = nc.dram_tensor(pfx + wname, [D, D], F32,
                                               kind="ExternalInput").ap()
    for bname in ("_bq", "_bk", "_bv", "_bo"):
        ctx.dram[pfx + bname] = nc.dram_tensor(pfx + bname, [128, 8], F32,
                                               kind="ExternalInput").ap()
    cnames = ["c_ident", "c_ones_row", "c_ones_row_f", "c_ones_col", "c_scal"]
    lnames = list(cnames)
    if kind == "l1a":
        cnames += ["c_ropecos", "c_ropesin", "c_roperm"]
        lnames += ["c_roperm"]
    for n in cnames:
        shape, dt = CONST_SPECS[n]
        ctx.dram[n] = nc.dram_tensor(n, shape, dt, kind="ExternalInput").ap()
    alloc_shared(ctx)
    load_consts(ctx, lnames)
    xres = ctx.b["xres"]
    tlds = [p.dma("gpsimd", xres[:, c * T:(c + 1) * T],
                  xin[c * 128:(c + 1) * 128, :], chan="xio") for c in range(NC)]
    startw = (p.dfence("xio"), p.dfence("misc"))
    for e in ("vector", "tensor", "scalar", "gpsimd"):
        p.op(e, None, waits=startw)
    tdone = emit_mha(ctx, kind, upto)
    for c in range(NC):
        p.dma("gpsimd", xout[c * 128:(c + 1) * 128, :],
              xres[:, c * T:(c + 1) * T],
              waits=(tdone, p.fence("scalar"), p.fence("tensor"))
              if c == 0 else (), chan="xio")
    p.op("gpsimd", None, waits=(p.dfence("xio"),))
    return p.build()


def prep_mha_weights(kind, inputs):
    pfx = {"l0a": "w0a", "l1a": "w1a"}[kind]
    ipfx = {"l0a": "L0_attn_a", "l1a": "L1_attn_a"}[kind]
    d = {}
    if kind == "l0a":
        for nm in ("Wq", "Wk", "Wv"):
            w = np.asarray(inputs[f"{ipfx}_{nm}"])
            d[pfx + "_w" + nm[1].lower()] = _f32(w.transpose(2, 0, 1).reshape(D, D))
        for nm in ("bq", "bk", "bv"):
            d[pfx + "_" + nm] = _bias_cols(np.asarray(inputs[f"{ipfx}_{nm}"]).reshape(-1))
    else:
        for nm in ("Wq", "Wk", "Wv"):
            w = np.asarray(inputs[f"{ipfx}_{nm}"])
            d[pfx + "_w" + nm[1].lower()] = _f32(w.T)
        for nm in ("bq", "bk", "bv"):
            d[pfx + "_" + nm] = _bias_cols(np.asarray(inputs[f"{ipfx}_{nm}"]))
    wo = np.asarray(inputs[f"{ipfx}_Wo"])
    d[pfx + "_wo"] = _f32(wo.T)
    d[pfx + "_bo"] = _bias_cols(np.asarray(inputs[f"{ipfx}_bo"]))
    return d


def kernel(**inputs):
    raise NotImplementedError


def _prep_all(inputs):
    assert inputs["L0_attn_a_Wq"] is not None and inputs["L0_attn_a_Wk"] \
        is not None and inputs["L0_attn_a_Wv"] is not None
    assert inputs["L1_attn_a_Wq"] is not None and inputs["L1_attn_a_Wk"] \
        is not None and inputs["L1_attn_a_Wv"] is not None
    wd = {}
    wd.update(prep_mha_weights("l0a", inputs))
    wd.update(prep_mha_weights("l1a", inputs))
    wd.update({
        "moe_gate_w": _f32(inputs["L0_ffn_gate_w"]),
        "moe_base_wgu": _f32(inputs["L0_ffn_base_wgu"]),
        "moe_base_wd": _f32(inputs["L0_ffn_base_wd"]),
        "moe_exp_wgu": _f32(np.asarray(inputs["L0_ffn_exp_wgu"]).reshape(
            8 * 1024, 5632)),
        "moe_exp_wd": _f32(np.asarray(inputs["L0_ffn_exp_wd"]).reshape(
            8 * 2816, 1024)),
        "cb_w_attn": _f32(np.asarray(inputs["L0_attn_b_w_attn"]).T),
        "cb_w_proj": _f32(np.asarray(inputs["L0_attn_b_w_proj"]).T),
        "bit_w1": _f32(np.asarray(inputs["L1_ffn_W1"]).T),
        "bit_w2": _f32(np.asarray(inputs["L1_ffn_W2"]).T),
        "bit_b1": _bias_cols(np.asarray(inputs["L1_ffn_b1"])),
        "bit_b2": _bias_cols(np.asarray(inputs["L1_ffn_b2"])),
        "hy_wqkv": _f32(inputs["L1_attn_b_Wqkv_w"]),
        "hy_wo": _f32(inputs["L1_attn_b_out_w"]),
        "hy_gw1": _f32(inputs["L1_attn_b_gate_w1"]),
        "hy_gw2": _f32(inputs["L1_attn_b_gate_w2"]),
        "hy_bqkv": _bias_cols(np.asarray(inputs["L1_attn_b_Wqkv_b"])),
        "hy_bo": _bias_cols(np.asarray(inputs["L1_attn_b_out_b"])),
        "hy_bg1": _bias_cols(np.asarray(inputs["L1_attn_b_gate_b1"])),
        "hy_b2row": _f32(np.tile(np.asarray(
            inputs["L1_attn_b_gate_b2"])[None, :], (128, 1))),
    })
    delta = np.asarray(inputs["L0_attn_b_delta"])
    dcol = np.zeros((128, 8), np.float32)
    for h in range(NH):
        dcol[(h % 2) * 64:(h % 2) * 64 + 64, h // 2] = delta[h]
    wd["cb_delta"] = dcol
    lam = np.asarray(inputs["L1_attn_b_Lambda_raw"])
    gam = np.asarray(inputs["L1_attn_b_Gamma"])
    wst = np.asarray(inputs["L1_attn_b_W_static"])
    lamL = np.zeros((128, 512), np.float32)
    gamL = np.zeros((128, 512), np.float32)
    wstL = np.zeros((128, 512), np.float32)
    for h in range(NH):
        hh, hl = h // 2, h % 2
        lamL[hl * 64:(hl + 1) * 64, hh * 64:(hh + 1) * 64] = lam[h]
        gamL[hl * 64:(hl + 1) * 64, hh * 64:(hh + 1) * 64] = gam[h]
        wstL[hl * 64:(hl + 1) * 64, hh * 64:(hh + 1) * 64] = wst[h].T
    wd["hy_lam"] = lamL
    wd["hy_gam"] = gamL
    wd["hy_wst"] = wstL
    wd.update(host_constants())
    return wd


_SEQ = [("l0a", lambda: build_sub_mha("l0a")),
        ("moe", build_sub_moe),
        ("cos", build_sub_cosformer),
        ("moe", build_sub_moe),
        ("l1a", lambda: build_sub_mha("l1a")),
        ("bit", build_sub_bitffn),
        ("hyb", build_sub_hybrid),
        ("bit", build_sub_bitffn)]


def _declared_inputs(nc):
    names = set()
    for alloc in nc.m.functions[0].allocations:
        if isinstance(alloc, mybir.MemoryLocationSet) and                 alloc.kind == "ExternalInput":
            names.add(alloc.memorylocations[0].name)
    names.discard("partition_id")
    return names


def kernel_unfused(**inputs):
    wd = _prep_all(inputs)
    x = np.asarray(inputs["L0_attn_a_hidden_state"])
    xs = [_f32(x[b].T) for b in range(8)]
    for key, builder in _SEQ:
        if key not in _CACHE:
            _CACHE[key] = builder()
        nc = _CACHE[key]
        decl = _declared_inputs(nc)
        in_maps = []
        for b in range(8):
            m = {"xin": xs[b]}
            m.update({k: v for k, v in wd.items() if k in decl})
            in_maps.append(m)
        res = run_bass_kernel_spmd(nc, in_maps, core_ids=list(range(8)))
        xs = [np.ascontiguousarray(res.results[b]["xout"])
              for b in range(8)]
    out = np.stack([xs[b].T for b in range(8)], axis=0)
    return np.ascontiguousarray(out, dtype=np.float32)


def phase_barrier(p):
    waits = tuple(p.fence(e) for e in Prog.ENGINES) + \
        tuple(p.dfence(c) for c in list(p.dchans))
    toks = tuple(p.op(e, None, waits=waits) for e in Prog.ENGINES)
    for e in Prog.ENGINES:
        p.op(e, None, waits=toks)


def build_fused(nphases=8, order=None):
    nc = bass.Bass("TRN2", target_bir_lowering=False)
    p = Prog(nc)
    ctx = Ctx(nc, p)
    xin = nc.dram_tensor("xin", [D, T], F32, kind="ExternalInput").ap()
    xout = nc.dram_tensor("xout", [D, T], F32, kind="ExternalOutput").ap()
    decls = [("w0a_wq", [D, D]), ("w0a_wk", [D, D]), ("w0a_wv", [D, D]),
             ("w0a_wo", [D, D]), ("w0a_bq", [128, 8]), ("w0a_bk", [128, 8]),
             ("w0a_bv", [128, 8]), ("w0a_bo", [128, 8]),
             ("w1a_wq", [D, D]), ("w1a_wk", [D, D]), ("w1a_wv", [D, D]),
             ("w1a_wo", [D, D]), ("w1a_bq", [128, 8]), ("w1a_bk", [128, 8]),
             ("w1a_bv", [128, 8]), ("w1a_bo", [128, 8]),
             ("moe_gate_w", [D, 8]), ("moe_base_wgu", [D, 5632]),
             ("moe_base_wd", [2816, D]), ("moe_exp_wgu", [8 * D, 5632]),
             ("moe_exp_wd", [8 * 2816, D]),
             ("cb_w_attn", [D, 3 * D]), ("cb_w_proj", [D, D]),
             ("cb_delta", [128, 8]),
             ("bit_w1", [D, 4096]), ("bit_w2", [4096, D]),
             ("bit_b1", [128, 32]), ("bit_b2", [128, 8]),
             ("hy_wqkv", [D, 3 * D]), ("hy_wo", [D, D]),
             ("hy_gw1", [D, 512]), ("hy_gw2", [512, 16]),
             ("hy_bqkv", [128, 24]), ("hy_bo", [128, 8]),
             ("hy_bg1", [128, 4]), ("hy_b2row", [128, 16]),
             ("hy_lam", [128, 512]), ("hy_gam", [128, 512]),
             ("hy_wst", [128, 512])]
    for nm, shape in decls:
        ctx.dram[nm] = nc.dram_tensor(nm, shape, F32,
                                      kind="ExternalInput").ap()
    ctx.dram["c_poscos_b"] = nc.dram_tensor("c_poscos_b", [128, T], BF16,
                                            kind="ExternalInput").ap()
    ctx.dram["c_possin_b"] = nc.dram_tensor("c_possin_b", [128, T], BF16,
                                            kind="ExternalInput").ap()
    for nm, shape in (("hy_qd", [D, T]), ("hy_kd", [D, T]), ("hy_vd", [D, T]),
                      ("hy_yd", [D, T]), ("hy_qt", [T, D]), ("hy_kt", [T, D]),
                      ("hy_vt", [T, D])):
        ctx.dram[nm] = nc.dram_tensor(nm, shape, BF16, kind="Internal").ap()
    ctx.dram["hy_xd"] = nc.dram_tensor("hy_xd", [D, T], F32,
                                       kind="Internal").ap()
    ctx.dram["bit_w1q"] = nc.dram_tensor("bit_w1q", [D, 4096], BF16,
                                         kind="Internal").ap()
    ctx.dram["bit_w2q"] = nc.dram_tensor("bit_w2q", [4096, D], BF16,
                                         kind="Internal").ap()
    cnames = ["c_ident", "c_ident_f", "c_ones_row_f",
              "c_ones_col", "c_ones_col_f", "c_scal", "c_delta16",
              "c_ropecos", "c_ropesin", "c_roperm", "c_cmask"]
    for n in cnames:
        shape, dt = CONST_SPECS[n]
        ctx.dram[n] = nc.dram_tensor(n, shape, dt, kind="ExternalInput").ap()
    alloc_shared(ctx)
    load_consts(ctx, [n for n in cnames
                      if n not in ("c_cmask", "c_ropecos", "c_ropesin",
                                   "c_roperm")])
    xres = ctx.b["xres"]
    for c in range(NC):
        p.dma("gpsimd", xres[:, c * T:(c + 1) * T],
              xin[c * 128:(c + 1) * 128, :], chan="xio")
    startw = (p.dfence("xio"), p.dfence("misc"))
    for e in ("vector", "tensor", "scalar", "gpsimd"):
        p.op(e, None, waits=startw)
    emits = [lambda: emit_mha(ctx, "l0a"), lambda: emit_moe(ctx),
             lambda: emit_cosformer(ctx), lambda: emit_moe(ctx),
             lambda: emit_mha(ctx, "l1a"), lambda: emit_bitffn(ctx),
             lambda: emit_hybrid(ctx), lambda: emit_bitffn(ctx)]
    sel = order if order is not None else list(range(nphases))
    for i in sel:
        emits[i]()
        phase_barrier(p)
    for c in range(NC):
        p.dma("gpsimd", xout[c * 128:(c + 1) * 128, :],
              xres[:, c * T:(c + 1) * T], chan="xio")
    p.op("gpsimd", None, waits=(p.dfence("xio"),))
    return p.build()


def kernel_fused(**inputs):
    wd = _prep_all(inputs)
    x = np.asarray(inputs["L0_attn_a_hidden_state"])
    if "fused" not in _CACHE:
        _CACHE["fused"] = build_fused()
    nc = _CACHE["fused"]
    decl = _declared_inputs(nc)
    in_maps = []
    for bb in range(8):
        m = {"xin": _f32(x[bb].T)}
        m.update({k: v for k, v in wd.items() if k in decl})
        in_maps.append(m)
    res = run_bass_kernel_spmd(nc, in_maps, core_ids=list(range(8)))
    out = np.stack([res.results[bb]["xout"].T for bb in range(8)], axis=0)
    return np.ascontiguousarray(out, dtype=np.float32)


kernel = kernel_fused
